# version 05
# baseline (speedup 1.0000x reference)
import os
os.environ.setdefault("TRN_ALLOW_ICI_ALL_TO_ALL", "1")

import numpy as np
import ml_dtypes
import concourse.bass as bass
import concourse.mybir as mybir
from concourse.bass_utils import run_bass_kernel_spmd
from concourse.aot_env import allow_ici_all_to_all

F32 = mybir.dt.float32
BF16 = mybir.dt.bfloat16
AF = mybir.ActivationFunctionType
ALU = mybir.AluOpType
AX = mybir.AxisListType
BF = ml_dtypes.bfloat16

NCORE = 8
C = 2048
TOWN = 1024
NT = 16
L = 64
EPS_F32 = float(np.finfo(np.float32).eps)

ENGINES = ("tensor", "vector", "scalar", "sync", "gpsimd")


class Emitter:

    def __init__(self):
        self.streams = {e: [] for e in ENGINES}
        self.cnt = {e: 0 for e in ENGINES}
        self.lastw = {}
        self.readers = {}
        self.maxwait = {e: {} for e in ENGINES}
        self.dstreams = {}
        self.know = {e: {} for e in ENGINES}
        self.kckpt = {e: [(0, {})] for e in ENGINES}
        self._know_now = {}
        self._kdirty = False

    def _stream(self, name):
        return self.dstreams.setdefault(
            name, {"cnt": 0, "ends": [], "know": [(0, {})]})

    def dcut(self, name):
        st = self._stream(name)
        if not st["ends"] or st["ends"][-1] != st["cnt"]:
            st["ends"].append(st["cnt"])
            st["know"].append((len(st["ends"]) - 1,
                               dict(st.get("_open_know", {}))))

    def _ckpt(self, eng):
        lst = self.kckpt[eng]
        if lst[-1][0] != self.cnt[eng] or len(lst) == 1:
            lst.append((self.cnt[eng], dict(self.know[eng])))
        else:
            lst[-1] = (self.cnt[eng], dict(self.know[eng]))

    def _know_at(self, Y, c):
        lst = self.kckpt[Y]
        lo, hi = 0, len(lst) - 1
        best = lst[0][1]
        while lo <= hi:
            mid = (lo + hi) // 2
            if lst[mid][0] < c:
                best = lst[mid][1]
                lo = mid + 1
            else:
                hi = mid - 1
        return best

    def _merge_know(self, eng, other):
        k = self.know[eng]
        for key, val in other.items():
            if val > k.get(key, -1):
                k[key] = val

    def _dep(self, eng, deps):
        agg = {}
        for tok in deps:
            key = (tok[0], tok[1])
            if key not in agg or tok[2] > agg[key][2]:
                agg[key] = tok
        know = self.know[eng]
        for key, tok in sorted(agg.items(), key=lambda kv: str(kv[0])):
            if tok[2] <= know.get(key, -1):
                continue
            if tok[0] == "E" and tok[1] == eng:
                self.streams[eng].append(("dr",))
                know[key] = self.cnt[eng]
                self._kdirty = True
                continue
            know[key] = tok[2]
            self._kdirty = True
            if tok[0] == "E":
                self.streams[eng].append(("we", tok[1], tok[2]))
                self._merge_know(eng, self._know_at(tok[1], tok[2]))
            else:
                self.streams[eng].append(("wd", tok[1], tok[2]))
                st = self.dstreams[tok[1]]
                gk = st["know"]
                gidx = min(tok[2], len(gk) - 1 - 1) if False else tok[2]
                best = {}
                for gi, kd in gk:
                    if gi <= gidx:
                        best = kd
                    else:
                        break
                self._merge_know(eng, best)

    def _gather(self, reads, writes):
        deps = []
        for r in reads:
            lw = self.lastw.get(r)
            if lw:
                deps.append(lw)
        for w in writes:
            lw = self.lastw.get(w)
            if lw:
                deps.append(lw)
            for tok in self.readers.get(w, {}).values():
                deps.append(tok)
        return deps

    def _register(self, tok, reads, writes):
        for r in reads:
            d = self.readers.setdefault(r, {})
            key = (tok[0], tok[1])
            if key not in d or tok[2] > d[key][2]:
                d[key] = tok
        for w in writes:
            self.lastw[w] = tok
            self.readers[w] = {}

    def op(self, eng, fn, reads=(), writes=()):
        self._kdirty = False
        self._dep(eng, self._gather(reads, writes))
        if self._kdirty:
            self._ckpt(eng)
        self.cnt[eng] += 1
        self.streams[eng].append(("i", fn, eng, 1))
        tok = ("E", eng, self.cnt[eng])
        self._register(tok, reads, writes)
        return tok

    def dma(self, eng, stream, out_ap, in_ap, reads=(), writes=()):
        st0 = self._stream(stream)
        open_tok = ("D", stream, len(st0["ends"]))
        deps = [d for d in self._gather(reads, writes) if d != open_tok]
        self._kdirty = False
        self._dep(eng, deps)
        if self._kdirty:
            self._ckpt(eng)
        st0["cnt"] += 16
        self.streams[eng].append(
            ("i", lambda e, o=out_ap, i=in_ap: e.dma_start(out=o, in_=i),
             stream, 16))
        st0["_open_know"] = dict(self.know[eng])
        tok = ("D", stream, len(st0["ends"]))
        self._register(tok, reads, writes)
        return tok

    def coll(self, eng, stream, fn, reads=(), writes=()):
        self._kdirty = False
        self._dep(eng, self._gather(reads, writes))
        if self._kdirty:
            self._ckpt(eng)
        st = self._stream(stream)
        st["cnt"] += 1
        self.streams[eng].append(("i", fn, stream, 1))
        st["_open_know"] = dict(self.know[eng])
        self.dcut(stream)
        tok = ("D", stream, len(st["ends"]) - 1)
        self._register(tok, reads, writes)
        return tok

    def handoff(self, news, olds):
        rd = {}
        for old in olds:
            lw = self.lastw.pop(old, None)
            if lw:
                key = (lw[0], lw[1])
                if key not in rd or lw[2] > rd[key][2]:
                    rd[key] = lw
            for key, tok in self.readers.pop(old, {}).items():
                if key not in rd or tok[2] > rd[key][2]:
                    rd[key] = tok
        for new in news:
            self.lastw.pop(new, None)
            self.readers[new] = dict(rd)

    def play(self, nc):
        for s in self.dstreams:
            self.dcut(s)
        import contextlib
        with contextlib.ExitStack() as stack:
            sems = {}
            for e in ENGINES:
                sems[("E", e)] = stack.enter_context(nc.semaphore("se_" + e))
            for s in self.dstreams:
                sems[("D", s)] = stack.enter_context(nc.semaphore("sd_" + s))
            block = stack.enter_context(nc.Block())

            def runner(name):
                def run(e):
                    for item in self.streams[name]:
                        kind = item[0]
                        if kind == "dr":
                            e.drain()
                        elif kind == "we":
                            e.wait_ge(sems[("E", item[1])], item[2])
                        elif kind == "wd":
                            st = self.dstreams[item[1]]
                            gidx = min(item[2], len(st["ends"]) - 1)
                            e.wait_ge(sems[("D", item[1])], st["ends"][gidx])
                        else:
                            _, fn, target, inc = item
                            if target in ENGINES:
                                fn(e).then_inc(sems[("E", target)], inc)
                            else:
                                fn(e).then_inc(sems[("D", target)], inc)
                return run

            block.tensor(runner("tensor"))
            block.vector(runner("vector"))
            block.scalar(runner("scalar"))
            block.sync(runner("sync"))
            block.gpsimd(runner("gpsimd"))


def host_constants():
    ut64 = np.triu(np.ones((L, L), np.float32))
    ut64s = np.triu(np.ones((L, L), np.float32), 1)
    lt64s = np.tril(np.ones((L, L), np.float32), -1)
    c = {}
    c["c_identb"] = np.eye(128, dtype=np.float32).astype(BF)
    c["c_identf"] = np.eye(128, dtype=np.float32)
    c["c_ut64x2"] = np.tile(ut64, (2, 1))
    c["c_ones"] = np.ones((128, 64), np.float32)
    c["c_negones"] = -np.ones((128, 64), np.float32)
    c["c_mstrict"] = np.tile(ut64s, (2, 2))
    c["c_mincl"] = np.tile(ut64, (2, 2))
    c["c_mstrictlow"] = np.tile(lt64s, (2, 2))
    blk = np.zeros((128, 128), np.float32)
    blk[0:64, 0:64] = ut64
    blk[64:128, 64:128] = ut64
    c["c_utblk"] = blk
    e63 = np.zeros((128, 2), np.float32)
    e63[63, 0] = 1.0
    e63[127, 1] = 1.0
    c["c_e63"] = e63
    ex2 = np.zeros((2, 128), np.float32)
    ex2[0, 0:64] = 1.0
    ex2[1, 64:128] = 1.0
    c["c_exp2f"] = ex2
    c["c_exp2b"] = ex2.astype(BF)
    c["c_ones1f"] = np.ones((1, 128), np.float32)
    c["c_ones1b"] = np.ones((1, 128), np.float32).astype(BF)
    bd = np.zeros((128, 2), np.float32)
    bd[0:64, 0] = 1.0
    bd[64:128, 1] = 1.0
    c["c_blkdiag2b"] = bd.astype(BF)
    c["c_onesb"] = np.ones((128, 64), np.float32).astype(BF)
    pool = np.tile(np.array([[1e-5, 64e-6, 1e-6, 1.0, float(np.finfo(np.float32).eps)]],
                            np.float32), (128, 1))
    c["c_pool"] = pool
    return c


CONST_SPECS = {
    "c_identb": ([128, 128], BF16), "c_identf": ([128, 128], F32),
    "c_ut64x2": ([128, 64], F32), "c_ones": ([128, 64], F32),
    "c_negones": ([128, 64], F32), "c_mstrict": ([128, 128], F32),
    "c_mincl": ([128, 128], F32), "c_mstrictlow": ([128, 128], F32),
    "c_utblk": ([128, 128], F32), "c_e63": ([128, 2], F32),
    "c_exp2f": ([2, 128], F32), "c_exp2b": ([2, 128], BF16),
    "c_ones1f": ([1, 128], F32), "c_ones1b": ([1, 128], BF16),
    "c_blkdiag2b": ([128, 2], BF16),
    "c_onesb": ([128, 64], BF16),
    "c_pool": ([128, 5], F32),
}

XNK = [tuple(f"xn{p}.{fc}" for fc in range(16)) for p in range(2)]
XNFK = tuple(f"xnf.{fc}" for fc in range(16))
OFMK = tuple(f"ofm.{vc}" for vc in range(16))


def build_program():
    STAGE = float(os.environ.get("MK_STAGE", "9"))
    nc = bass.Bass("TRN2", target_bir_lowering=False, debug=False)
    st = nc._state

    d_xtok = nc.dram_tensor("x_tok", [TOWN, C], F32, kind="ExternalInput")
    d_xfm = nc.dram_tensor("x_fm", [C, TOWN], F32, kind="ExternalInput")
    d_wmix = nc.dram_tensor("w_mix", [C, 772], F32, kind="ExternalInput")
    d_wconv = nc.dram_tensor("w_conv", [512, 4], F32, kind="ExternalInput")
    d_vhead = nc.dram_tensor("vhead", [1, 8], F32, kind="ExternalInput")
    d_wo = nc.dram_tensor("w_o", [C, C], F32, kind="ExternalInput")
    d_w0a = nc.dram_tensor("w0a", [C, 4096], F32, kind="ExternalInput")
    d_w0b = nc.dram_tensor("w0b", [4096, C], F32, kind="ExternalInput")
    d_w1a = nc.dram_tensor("w1a", [C, 4096], F32, kind="ExternalInput")
    d_w1b = nc.dram_tensor("w1b", [4096, C], F32, kind="ExternalInput")
    d_b0a = nc.dram_tensor("b0a_t", [128, 32], F32, kind="ExternalInput")
    d_b1a = nc.dram_tensor("b1a_t", [128, 32], F32, kind="ExternalInput")
    d_b0b = nc.dram_tensor("b0b_t", [128, 16], F32, kind="ExternalInput")
    d_b1b = nc.dram_tensor("b1b_t", [128, 16], F32, kind="ExternalInput")
    d_wg = nc.dram_tensor("wg", [C, 2], F32, kind="ExternalInput")
    d_out = nc.dram_tensor("out_fm", [C, TOWN], F32, kind="ExternalOutput")
    d_const = {k: nc.dram_tensor("d_" + k, sh, dt, kind="ExternalInput")
               for k, (sh, dt) in CONST_SPECS.items()}

    ag_in = nc.dram_tensor("ag_in", [C, TOWN], BF16)
    ag_out = nc.dram_tensor("ag_out", [NCORE * C, TOWN], BF16)
    a2a_in = nc.dram_tensor("a2a_in", [NCORE * 256, TOWN], BF16)
    a2a_out = nc.dram_tensor("a2a_out", [NCORE * 256, TOWN], BF16)
    h0_dram = nc.dram_tensor("h0_dram", [4096, TOWN], BF16)
    h1_dram = nc.dram_tensor("h1_dram", [4096, TOWN], BF16)
    wo_b = nc.dram_tensor("wo_b", [C, C], BF16)
    w0a_b = nc.dram_tensor("w0a_b", [C, 4096], BF16)
    w1a_b = nc.dram_tensor("w1a_b", [C, 4096], BF16)
    w0b_b = nc.dram_tensor("w0b_b", [4096, C], BF16)
    w1b_b = nc.dram_tensor("w1b_b", [4096, C], BF16)

    sb = {}

    def S(name, shape, dtype):
        sb[name] = st.alloc_sbuf_tensor(name, shape, dtype)
        return sb[name]

    for k, (sh, dt) in CONST_SPECS.items():
        S(k, sh, dt)
    S("convw", [128, 16], F32)
    S("vh_row", [1, 8], F32)
    S("vh_all", [128, 8], F32)
    S("negexpA", [128, 2], F32)
    S("biasb", [128, 2], F32)
    S("WBUF", [128, 12352], BF16)
    S("xtile", [128, 2048], F32)
    S("bn24", [128, 24], F32)
    S("mv", [128, 2], F32)
    S("rs", [128, 1], F32)
    S("negmrs", [128, 1], F32)
    S("mtmp", [128, 1], F32)
    S("mtmp2", [128, 1], F32)
    S("xnstage", [128, 2048], BF16)
    S("BB0", [128, 16384], BF16)
    S("ARE", [128, 16384], BF16)
    S("conva", [128, 8 * 515], BF16)
    S("convt", [128, 2 * 512], F32)
    S("qsil", [128, 512], BF16)
    S("ksil", [128, 512], BF16)
    S("vsil", [128, 1024], BF16)
    S("sqbuf", [128, 512], BF16)
    S("rsq2", [2, 512], BF16)
    S("rsqt", [2, 512], F32)
    S("qhat", [128, 512], BF16)
    S("khat", [128, 512], BF16)
    S("Ktok", [128, 512], BF16)
    S("Vtok", [128, 1024], BF16)
    S("gates", [128, 1024], BF16)
    S("thin_sb", [4, 512], F32)
    S("thin_tok", [128, 16], F32)
    S("g_all", [128, 8], F32)
    S("sp_tmp", [128, 8], F32)
    S("sp_tmp2", [128, 8], F32)
    S("sig2_all", [128, 8], F32)
    S("brow2", [128, 64], BF16)
    S("G_all", [128, 8], F32)
    S("eG_all", [128, 8], F32)
    S("negeG_all", [128, 8], F32)
    S("d2_tmp", [128, 8], F32)
    S("eGLmG", [128, 8], F32)
    S("GL2", [2, 8], F32)
    S("GLflat", [1, 16], F32)
    S("eGLflat", [1, 16], F32)
    S("eGLk", [128, 16], F32)
    S("gUT", [128, 2 * 64], F32)
    S("expD", [128, 128], F32)
    S("expDT", [128, 128], F32)
    S("tmpA", [128, 128], F32)
    S("tmpB", [128, 128], F32)
    S("nst", [128, 2 * 384], BF16)
    S("vcur", [128, 2 * 2 * 64], BF16)
    S("attnT", [128, 2 * 64], BF16)
    S("Vsl", [128, 2 * 2 * 128], BF16)
    S("Ubuf", [128, 2 * 128], BF16)
    S("kbar", [128, 2 * 64], BF16)
    S("Otmp", [128, 2 * 128], F32)
    S("Oat", [128, 2 * 128], F32)
    S("Osq", [128, 128], F32)
    S("ssO", [128, 2], F32)
    S("rsO", [128, 2], F32)
    S("rsOt", [128, 2], F32)
    S("ONp", [128, 2 * 128], BF16)
    S("OTr", [128, 2 * 128], BF16)
    S("Ssb", [128, 512], F32)
    S("Ssb_b", [128, 512], BF16)
    S("staging", [128, 1024], BF16)
    S("wg_sb", [128, 32], F32)
    S("wd_f", [128, 16], F32)
    S("wd_b", [128, 16], BF16)
    S("FSCR", [128, 8192], BF16)
    S("mrow", [1, 256], BF16)
    S("mball", [128, 1024], BF16)
    S("b0a_sb", [128, 32], F32)
    S("b1a_sb", [128, 32], F32)
    S("b0b_sb", [128, 16], F32)
    S("b1b_sb", [128, 16], F32)
    S("hstage", [128, 4 * 256], BF16)
    S("hbuf", [128, 8192], BF16)
    S("fb0", [128, 128], F32)
    S("fb1", [128, 128], F32)
    S("fb2", [128, 128], F32)
    S("fbo", [128, 2 * 128], F32)
    S("xfr", [128, 2 * 128], F32)

    ps = {}

    def P(name, shape, dtype=F32):
        ps[name] = st.alloc_psum_tensor(name, shape, dtype)
        return ps[name]

    P("PB0", [128, 512])
    P("PB1", [128, 512])
    P("PB2", [128, 512])
    P("PB3", [128, 512])
    P("PB4", [128, 512])
    P("PB5", [128, 512])
    P("PB6", [128, 512])
    P("PB7", [128, 512])

    E = Emitter()

    def dma(eng, stream, out_ap, in_ap, reads, writes):
        E.dma(eng, stream, out_ap, in_ap, reads=reads, writes=writes)

    col8 = lambda h, subq: 4 * h + subq


    def _finish_partial():
        E.dma("sync", "ydbg", d_out[0:128, 0:128], sb["Ssb"][:, 0:128],
              reads=(), writes=("dbgout",))
        E.dcut("ydbg")
        E.op("sync", lambda e: e.nop(), reads=("dbgout",), writes=())
        E.play(nc)
        return nc

    for k in CONST_SPECS:
        dma("sync", "yc", sb[k][:, :], d_const[k].ap(), (), (k,))
    for ccv in range(4):
        dma("sync", "yc", sb["convw"][:, 4 * ccv:4 * ccv + 4],
            d_wconv[128 * ccv:128 * ccv + 128, :], (), ("convw",))
    dma("sync", "yc", sb["vh_row"][:, :], d_vhead.ap(), (), ("vh_row",))
    dma("sync", "yc", sb["wg_sb"][:, :],
        d_wg.ap().rearrange("(a p) c -> p a c", p=128), (), ("wg_sb",))
    dma("sync", "yc", sb["b0a_sb"][:, :], d_b0a.ap(), (), ("b0a_sb",))
    dma("sync", "yc", sb["b1a_sb"][:, :], d_b1a.ap(), (), ("b1a_sb",))
    dma("sync", "yc", sb["b0b_sb"][:, :], d_b0b.ap(), (), ("b0b_sb",))
    dma("sync", "yc", sb["b1b_sb"][:, :], d_b1b.ap(), (), ("b1b_sb",))
    E.dcut("yc")
    for fc in range(16):
        dma("gpsimd", "gw", sb["WBUF"][:, 772 * fc:772 * (fc + 1)],
            d_wmix[128 * fc:128 * (fc + 1), :], (), ("wmix",))
    for src, dst, key in ((d_wo, wo_b, "wob"), (d_w0a, w0a_b, "w0ab"),
                          (d_w1a, w1a_b, "w1ab"), (d_w0b, w0b_b, "w0bb"),
                          (d_w1b, w1b_b, "w1bb")):
        rows = src.ap().shape[0]
        q = rows // 4
        for i in range(4):
            dma("gpsimd", "gc", dst[q * i:q * (i + 1), :], src[q * i:q * (i + 1), :],
                (), (f"{key}.{i}",))
    E.dcut("gw")
    E.dcut("gc")
    WOBK = tuple(f"wob.{i}" for i in range(4))
    W0AK = tuple(f"w0ab.{i}" for i in range(4))
    W1AK = tuple(f"w1ab.{i}" for i in range(4))
    W0BK = tuple(f"w0bb.{i}" for i in range(4))
    W1BK = tuple(f"w1bb.{i}" for i in range(4))

    if STAGE < 0.3:
        return _finish_partial()
    E.op("tensor", lambda e: e.matmul(ps["PB5"][:, 0:8], sb["c_ones1f"][:, :],
                                      sb["vh_row"][:, :], start=True, stop=True),
         ("c_ones1f", "vh_row"), ("B5",))
    E.op("scalar", lambda e: e.copy(sb["vh_all"][:, :], ps["PB5"][:, 0:8]),
         ("B5",), ("vh_all",))
    if STAGE < 0.35:
        return _finish_partial()
    E.op("scalar", lambda e: e.activation(sb["negexpA"][:, :], sb["vh_all"][:, 0:2], AF.Exp),
         ("vh_all",), ("negexpA",))
    E.op("scalar", lambda e: e.mul(sb["negexpA"][:, :], sb["negexpA"][:, :], -1.0),
         ("negexpA",), ("negexpA",))
    if STAGE < 0.4:
        return _finish_partial()
    E.op("vector", lambda e: e.tensor_add(sb["biasb"][:, :], sb["vh_all"][:, 4:6],
                                          sb["vh_all"][:, 6:8]),
         ("vh_all",), ("biasb",))
    if STAGE < 0.45:
        return _finish_partial()
    E.op("vector", lambda e: e.tensor_tensor(
            sb["wd_f"][:, :],
            bass.AP(sb["wg_sb"], 0, [[32, 128], [2, 16]]),
            bass.AP(sb["wg_sb"], 1, [[32, 128], [2, 16]]), ALU.subtract),
         ("wg_sb",), ("wd_f",))
    E.op("scalar", lambda e: e.copy(sb["wd_b"][:, :], sb["wd_f"][:, :]),
         ("wd_f",), ("wd_b",))
    if STAGE < 0.5:
        return _finish_partial()
    E.op("vector", lambda e: e.memset(sb["Ssb"][:, 0:256], 0.0), (), ("S.0.0", "S.0.1"))
    E.op("vector", lambda e: e.memset(sb["Ssb"][:, 256:512], 0.0), (), ("S.1.0", "S.1.1"))
    E.op("vector", lambda e: e.memset(sb["Ssb_b"][:, 0:256], 0.0), (), ("Sb.0.0", "Sb.0.1"))
    E.op("vector", lambda e: e.memset(sb["Ssb_b"][:, 256:512], 0.0), (), ("Sb.1.0", "Sb.1.1"))

    if STAGE < 0.6:
        return _finish_partial()
    _ln_iters = 8 if STAGE >= 0.74 else (4 if STAGE >= 0.72 else (2 if STAGE >= 0.71 else 1))
    for i in range(_ln_iters):
        dma("sync", "yx", sb["xtile"][:, :], d_xtok[128 * i:128 * (i + 1), :], (), ("xtile",))
        E.dcut("yx")
        for q in range(4):
            E.op("vector", lambda e, q=q: e.bn_stats(
                    sb["bn24"][:, 6 * q:6 * (q + 1)],
                    sb["xtile"][:, 512 * q:512 * (q + 1)]),
                 ("xtile",), (f"bn24.{q}",))
        E.op("vector", lambda e: e.bn_aggr(sb["mv"][:, :], sb["bn24"][:, :]),
             ("bn24.0", "bn24.1", "bn24.2", "bn24.3"), ("mv",))
        E.op("scalar", lambda e: e.activation(sb["mtmp2"][:, :], sb["mv"][:, 1:2],
                                              AF.Sqrt, bias=sb["c_pool"][:, 0:1]),
             ("mv", "c_pool"), ("mtmp2",))
        E.op("vector", lambda e: e.reciprocal(sb["rs"][:, :], sb["mtmp2"][:, :]),
             ("mtmp2",), ("rs",))
        E.op("vector", lambda e: e.tensor_mul(sb["mtmp"][:, :], sb["mv"][:, 0:1],
                                              sb["rs"][:, :]),
             ("mv", "rs"), ("mtmp",))
        E.op("vector", lambda e: e.tensor_scalar_mul(sb["negmrs"][:, :],
                                                     sb["mtmp"][:, :], -1.0),
             ("mtmp",), ("negmrs",))
        E.op("vector", lambda e: e.tensor_scalar(
                sb["xnstage"][:, :], sb["xtile"][:, :],
                sb["rs"][:, 0:1], sb["negmrs"][:, 0:1], ALU.mult, ALU.add),
             ("xtile", "rs", "negmrs"), ("xnstage",))
        for fc in range(0 if (STAGE < 0.7 or abs(STAGE - 0.74) < 1e-9) else 16):
            slot = fc % 2
            pso = (ps["PB6"] if slot == 0 else ps["PB7"])[:, 0:128]
            E.op("tensor", lambda e, fc=fc, pso=pso: e.matmul(
                    pso, sb["xnstage"][:, 128 * fc:128 * (fc + 1)], sb["c_identb"][:, :],
                    start=True, stop=True),
                 ("xnstage", "c_identb"), ("B6" if slot == 0 else "B7",))
            E.op("scalar", lambda e, fc=fc, i=i, pso=pso: e.copy(
                    sb["ARE"][:, 1024 * fc + 128 * i:1024 * fc + 128 * (i + 1)], pso),
                 ("B6" if slot == 0 else "B7",), (f"LNare.{fc}.{i}",))
    if STAGE < 0.8:
        return _finish_partial()
    for fc in range(16):
        dma("sync", "ya", ag_in[128 * fc:128 * (fc + 1), :],
            sb["ARE"][:, 1024 * fc:1024 * (fc + 1)],
            tuple(f"LNare.{fc}.{i}" for i in range(8)), (f"agin.{fc}",))
    E.dcut("ya")
    AGINK = tuple(f"agin.{fc}" for fc in range(16))
    if STAGE < 0.9:
        return _finish_partial()
    E.coll("gpsimd", "cag", lambda e: e.collective_compute(
            "AllGather", ALU.bypass, replica_groups=[list(range(NCORE))],
            ins=[ag_in.ap().opt()], outs=[ag_out.ap().opt()]),
         AGINK, ("agout",))

 
    if STAGE < 2:
        return _finish_partial()

    for j in range(NT):
        pj = j % 2
        rblk = j // 2
        seq = j // 8
        jj = j % 8
        xnbuf = sb["BB0"][:, 8192 * pj:8192 * (pj + 1)]
        for fc in range(16):
            dma("sync", f"yn{pj}", xnbuf[:, 512 * fc:512 * (fc + 1)],
                ag_out[2048 * rblk + 128 * fc:2048 * rblk + 128 * (fc + 1),
                       512 * pj:512 * pj + 512],
                ("agout",), (XNK[pj][fc],))
        E.dcut(f"yn{pj}")

        for oc in range(7):
            bank = ps["PB0"] if oc % 2 == 0 else ps["PB1"]
            bkk = f"PB{oc % 2}"
            m0 = 768 if oc == 6 else 128 * oc
            mrows = 4 if oc == 6 else 128

            def projgroup(e, m0=m0, mrows=mrows, bank=bank, xnbuf=xnbuf):
                inst = None
                for fc in range(16):
                    inst = e.matmul(
                        bank[0:mrows, 0:512],
                        sb["WBUF"][:, 772 * fc + m0:772 * fc + m0 + mrows],
                        xnbuf[:, 512 * fc:512 * (fc + 1)],
                        start=(fc == 0), stop=(fc == 15))
                return inst
            E.op("tensor", projgroup, ("wmix",) + XNK[pj], (bkk,))

            if oc <= 3:
                cb = sb["conva"][:, 515 * (oc + 4 * pj):515 * (oc + 4 * pj) + 515]
                cbp = sb["conva"][:, 515 * (oc + 4 * (1 - pj)):515 * (oc + 4 * (1 - pj)) + 515]
                kcb = f"conv{oc}.{pj}"
                kcbp = f"conv{oc}.{1 - pj}"
                if jj == 0:
                    E.op("vector", lambda e, cb=cb: e.memset(cb[:, 0:3], 0.0),
                         (), (kcb,))
                else:
                    E.op("vector", lambda e, cb=cb, cbp=cbp: e.tensor_copy(
                            cb[:, 0:3], cbp[:, 512:515]), (kcbp,), (kcb,))
                E.op("scalar", lambda e, cb=cb, bank=bank: e.copy(
                        cb[:, 3:515], bank[:, 0:512]), (bkk,), (kcb,))
                ct = sb["convt"][:, 512 * (oc % 2):512 * (oc % 2) + 512]
                kct = f"convt{oc % 2}"
                E.op("vector", lambda e, cb=cb, ct=ct, oc=oc: e.tensor_scalar_mul(
                        ct, cb[:, 0:512], sb["convw"][:, 4 * oc:4 * oc + 1]),
                     (kcb, "convw"), (kct,))
                for tap in range(1, 4):
                    E.op("vector", lambda e, cb=cb, ct=ct, oc=oc, tap=tap:
                         e.scalar_tensor_tensor(
                             ct, cb[:, tap:tap + 512],
                             sb["convw"][:, 4 * oc + tap:4 * oc + tap + 1],
                             ct, ALU.mult, ALU.add),
                         (kcb, "convw", kct), (kct,))
                dstb = [sb["qsil"][:, :], sb["ksil"][:, :],
                        sb["vsil"][:, 0:512], sb["vsil"][:, 512:1024]][oc]
                kdst = ["qsil", "ksil", "vsil0", "vsil1"][oc]
                E.op("scalar", lambda e, ct=ct, dstb=dstb: e.activation(
                        dstb, ct, AF.Silu), (kct,), (kdst,))
            elif oc <= 5:
                h = oc - 4
                E.op("scalar", lambda e, bank=bank, h=h: e.activation(
                        sb["gates"][:, 512 * h:512 * (h + 1)], bank[:, 0:512],
                        AF.Silu), (bkk,), (f"gate{h}",))
            else:
                E.op("scalar", lambda e, bank=bank: e.copy(
                        sb["thin_sb"][:, :], bank[0:4, 0:512]), (bkk,), ("thin_sb",))

        for which in range(2):
            src = sb["qsil"] if which == 0 else sb["ksil"]
            ksrc = "qsil" if which == 0 else "ksil"
            dst = sb["qhat"] if which == 0 else sb["khat"]
            hatk = "qhat" if which == 0 else "khat"
            scl = 64.0 if which == 0 else 1.0
            biap = sb["c_pool"][0:2, 1:2] if which == 0 else sb["c_pool"][0:2, 2:3]
            E.op("vector", lambda e, src=src: e.tensor_mul(
                    sb["sqbuf"][:, :], src[:, :], src[:, :]), (ksrc,), ("sqbuf",))
            E.op("tensor", lambda e: e.matmul(
                    ps["PB5"][0:2, 0:512], sb["c_blkdiag2b"][:, :], sb["sqbuf"][:, :],
                    start=True, stop=True),
                 ("c_blkdiag2b", "sqbuf"), ("B5",))
            E.op("scalar", lambda e, scl=scl, biap=biap: e.activation(
                    sb["rsqt"][:, :], ps["PB5"][0:2, 0:512], AF.Sqrt,
                    bias=biap, scale=scl),
                 ("B5", "c_pool"), ("rsqt",))
            def _recip_b(e):
                with nc.allow_low_precision(reason="bf16 l2norm scale operand"):
                    return e.reciprocal(sb["rsq2"][:, :], sb["rsqt"][:, :])
            E.op("vector", _recip_b, ("rsqt",), ("rsq2",))
            E.op("tensor", lambda e: e.matmul(
                    ps["PB5"][:, 0:512], sb["c_exp2b"][:, :], sb["rsq2"][:, :],
                    start=True, stop=True),
                 ("c_exp2b", "rsq2"), ("B5",))
            E.op("vector", lambda e, src=src, dst=dst: e.tensor_mul(
                    dst[:, :], src[:, :], ps["PB5"][:, 0:512]),
                 (ksrc, "B5"), (hatk,))

        for subq in range(4):
            slot = subq % 2
            pso = (ps["PB6"] if slot == 0 else ps["PB7"])[:, 0:128]
            E.op("tensor", lambda e, subq=subq, pso=pso: e.matmul(
                    pso, sb["khat"][:, 128 * subq:128 * (subq + 1)], sb["c_identb"][:, :],
                    start=True, stop=True),
                 ("khat", "c_identb"), ("B6" if slot == 0 else "B7",))
            E.op("scalar", lambda e, subq=subq, pso=pso: e.copy(
                    sb["Ktok"][:, 128 * subq:128 * (subq + 1)], pso),
                 ("B6" if slot == 0 else "B7",), (f"Ktok.{subq}",))
        for h in range(2):
            for subq in range(4):
                slot = (h * 4 + subq) % 2
                pso = (ps["PB6"] if slot == 0 else ps["PB7"])[:, 0:128]
                E.op("tensor", lambda e, h=h, subq=subq, pso=pso: e.matmul(
                        pso, sb["vsil"][:, 512 * h + 128 * subq:512 * h + 128 * (subq + 1)],
                        sb["c_identb"][:, :], start=True, stop=True),
                     (f"vsil{h}", "c_identb"), ("B6" if slot == 0 else "B7",))
                E.op("scalar", lambda e, h=h, subq=subq, pso=pso: e.copy(
                        sb["Vtok"][:, 128 * col8(h, subq):128 * (col8(h, subq) + 1)], pso),
                     ("B6" if slot == 0 else "B7",), (f"Vtok.{col8(h, subq)}",))
        for subq in range(4):
            tslot = subq % 2
            pso = ps["PB4"][:, 384 + 4 * tslot:384 + 4 * (tslot + 1)]
            E.op("tensor", lambda e, subq=subq, pso=pso: e.matmul(
                    pso, sb["thin_sb"][:, 128 * subq:128 * (subq + 1)],
                    sb["c_identf"][0:4, 0:4], start=True, stop=True),
                 ("thin_sb", "c_identf"), ("B4",))
            E.op("vector", lambda e, subq=subq, pso=pso: e.tensor_copy(
                    bass.AP(sb["thin_tok"], subq, [[16, 128], [4, 4]]), pso),
                 ("B4",), ("thin_tok",))

        for h in range(2):
            E.op("vector", lambda e, h=h: e.tensor_scalar_add(
                    sb["sp_tmp"][:, 4 * h:4 * h + 4], sb["thin_tok"][:, 4 * h:4 * (h + 1)],
                    sb["vh_all"][:, 2 + h:3 + h]),
                 ("thin_tok", "vh_all"), ("sp_tmp",))
        E.op("scalar", lambda e: e.activation(sb["sp_tmp2"][:, :], sb["sp_tmp"][:, :], AF.Abs),
             ("sp_tmp",), ("sp_tmp2",))
        E.op("scalar", lambda e: e.activation(sb["sp_tmp2"][:, :], sb["sp_tmp2"][:, :],
                                              AF.Exp, scale=-1.0),
             ("sp_tmp2",), ("sp_tmp2",))
        E.op("scalar", lambda e: e.activation(sb["sp_tmp2"][:, :], sb["sp_tmp2"][:, :],
                                              AF.Ln, bias=sb["c_pool"][:, 3:4]),
             ("sp_tmp2", "c_pool"), ("sp_tmp2",))
        E.op("scalar", lambda e: e.activation(sb["sp_tmp"][:, :], sb["sp_tmp"][:, :], AF.Relu),
             ("sp_tmp",), ("sp_tmp",))
        E.op("vector", lambda e: e.tensor_add(sb["sp_tmp"][:, :], sb["sp_tmp"][:, :],
                                              sb["sp_tmp2"][:, :]),
             ("sp_tmp", "sp_tmp2"), ("sp_tmp",))
        for h in range(2):
            E.op("vector", lambda e, h=h: e.tensor_scalar_mul(
                    sb["g_all"][:, 4 * h:4 * h + 4], sb["sp_tmp"][:, 4 * h:4 * h + 4],
                    sb["negexpA"][:, h:h + 1]),
                 ("sp_tmp", "negexpA"), ("g_all",))
        for h in range(2):
            E.op("vector", lambda e, h=h: e.tensor_scalar_add(
                    sb["sp_tmp2"][:, 4 * h:4 * h + 4],
                    sb["thin_tok"][:, 8 + 4 * h:12 + 4 * h],
                    sb["biasb"][:, h:h + 1]),
                 ("thin_tok", "biasb"), ("sp_tmp2",))
        E.op("scalar", lambda e: e.activation(sb["sig2_all"][:, :], sb["sp_tmp2"][:, :],
                                              AF.Sigmoid),
             ("sp_tmp2",), ("sig2_all",))
        E.op("vector", lambda e: e.tensor_scalar_mul(sb["sig2_all"][:, :],
                                                     sb["sig2_all"][:, :], 2.0),
             ("sig2_all",), ("sig2_all",))
        E.op("tensor", lambda e: e.matmul(ps["PB4"][:, 392:400], sb["c_utblk"][:, :],
                                          sb["g_all"][:, :], start=True, stop=True),
             ("c_utblk", "g_all"), ("B4",))
        E.op("scalar", lambda e: e.copy(sb["G_all"][:, :], ps["PB4"][:, 392:400]),
             ("B4",), ("G_all",))
        E.op("scalar", lambda e: e.activation(sb["eG_all"][:, :], sb["G_all"][:, :], AF.Exp),
             ("G_all",), ("eG_all",))
        E.op("vector", lambda e: e.tensor_scalar_mul(sb["negeG_all"][:, :],
                                                     sb["eG_all"][:, :], -1.0),
             ("eG_all",), ("negeG_all",))
        E.op("tensor", lambda e: e.matmul(ps["PB4"][0:2, 400:408], sb["c_e63"][:, :],
                                          sb["G_all"][:, :], start=True, stop=True),
             ("c_e63", "G_all"), ("B4",))
        E.op("vector", lambda e: e.tensor_copy(sb["GL2"][:, :], ps["PB4"][0:2, 400:408]),
             ("B4",), ("GL2",))
        E.op("tensor", lambda e: e.matmul(ps["PB4"][:, 408:416], sb["c_exp2f"][:, :],
                                          sb["GL2"][:, :], start=True, stop=True),
             ("GL2", "c_exp2f"), ("B4",))
        E.op("vector", lambda e: e.tensor_sub(sb["d2_tmp"][:, :], ps["PB4"][:, 408:416],
                                              sb["G_all"][:, :]),
             ("B4", "G_all"), ("d2_tmp",))
        E.op("scalar", lambda e: e.activation(sb["eGLmG"][:, :], sb["d2_tmp"][:, :], AF.Exp),
             ("d2_tmp",), ("eGLmG",))
        for p in range(2):
            E.op("tensor", lambda e, p=p: e.matmul(
                    ps["PB4"][0:1, 416 + 8 * p:424 + 8 * p], sb["c_e63"][:, p:p + 1],
                    sb["G_all"][:, :], start=True, stop=True),
                 ("c_e63", "G_all"), ("B4",))
            E.op("vector", lambda e, p=p: e.tensor_copy(
                    sb["GLflat"][:, 8 * p:8 * (p + 1)],
                    ps["PB4"][0:1, 416 + 8 * p:424 + 8 * p]),
                 ("B4",), ("GLflat",))
        E.op("scalar", lambda e: e.activation(sb["eGLflat"][:, :], sb["GLflat"][:, :], AF.Exp),
             ("GLflat",), ("eGLflat",))
        E.op("tensor", lambda e: e.matmul(ps["PB4"][:, 432:448], sb["c_ones1f"][:, :],
                                          sb["eGLflat"][:, :], start=True, stop=True),
             ("c_ones1f", "eGLflat"), ("B4",))
        E.op("vector", lambda e: e.tensor_copy(sb["eGLk"][:, :], ps["PB4"][:, 432:448]),
             ("B4",), ("eGLk",))

        for ccq in range(8):
            subq = ccq // 2
            p = ccq % 2
            if p == 0:
                for h in range(2):
                    E.op("vector", lambda e, h=h, subq=subq: e.tensor_scalar_mul(
                            sb["gUT"][:, 64 * h:64 * (h + 1)], sb["c_ut64x2"][:, :],
                            sb["g_all"][:, col8(h, subq):col8(h, subq) + 1]),
                         ("g_all", "c_ut64x2"), (f"gUT{h}",))
                for pq in range(2):
                    for h in range(2):
                        def dmm(e, pq=pq, h=h):
                            e.matmul(ps["PB2"][64 * pq:64 * pq + 64, 64 * h:64 * h + 64],
                                     sb["c_ones"][64 * pq:64 * pq + 64, :],
                                     sb["gUT"][64 * pq:64 * pq + 64, 64 * h:64 * h + 64],
                                     start=True, stop=False)
                            return e.matmul(
                                ps["PB2"][64 * pq:64 * pq + 64, 64 * h:64 * h + 64],
                                sb["gUT"][64 * pq:64 * pq + 64, 64 * h:64 * h + 64],
                                sb["c_negones"][64 * pq:64 * pq + 64, 0:64],
                                start=False, stop=True)
                        E.op("tensor", dmm, (f"gUT{h}", "c_ones", "c_negones"),
                             ("B2",))
                        tok0 = 128 * subq + 64 * pq
                        E.op("tensor", lambda e, pq=pq, h=h, tok0=tok0: e.matmul(
                                ps["PB2"][64 * pq:64 * pq + 64,
                                          128 + 64 * h:128 + 64 * h + 64],
                                sb["khat"][64 * h:64 * h + 64, tok0:tok0 + 64],
                                sb["khat"][64 * h:64 * h + 64, tok0:tok0 + 64],
                                start=True, stop=True),
                             ("khat",), ("B2",))
                        E.op("tensor", lambda e, pq=pq, h=h, tok0=tok0: e.matmul(
                                ps["PB2"][64 * pq:64 * pq + 64,
                                          256 + 64 * h:256 + 64 * h + 64],
                                sb["khat"][64 * h:64 * h + 64, tok0:tok0 + 64],
                                sb["qhat"][64 * h:64 * h + 64, tok0:tok0 + 64],
                                start=True, stop=True),
                             ("khat", "qhat"), ("B2",))
                    for h in range(2):
                        bsl = (2 * pq + h) % 2
                        E.op("tensor", lambda e, pq=pq, h=h, subq=subq, bsl=bsl: e.matmul(
                                ps["PB4"][32 * bsl:32 * bsl + 1, 448:512],
                                sb["sig2_all"][64 * pq:64 * pq + 64,
                                               col8(h, subq):col8(h, subq) + 1],
                                sb["c_identf"][64 * pq:64 * pq + 64,
                                               64 * pq:64 * pq + 64],
                                start=True, stop=True),
                             ("sig2_all", "c_identf"), ("B4",))
                        E.op("scalar", lambda e, bsl=bsl: e.copy(
                                sb["brow2"][32 * bsl:32 * bsl + 1, :],
                                ps["PB4"][32 * bsl:32 * bsl + 1, 448:512]),
                             ("B4",), (f"brow{bsl}",))
                        E.op("tensor", lambda e, pq=pq, h=h, bsl=bsl: e.matmul(
                                ps["PB2"][64 * pq:64 * pq + 64,
                                          384 + 64 * h:384 + 64 * h + 64],
                                sb["c_onesb"][32 * bsl:32 * bsl + 1, 0:64],
                                sb["brow2"][32 * bsl:32 * bsl + 1, :],
                                start=True, stop=True),
                             (f"brow{bsl}", "c_onesb"), ("B2",))
                dquads = tuple("B2" for pq in range(2) for h in range(2))
                kkq = tuple("B2" for pq in range(2) for h in range(2))
                kqq = tuple("B2" for pq in range(2) for h in range(2))
                bbq = tuple("B2" for pq in range(2) for h in range(2))
                E.op("vector", lambda e: e.tensor_scalar_min(
                        sb["tmpA"][:, :], ps["PB2"][:, 0:128], 0.0),
                     dquads, ("tmpA",))
                E.op("scalar", lambda e: e.activation(sb["expD"][:, :],
                                                      sb["tmpA"][:, :], AF.Exp),
                     ("tmpA",), ("expD",))
                E.op("vector", lambda e: e.tensor_scalar_max(
                        sb["tmpB"][:, :], ps["PB2"][:, 0:128], 0.0),
                     dquads, ("tmpB",))
                E.op("scalar", lambda e: e.activation(sb["expDT"][:, :],
                                                      sb["tmpB"][:, :], AF.Exp, scale=-1.0),
                     ("tmpB",), ("expDT",))
                E.op("vector", lambda e: e.tensor_mul(sb["tmpA"][:, :], sb["expD"][:, :],
                                                      ps["PB2"][:, 128:256]),
                     ("expD",) + kkq, ("tmpA",))
                E.op("vector", lambda e: e.tensor_mul(sb["tmpB"][:, :], sb["tmpA"][:, :],
                                                      sb["c_mstrict"][:, :]),
                     ("tmpA", "c_mstrict"), ("tmpB",))
                for h in range(2):
                    E.op("vector", lambda e, h=h, subq=subq: e.tensor_scalar_mul(
                            sb["nst"][:, 384 * h:384 * h + 64],
                            sb["tmpB"][:, 64 * h:64 * h + 64],
                            sb["sig2_all"][:, col8(h, subq):col8(h, subq) + 1]),
                         ("tmpB", "sig2_all"), (f"nst{h}.0",))
                E.op("vector", lambda e: e.tensor_mul(sb["tmpA"][:, :], sb["expD"][:, :],
                                                      sb["c_mincl"][:, :]),
                     ("expD", "c_mincl"), ("tmpA",))
                for h in range(2):
                    E.op("vector", lambda e, h=h: e.tensor_mul(
                            sb["attnT"][:, 64 * h:64 * h + 64],
                            sb["tmpA"][:, 64 * h:64 * h + 64],
                            ps["PB2"][:, 256 + 64 * h:256 + 64 * h + 64]),
                         ("tmpA",) + kqq, (f"attnT{h}",))
                E.op("vector", lambda e: e.tensor_mul(sb["tmpB"][:, :], sb["expDT"][:, :],
                                                      ps["PB2"][:, 128:256]),
                     ("expDT",) + kkq, ("tmpB",))
                E.op("vector", lambda e: e.tensor_mul(sb["tmpA"][:, :], sb["tmpB"][:, :],
                                                      sb["c_mstrictlow"][:, :]),
                     ("tmpB", "c_mstrictlow"), ("tmpA",))
                for h in range(2):
                    E.op("vector", lambda e, h=h: e.tensor_mul(
                            sb["vcur"][:, 128 * h:128 * h + 64],
                            sb["tmpA"][:, 64 * h:64 * h + 64],
                            ps["PB2"][:, 384 + 64 * h:384 + 64 * h + 64]),
                         ("tmpA",) + bbq, (f"vcur{h}.0",))
                for h in range(2):
                    for lvl in range(5):
                        vp = lvl % 2
                        ncur = sb["nst"][:, 384 * h + 64 * lvl:384 * h + 64 * (lvl + 1)]
                        nnew = sb["nst"][:, 384 * h + 64 * (lvl + 1):384 * h + 64 * (lvl + 2)]
                        vc_ = sb["vcur"][:, 128 * h + 64 * vp:128 * h + 64 * vp + 64]
                        vn_ = sb["vcur"][:, 128 * h + 64 * (1 - vp):128 * h + 64 * (1 - vp) + 64]

                        def nmm(e, ncur=ncur, vc_=vc_, h=h):
                            i1 = None
                            for pq in range(2):
                                i1 = e.matmul(
                                    ps["PB4"][64 * pq:64 * pq + 64, 64 * h:64 * h + 64],
                                    vc_[64 * pq:64 * pq + 64, :],
                                    ncur[64 * pq:64 * pq + 64, :],
                                    start=True, stop=True)
                            return i1
                        E.op("tensor", nmm, (f"nst{h}.{lvl}", f"vcur{h}.{vp}"),
                             ("B4",))
                        E.op("scalar", lambda e, nnew=nnew, h=h: e.activation(
                                nnew, ps["PB4"][:, 64 * h:64 * h + 64], AF.Copy, scale=-1.0),
                             ("B4",), (f"nst{h}.{lvl + 1}",))
                        if lvl < 4:
                            def vmm(e, ncur=ncur, vc_=vc_, h=h):
                                i1 = None
                                for pq in range(2):
                                    i1 = e.matmul(
                                        ps["PB4"][64 * pq:64 * pq + 64,
                                                  128 + 64 * h:128 + 64 * h + 64],
                                        ncur[64 * pq:64 * pq + 64, :],
                                        vc_[64 * pq:64 * pq + 64, :],
                                        start=True, stop=True)
                                return i1
                            E.op("tensor", vmm, (f"nst{h}.{lvl}", f"vcur{h}.{vp}"),
                                 ("B4",))
                            E.op("scalar", lambda e, vn_=vn_, h=h: e.activation(
                                    vn_, ps["PB4"][:, 128 + 64 * h:128 + 64 * h + 64],
                                    AF.Copy, scale=-1.0),
                                 ("B4",), (f"vcur{h}.{1 - vp}",))

            gchunk = 8 * jj + ccq
            sslot = gchunk % 2
            for h in range(2):
                c8 = col8(h, subq)
                sread = sb["Ssb_b"][64 * h:64 * h + 64,
                                    256 * sslot + 128 * seq:256 * sslot + 128 * seq + 128]
                sreadf = sb["Ssb"][64 * h:64 * h + 64,
                                   256 * sslot + 128 * seq:256 * sslot + 128 * seq + 128]
                swrite = sb["Ssb"][64 * h:64 * h + 64,
                                   256 * (1 - sslot) + 128 * seq:
                                   256 * (1 - sslot) + 128 * seq + 128]
                swrite_b = sb["Ssb_b"][64 * h:64 * h + 64,
                                       256 * (1 - sslot) + 128 * seq:
                                       256 * (1 - sslot) + 128 * seq + 128]
                ksr = f"Sb.{sslot}.{h}"
                ksrf = f"S.{sslot}.{h}"
                ksw = f"S.{1 - sslot}.{h}"
                kswb = f"Sb.{1 - sslot}.{h}"
                tok0 = 64 * ccq
                E.op("tensor", lambda e, h=h, tok0=tok0, p=p, sread=sread: e.matmul(
                        ps["PB3"][64 * p:64 * p + 64, 0:128],
                        sb["khat"][64 * h:64 * h + 64, tok0:tok0 + 64],
                        sread, start=True, stop=True),
                     ("khat", ksr), ("B3",))
                E.op("vector", lambda e, h=h, p=p, c8=c8: e.scalar_tensor_tensor(
                        sb["Vsl"][64 * p:64 * p + 64, 256 * h:256 * h + 128],
                        ps["PB3"][64 * p:64 * p + 64, 0:128],
                        sb["negeG_all"][64 * p:64 * p + 64, c8:c8 + 1],
                        sb["Vtok"][64 * p:64 * p + 64, 128 * c8:128 * c8 + 128],
                        ALU.mult, ALU.add),
                     ("B3", "negeG_all", f"Vtok.{c8}"), (f"Vsl{h}.0",))
                for step in range(6):
                    lvl = 5 - step
                    src = sb["Vsl"][:, 256 * h + 128 * (step % 2):
                                    256 * h + 128 * (step % 2) + 128]
                    dst = sb["Vsl"][:, 256 * h + 128 * ((step + 1) % 2):
                                    256 * h + 128 * ((step + 1) % 2) + 128]
                    nsl = sb["nst"][:, 384 * h + 64 * lvl:384 * h + 64 * (lvl + 1)]
                    E.op("tensor", lambda e, p=p, nsl=nsl, src=src: e.matmul(
                            ps["PB3"][64 * p:64 * p + 64, 128:256],
                            nsl[64 * p:64 * p + 64, :],
                            src[64 * p:64 * p + 64, :], start=True, stop=True),
                         (f"nst{h}.{lvl}", f"Vsl{h}.{step % 2}"), ("B3",))
                    E.op("vector", lambda e, p=p, src=src, dst=dst: e.scalar_tensor_tensor(
                            dst[64 * p:64 * p + 64, :],
                            ps["PB3"][64 * p:64 * p + 64, 128:256], -1.0,
                            src[64 * p:64 * p + 64, :], ALU.mult, ALU.add),
                         ("B3", f"Vsl{h}.{step % 2}"), (f"Vsl{h}.{(step + 1) % 2}",))
                E.op("vector", lambda e, h=h, p=p, c8=c8: e.tensor_scalar_mul(
                        sb["Ubuf"][64 * p:64 * p + 64, 128 * h:128 * h + 128],
                        sb["Vsl"][64 * p:64 * p + 64, 256 * h:256 * h + 128],
                        sb["sig2_all"][64 * p:64 * p + 64, c8:c8 + 1]),
                     (f"Vsl{h}.0", "sig2_all"), (f"Ubuf{h}",))
                E.op("tensor", lambda e, h=h, p=p: e.matmul(
                        ps["PB3"][64 * p:64 * p + 64, 256:384],
                        sb["attnT"][64 * p:64 * p + 64, 64 * h:64 * h + 64],
                        sb["Ubuf"][64 * p:64 * p + 64, 128 * h:128 * h + 128],
                        start=True, stop=True),
                     (f"attnT{h}", f"Ubuf{h}"), ("B3",))
                E.op("tensor", lambda e, h=h, p=p, tok0=tok0, sread=sread: e.matmul(
                        ps["PB3"][64 * p:64 * p + 64, 384:512],
                        sb["qhat"][64 * h:64 * h + 64, tok0:tok0 + 64],
                        sread, start=True, stop=True),
                     ("qhat", ksr), ("B3",))
                E.op("scalar", lambda e, h=h, p=p: e.copy(
                        sb["Oat"][64 * p:64 * p + 64, 128 * h:128 * h + 128],
                        ps["PB3"][64 * p:64 * p + 64, 256:384]),
                     ("B3",), (f"Oat{h}",))
                E.op("vector", lambda e, h=h, p=p, c8=c8: e.scalar_tensor_tensor(
                        sb["Otmp"][64 * p:64 * p + 64, 128 * h:128 * h + 128],
                        ps["PB3"][64 * p:64 * p + 64, 384:512],
                        sb["eG_all"][64 * p:64 * p + 64, c8:c8 + 1],
                        sb["Oat"][64 * p:64 * p + 64, 128 * h:128 * h + 128],
                        ALU.mult, ALU.add),
                     ("B3", f"Oat{h}", "eG_all"), (f"Otmp{h}",))
                E.op("vector", lambda e, h=h, p=p: e.tensor_mul(
                        sb["Osq"][64 * p:64 * p + 64, :],
                        sb["Otmp"][64 * p:64 * p + 64, 128 * h:128 * h + 128],
                        sb["Otmp"][64 * p:64 * p + 64, 128 * h:128 * h + 128]),
                     (f"Otmp{h}",), ("Osq",))
                E.op("vector", lambda e, h=h, p=p: e.reduce_sum(
                        sb["ssO"][64 * p:64 * p + 64, h:h + 1],
                        sb["Osq"][64 * p:64 * p + 64, :], axis=AX.X),
                     ("Osq",), (f"ssO{h}",))
                E.op("scalar", lambda e, h=h, p=p: e.activation(
                        sb["rsOt"][64 * p:64 * p + 64, h:h + 1],
                        sb["ssO"][64 * p:64 * p + 64, h:h + 1],
                        AF.Sqrt, bias=sb["c_pool"][64 * p:64 * p + 64, 4:5],
                        scale=1.0 / 128.0),
                     (f"ssO{h}", "c_pool"), (f"rsOt{h}",))
                E.op("vector", lambda e, h=h, p=p: e.reciprocal(
                        sb["rsO"][64 * p:64 * p + 64, h:h + 1],
                        sb["rsOt"][64 * p:64 * p + 64, h:h + 1]),
                     (f"rsOt{h}",), (f"rsO{h}",))
                E.op("vector", lambda e, h=h, p=p: e.tensor_scalar_mul(
                        sb["ONp"][64 * p:64 * p + 64, 128 * h:128 * h + 128],
                        sb["Otmp"][64 * p:64 * p + 64, 128 * h:128 * h + 128],
                        sb["rsO"][64 * p:64 * p + 64, h:h + 1]),
                     (f"Otmp{h}", f"rsO{h}"), (f"ONp{h}.{p}",))
                E.op("vector", lambda e, h=h, p=p, subq=subq, c8=c8: e.tensor_scalar_mul(
                        sb["kbar"][64 * p:64 * p + 64, 64 * h:64 * h + 64],
                        sb["Ktok"][64 * p:64 * p + 64,
                                   128 * subq + 64 * h:128 * subq + 64 * h + 64],
                        sb["eGLmG"][64 * p:64 * p + 64, c8:c8 + 1]),
                     (f"Ktok.{subq}", "eGLmG"), (f"kbar{h}",))
                E.op("tensor", lambda e, h=h, p=p: e.matmul(
                        ps["PB4"][64 * h:64 * h + 64, 256:384],
                        sb["kbar"][64 * p:64 * p + 64, 64 * h:64 * h + 64],
                        sb["Ubuf"][64 * p:64 * p + 64, 128 * h:128 * h + 128],
                        start=True, stop=True),
                     (f"kbar{h}", f"Ubuf{h}"), ("B4",))
                cflat = 8 * p + 4 * h + subq
                E.op("vector", lambda e, h=h, swrite=swrite, sreadf=sreadf, cflat=cflat:
                     e.scalar_tensor_tensor(
                         swrite, sreadf,
                         sb["eGLk"][64 * h:64 * h + 64, cflat:cflat + 1],
                         ps["PB4"][64 * h:64 * h + 64, 256:384], ALU.mult, ALU.add),
                     (ksrf, "eGLk", "B4"), (ksw,))
                E.op("scalar", lambda e, swrite=swrite, swrite_b=swrite_b: e.copy(
                        swrite_b, swrite),
                     (ksw,), (kswb,))

            if p == 1:
                for h in range(2):
                    slot = h
                    pso = (ps["PB6"] if slot == 0 else ps["PB7"])[:, 0:128]
                    E.op("tensor", lambda e, h=h, pso=pso: e.matmul(
                            pso, sb["ONp"][:, 128 * h:128 * h + 128], sb["c_identb"][:, :],
                            start=True, stop=True),
                         (f"ONp{h}.0", f"ONp{h}.1", "c_identb"), ("B6" if slot == 0 else "B7",))
                    E.op("scalar", lambda e, h=h, pso=pso: e.copy(
                            sb["OTr"][:, 128 * h:128 * h + 128], pso),
                         ("B6" if slot == 0 else "B7",), (f"OTr{h}",))
                    E.op("vector", lambda e, h=h, subq=subq: e.tensor_mul(
                            sb["staging"][:, 512 * h + 128 * subq:512 * h + 128 * (subq + 1)],
                            sb["OTr"][:, 128 * h:128 * h + 128],
                            sb["gates"][:, 512 * h + 128 * subq:512 * h + 128 * (subq + 1)]),
                         (f"OTr{h}", f"gate{h}"), (f"stag{h}.{subq}",))

        dblk = j // 2
        for h in range(2):
            dma("sync", f"ys{pj}", a2a_in[256 * dblk + 128 * h:256 * dblk + 128 * (h + 1),
                               512 * pj:512 * pj + 512],
                sb["staging"][:, 512 * h:512 * (h + 1)],
                tuple(f"stag{h}.{sq}" for sq in range(4)), (f"a2ain.{j}.{h}",))
        E.dcut(f"ys{pj}")

 
    if STAGE < 3:
        return _finish_partial()

    def a2a(e):
        with allow_ici_all_to_all():
            return e.collective_compute(
                "AllToAll", ALU.bypass, replica_groups=[list(range(NCORE))],
                ins=[a2a_in.ap().opt()], outs=[a2a_out.ap().opt()])
    E.coll("gpsimd", "ca2a", a2a,
           tuple(f"a2ain.{j}.{h}" for j in range(NT) for h in range(2)),
           ("a2aout",))

 
    if STAGE < 4:
        return _finish_partial()

    w0a_r = w0a_b.ap().rearrange("(a p) (b q) -> b p a q", p=128, q=128)
    w1a_r = w1a_b.ap().rearrange("(a p) (b q) -> b p a q", p=128, q=128)
    for ft in range(4):
        for fc in range(16):
            dma("sync", "yf", sb["FSCR"][:, 256 * fc:256 * (fc + 1)],
                ag_in[128 * fc:128 * (fc + 1), 256 * ft:256 * (ft + 1)],
                (f"agin.{fc}",), (XNFK[fc],))
        E.dcut("yf")

        def maskmm(e):
            inst = None
            for fc in range(16):
                inst = e.matmul(ps["PB5"][0:1, 0:256],
                                sb["wd_b"][:, fc:fc + 1],
                                sb["FSCR"][:, 256 * fc:256 * (fc + 1)],
                                start=(fc == 0), stop=(fc == 15))
            return inst
        E.op("tensor", maskmm, ("wd_b",) + XNFK, ("B5",))
        E.op("vector", lambda e: e.tensor_scalar(sb["mrow"][:, :], ps["PB5"][0:1, 0:256],
                                                 0.0, None, ALU.is_ge),
             ("B5",), ("mrow",))
        E.op("tensor", lambda e: e.matmul(ps["PB5"][:, 256:512], sb["c_ones1b"][:, :],
                                          sb["mrow"][:, :], start=True, stop=True),
             ("c_ones1b", "mrow"), ("B5",))
        E.op("scalar", lambda e, ft=ft: e.copy(sb["mball"][:, 256 * ft:256 * (ft + 1)],
                                               ps["PB5"][:, 256:512]),
             ("B5",), ("mball",))
        for ex in range(2):
            wa_r = w0a_r if ex == 0 else w1a_r
            wkeys = W0AK if ex == 0 else W1AK
            biask = "b0a_sb" if ex == 0 else "b1a_sb"
            act = AF.Gelu if ex == 0 else AF.Relu
            hd = h0_dram if ex == 0 else h1_dram
            deng = "sync" if ex == 0 else "scalar"
            for ic in range(32):
                wp = ic % 2
                wsl = sb["FSCR"][:, 4096 + 2048 * wp:4096 + 2048 * (wp + 1)]
                kw = f"wstr.{wp}"
                wstream = ("ywa" if deng == "sync" else "swa") + str(wp)
                dma(deng, wstream, wsl.rearrange("p (a q) -> p a q", q=128), wa_r[ic],
                    wkeys, (kw,))
                E.dcut(wstream)
                bank = ps["PB0"] if ex == 0 else ps["PB1"]
                bkk = f"PB{ex}"

                def hmm(e, wsl=wsl, bank=bank):
                    inst = None
                    for fc in range(16):
                        inst = e.matmul(bank[:, 0:256],
                                        wsl[:, 128 * fc:128 * (fc + 1)],
                                        sb["FSCR"][:, 256 * fc:256 * (fc + 1)],
                                        start=(fc == 0), stop=(fc == 15))
                    return inst
                E.op("tensor", hmm, (kw,) + XNFK, (bkk,))
                hsl = ic % 2
                hst = sb["hstage"][:, 512 * ex + 256 * hsl:512 * ex + 256 * (hsl + 1)]
                khs = f"hstage{ex}.{hsl}"
                E.op("scalar", lambda e, bank=bank, hst=hst, ic=ic, biask=biask, act=act:
                     e.activation(hst, bank[:, 0:256], act, bias=sb[biask][:, ic:ic + 1]),
                     (bkk, biask), (khs,))
                hstream = f"gh{ex}{ic % 2}"
                dma("gpsimd", hstream, hd[128 * ic:128 * (ic + 1), 256 * ft:256 * (ft + 1)],
                    hst, (khs,), (f"hd{ex}.{ic}",))
                E.dcut(hstream)
    HD0K = tuple(f"hd0.{ic}" for ic in range(32))
    HD1K = tuple(f"hd1.{ic}" for ic in range(32))

 
    if STAGE < 5:
        return _finish_partial()

    E.handoff(("wo.0", "wo.1"), ("wmix",))
    E.handoff(OFMK, XNFK + ("wstr.0", "wstr.1"))
    E.handoff(tuple(f"ARE.{cc}" for cc in range(16)),
              tuple(f"LNare.{fc}.{i}" for fc in range(16) for i in range(8))
              + AGINK)
    wo_r = wo_b.ap().rearrange("(a p) (b q) -> b p a q", p=128, q=128)
    for t2 in range(2):
        for vc in range(16):
            dma("sync", "yo", sb["FSCR"][:, 512 * vc:512 * (vc + 1)],
                a2a_out[128 * vc:128 * (vc + 1), 512 * t2:512 * (t2 + 1)],
                ("a2aout",), (OFMK[vc],))
        E.dcut("yo")
        for ccq in range(16):
            wp = ccq % 2
            wsl = sb["WBUF"][:, 2048 * wp:2048 * (wp + 1)]
            kw = f"wo.{wp}"
            dma("scalar", f"swo{wp}", wsl.rearrange("p (a q) -> p a q", q=128),
                wo_r[ccq], WOBK, (kw,))
            E.dcut(f"swo{wp}")

            def owmm(e, wsl=wsl):
                inst = None
                for vc in range(16):
                    inst = e.matmul(ps["PB0"][:, 0:512],
                                    wsl[:, 128 * vc:128 * (vc + 1)],
                                    sb["FSCR"][:, 512 * vc:512 * (vc + 1)],
                                    start=(vc == 0), stop=(vc == 15))
                return inst
            E.op("tensor", owmm, (kw,) + OFMK, ("PB0",))
            E.op("scalar", lambda e, ccq=ccq, t2=t2: e.copy(
                    sb["ARE"][:, 1024 * ccq + 512 * t2:1024 * ccq + 512 * (t2 + 1)],
                    ps["PB0"][:, 0:512]),
                 ("PB0",), (f"ARE.{ccq}",))

 
    if STAGE < 6:
        return _finish_partial()

    E.handoff(tuple(f"w0bB.{cj}" for cj in range(4)), ("xn0.0",) + XNK[0] + XNK[1])
    E.handoff(("w1bB.0", "w1bB.1", "w1bB.2"), ("wo.0", "wo.1"))
    E.handoff(("w1bB.3",), OFMK)
    w0b_r = w0b_b.ap().rearrange("(a p) (b q) -> b p a q", p=128, q=128)
    w1b_r = w1b_b.ap().rearrange("(a p) (b q) -> b p a q", p=128, q=128)
    h0_r = h0_dram.ap().rearrange("(a p) (b q) -> b p a q", p=128, q=128)
    h1_r = h1_dram.ap().rearrange("(a p) (b q) -> b p a q", p=128, q=128)
    OUTK = []
    for cblk in range(4):
        for cj in range(4):
            ccq = 4 * cblk + cj
            dma("sync", "yb", sb["BB0"][:, 4096 * cj:4096 * (cj + 1)]
                .rearrange("p (a q) -> p a q", q=128), w0b_r[ccq],
                W0BK, (f"w0bB.{cj}",))
            if cj < 3:
                w1dst = sb["WBUF"][:, 4096 * cj:4096 * (cj + 1)]
            else:
                w1dst = sb["FSCR"][:, 0:4096]
            dma("scalar", "sb1", w1dst.rearrange("p (a q) -> p a q", q=128),
                w1b_r[ccq], W1BK, (f"w1bB.{cj}",))
        E.dcut("yb")
        E.dcut("sb1")
        for ft in range(8):
            dma("sync", "yh", sb["hbuf"][:, 0:4096].rearrange("p (a q) -> p a q", q=128),
                h0_r[ft], HD0K, ("hb0",))
            dma("sync", "yh", sb["hbuf"][:, 4096:8192].rearrange("p (a q) -> p a q", q=128),
                h1_r[ft], HD1K, ("hb1",))
            E.dcut("yh")
            for cj in range(4):
                ccq = 4 * cblk + cj
                if cj < 3:
                    w1sl = sb["WBUF"][:, 4096 * cj:4096 * (cj + 1)]
                else:
                    w1sl = sb["FSCR"][:, 0:4096]

                def o0mm(e, cj=cj):
                    inst = None
                    for ic in range(32):
                        inst = e.matmul(ps["PB0"][:, 0:128],
                                        sb["BB0"][:, 4096 * cj + 128 * ic:
                                                  4096 * cj + 128 * (ic + 1)],
                                        sb["hbuf"][:, 128 * ic:128 * (ic + 1)],
                                        start=(ic == 0), stop=(ic == 31))
                    return inst
                E.op("tensor", o0mm, (f"w0bB.{cj}", "hb0"), ("PB0",))

                def o1mm(e, w1sl=w1sl):
                    inst = None
                    for ic in range(32):
                        inst = e.matmul(ps["PB1"][:, 0:128],
                                        w1sl[:, 128 * ic:128 * (ic + 1)],
                                        sb["hbuf"][:, 4096 + 128 * ic:
                                                   4096 + 128 * (ic + 1)],
                                        start=(ic == 0), stop=(ic == 31))
                    return inst
                E.op("tensor", o1mm, (f"w1bB.{cj}", "hb1"), ("PB1",))
                xp = (cblk * 32 + ft * 4 + cj) % 2
                xsl = sb["xfr"][:, 128 * xp:128 * (xp + 1)]
                kxf = f"xfr.{xp}"
                dma("gpsimd", f"gx{xp}", xsl, d_xfm[128 * ccq:128 * (ccq + 1),
                                         128 * ft:128 * (ft + 1)], (), (kxf,))
                E.dcut(f"gx{xp}")
                E.op("vector", lambda e, ccq=ccq: e.tensor_scalar_add(
                        sb["fb0"][:, :], ps["PB0"][:, 0:128],
                        sb["b0b_sb"][:, ccq:ccq + 1]),
                     ("PB0", "b0b_sb"), ("fb0",))
                E.op("vector", lambda e, ccq=ccq: e.tensor_scalar_add(
                        sb["fb1"][:, :], ps["PB1"][:, 0:128],
                        sb["b1b_sb"][:, ccq:ccq + 1]),
                     ("PB1", "b1b_sb"), ("fb1",))
                E.op("vector", lambda e: e.tensor_sub(sb["fb2"][:, :], sb["fb0"][:, :],
                                                      sb["fb1"][:, :]),
                     ("fb0", "fb1"), ("fb2",))
                E.op("vector", lambda e, ft=ft: e.tensor_mul(
                        sb["fb0"][:, :], sb["fb2"][:, :],
                        sb["mball"][:, 128 * ft:128 * (ft + 1)]),
                     ("fb2", "mball"), ("fb0",))
                E.op("vector", lambda e: e.tensor_add(sb["fb2"][:, :], sb["fb1"][:, :],
                                                      sb["fb0"][:, :]),
                     ("fb1", "fb0"), ("fb2",))
                E.op("vector", lambda e, ccq=ccq, ft=ft: e.tensor_add(
                        sb["fb0"][:, :], sb["fb2"][:, :],
                        sb["ARE"][:, 1024 * ccq + 128 * ft:1024 * ccq + 128 * (ft + 1)]),
                     ("fb2", f"ARE.{ccq}"), ("fb0",))
                op_ = (cblk * 32 + ft * 4 + cj) % 2
                osl = sb["fbo"][:, 128 * op_:128 * (op_ + 1)]
                ko = f"fbo.{op_}"
                E.op("vector", lambda e, xsl=xsl, osl=osl: e.tensor_add(
                        osl, sb["fb0"][:, :], xsl),
                     ("fb0", kxf), (ko,))
                okey = f"outw.{ccq}.{ft}"
                OUTK.append(okey)
                dma("sync", f"yw{op_}", d_out[128 * ccq:128 * (ccq + 1),
                                              128 * ft:128 * (ft + 1)],
                    osl, (ko,), (okey,))
                E.dcut(f"yw{op_}")

    E.op("sync", lambda e: e.nop(), reads=tuple(OUTK), writes=())

    E.play(nc)
    return nc


_PROG_CACHE = {}


def kernel(**inputs):
    x = np.asarray(inputs["L0_attn_x"])
    qk_w = np.asarray(inputs["L0_attn_qk_w"])
    vg_w = np.asarray(inputs["L0_attn_vg_w"])
    o_w = np.asarray(inputs["L0_attn_o_w"])
    a_w = np.asarray(inputs["L0_attn_a_w"])
    b_w = np.asarray(inputs["L0_attn_b_w"])
    b_bias = np.asarray(inputs["L0_attn_b_bias"])
    beta_bias = np.asarray(inputs["L0_attn_beta_bias"])
    A_log = np.asarray(inputs["L0_attn_A_log"])
    dt_bias = np.asarray(inputs["L0_attn_dt_bias"])
    q_conv_w = np.asarray(inputs["L0_attn_q_conv_w"])
    k_conv_w = np.asarray(inputs["L0_attn_k_conv_w"])
    v_conv_w = np.asarray(inputs["L0_attn_v_conv_w"])
    Wg = np.asarray(inputs["L0_ffn_Wg"])
    W0a = np.asarray(inputs["L0_ffn_W0a"])
    b0a = np.asarray(inputs["L0_ffn_b0a"])
    W0b = np.asarray(inputs["L0_ffn_W0b"])
    b0b = np.asarray(inputs["L0_ffn_b0b"])
    W1a = np.asarray(inputs["L0_ffn_W1a"])
    b1a = np.asarray(inputs["L0_ffn_b1a"])
    W1b = np.asarray(inputs["L0_ffn_W1b"])
    b1b = np.asarray(inputs["L0_ffn_b1b"])

    shared = {
        "w_o": np.ascontiguousarray(o_w.T),
        "w0a": np.ascontiguousarray(W0a.T),
        "w0b": np.ascontiguousarray(W0b.T),
        "w1a": np.ascontiguousarray(W1a.T),
        "w1b": np.ascontiguousarray(W1b.T),
        "b0a_t": np.ascontiguousarray(b0a.reshape(32, 128).T),
        "b1a_t": np.ascontiguousarray(b1a.reshape(32, 128).T),
        "b0b_t": np.ascontiguousarray(b0b.reshape(16, 128).T),
        "b1b_t": np.ascontiguousarray(b1b.reshape(16, 128).T),
        "wg": np.ascontiguousarray(Wg.T),
    }
    shared.update({"d_" + k: v for k, v in host_constants().items()})

    in_maps = []
    for c in range(NCORE):
        bsel = c // 4
        t0 = (c % 4) * TOWN
        x_own = np.ascontiguousarray(x[bsel, t0:t0 + TOWN])
        w_mix = np.ascontiguousarray(np.concatenate([
            qk_w[128 * c:128 * (c + 1)],
            qk_w[1024 + 128 * c:1024 + 128 * (c + 1)],
            vg_w[256 * c:256 * (c + 1)],
            vg_w[2048 + 256 * c:2048 + 256 * (c + 1)],
            a_w[2 * c:2 * c + 2],
            b_w[2 * c:2 * c + 2],
        ], axis=0).T)
        w_conv = np.ascontiguousarray(np.concatenate([
            q_conv_w[128 * c:128 * (c + 1), 0, :],
            k_conv_w[128 * c:128 * (c + 1), 0, :],
            v_conv_w[256 * c:256 * (c + 1), 0, :],
        ], axis=0))
        vhead = np.ascontiguousarray(np.concatenate([
            A_log[2 * c:2 * c + 2], dt_bias[2 * c:2 * c + 2],
            b_bias[2 * c:2 * c + 2], beta_bias[2 * c:2 * c + 2],
        ]).reshape(1, 8))
        m = {
            "x_tok": x_own,
            "x_fm": np.ascontiguousarray(x_own.T),
            "w_mix": w_mix,
            "w_conv": w_conv,
            "vhead": vhead,
        }
        m.update(shared)
        in_maps.append(m)

    if "nc" not in _PROG_CACHE:
        _PROG_CACHE["nc"] = build_program()
    nc = _PROG_CACHE["nc"]
    res = run_bass_kernel_spmd(nc, in_maps, core_ids=list(range(NCORE)))
    out = np.empty((2, 4096, 2048), np.float32)
    for c in range(NCORE):
        bsel = c // 4
        t0 = (c % 4) * TOWN
        out[bsel, t0:t0 + TOWN, :] = np.asarray(res.results[c]["out_fm"]).T
    return out


# version 09
# speedup vs baseline: 1.3150x; 1.3150x over previous
import os
os.environ.setdefault("TRN_ALLOW_ICI_ALL_TO_ALL", "1")

import numpy as np
import ml_dtypes
import concourse.bass as bass
import concourse.mybir as mybir
from concourse.bass_utils import run_bass_kernel_spmd
from concourse.aot_env import allow_ici_all_to_all

F32 = mybir.dt.float32
BF16 = mybir.dt.bfloat16
AF = mybir.ActivationFunctionType
ALU = mybir.AluOpType
AX = mybir.AxisListType
BF = ml_dtypes.bfloat16

NCORE = 8
C = 2048
TOWN = 1024
NT = 16
L = 64
EPS_F32 = float(np.finfo(np.float32).eps)

ENGINES = ("tensor", "vector", "scalar", "sync", "gpsimd")


class Emitter:

    def __init__(self):
        self.streams = {e: [] for e in ENGINES}
        self.cnt = {e: 0 for e in ENGINES}
        self.lastw = {}
        self.readers = {}
        self.maxwait = {e: {} for e in ENGINES}
        self.dstreams = {}
        self.know = {e: {} for e in ENGINES}
        self.kckpt = {e: [(0, {})] for e in ENGINES}
        self._know_now = {}
        self._kdirty = False

    def _stream(self, name):
        return self.dstreams.setdefault(
            name, {"cnt": 0, "ends": [], "know": [(0, {})]})

    def dcut(self, name):
        st = self._stream(name)
        if not st["ends"] or st["ends"][-1] != st["cnt"]:
            st["ends"].append(st["cnt"])
            st["know"].append((len(st["ends"]) - 1,
                               dict(st.get("_open_know", {}))))

    def _ckpt(self, eng):
        lst = self.kckpt[eng]
        if lst[-1][0] != self.cnt[eng] or len(lst) == 1:
            lst.append((self.cnt[eng], dict(self.know[eng])))
        else:
            lst[-1] = (self.cnt[eng], dict(self.know[eng]))

    def _know_at(self, Y, c):
        lst = self.kckpt[Y]
        lo, hi = 0, len(lst) - 1
        best = lst[0][1]
        while lo <= hi:
            mid = (lo + hi) // 2
            if lst[mid][0] < c:
                best = lst[mid][1]
                lo = mid + 1
            else:
                hi = mid - 1
        return best

    def _merge_know(self, eng, other):
        k = self.know[eng]
        for key, val in other.items():
            if val > k.get(key, -1):
                k[key] = val

    def _dep(self, eng, deps):
        agg = {}
        for tok in deps:
            key = (tok[0], tok[1])
            if key not in agg or tok[2] > agg[key][2]:
                agg[key] = tok
        know = self.know[eng]
        for key, tok in sorted(agg.items(), key=lambda kv: str(kv[0])):
            if tok[2] <= know.get(key, -1):
                continue
            if tok[0] == "E" and tok[1] == eng:
                self.streams[eng].append(("dr",))
                know[key] = self.cnt[eng]
                self._kdirty = True
                continue
            know[key] = tok[2]
            self._kdirty = True
            if tok[0] == "E":
                self.streams[eng].append(("we", tok[1], tok[2]))
                self._merge_know(eng, self._know_at(tok[1], tok[2]))
            else:
                self.streams[eng].append(("wd", tok[1], tok[2]))
                st = self.dstreams[tok[1]]
                gk = st["know"]
                gidx = min(tok[2], len(gk) - 1 - 1) if False else tok[2]
                best = {}
                for gi, kd in gk:
                    if gi <= gidx:
                        best = kd
                    else:
                        break
                self._merge_know(eng, best)

    def _gather(self, reads, writes):
        deps = []
        for r in reads:
            lw = self.lastw.get(r)
            if lw:
                deps.append(lw)
        for w in writes:
            lw = self.lastw.get(w)
            if lw:
                deps.append(lw)
            for tok in self.readers.get(w, {}).values():
                deps.append(tok)
        return deps

    def _register(self, tok, reads, writes):
        for r in reads:
            d = self.readers.setdefault(r, {})
            key = (tok[0], tok[1])
            if key not in d or tok[2] > d[key][2]:
                d[key] = tok
        for w in writes:
            self.lastw[w] = tok
            self.readers[w] = {}

    def op(self, eng, fn, reads=(), writes=()):
        self._kdirty = False
        self._dep(eng, self._gather(reads, writes))
        if self._kdirty:
            self._ckpt(eng)
        self.cnt[eng] += 1
        self.streams[eng].append(("i", fn, eng, 1))
        tok = ("E", eng, self.cnt[eng])
        self._register(tok, reads, writes)
        return tok

    def dma(self, eng, stream, out_ap, in_ap, reads=(), writes=()):
        st0 = self._stream(stream)
        open_tok = ("D", stream, len(st0["ends"]))
        deps = [d for d in self._gather(reads, writes) if d != open_tok]
        self._kdirty = False
        self._dep(eng, deps)
        if self._kdirty:
            self._ckpt(eng)
        st0["cnt"] += 16
        self.streams[eng].append(
            ("i", lambda e, o=out_ap, i=in_ap: e.dma_start(out=o, in_=i),
             stream, 16))
        st0["_open_know"] = dict(self.know[eng])
        tok = ("D", stream, len(st0["ends"]))
        self._register(tok, reads, writes)
        return tok

    def coll(self, eng, stream, fn, reads=(), writes=()):
        self._kdirty = False
        self._dep(eng, self._gather(reads, writes))
        if self._kdirty:
            self._ckpt(eng)
        st = self._stream(stream)
        st["cnt"] += 1
        self.streams[eng].append(("i", fn, stream, 1))
        st["_open_know"] = dict(self.know[eng])
        self.dcut(stream)
        tok = ("D", stream, len(st["ends"]) - 1)
        self._register(tok, reads, writes)
        return tok

    def handoff(self, news, olds):
        rd = {}
        for old in olds:
            lw = self.lastw.pop(old, None)
            if lw:
                key = (lw[0], lw[1])
                if key not in rd or lw[2] > rd[key][2]:
                    rd[key] = lw
            for key, tok in self.readers.pop(old, {}).items():
                if key not in rd or tok[2] > rd[key][2]:
                    rd[key] = tok
        for new in news:
            self.lastw.pop(new, None)
            self.readers[new] = dict(rd)

    def play(self, nc):
        for s in self.dstreams:
            self.dcut(s)
        import contextlib
        with contextlib.ExitStack() as stack:
            sems = {}
            for e in ENGINES:
                sems[("E", e)] = stack.enter_context(nc.semaphore("se_" + e))
            for s in self.dstreams:
                sems[("D", s)] = stack.enter_context(nc.semaphore("sd_" + s))
            block = stack.enter_context(nc.Block())

            def runner(name):
                def run(e):
                    for item in self.streams[name]:
                        kind = item[0]
                        if kind == "dr":
                            e.drain()
                        elif kind == "we":
                            e.wait_ge(sems[("E", item[1])], item[2])
                        elif kind == "wd":
                            st = self.dstreams[item[1]]
                            gidx = min(item[2], len(st["ends"]) - 1)
                            e.wait_ge(sems[("D", item[1])], st["ends"][gidx])
                        else:
                            _, fn, target, inc = item
                            if target in ENGINES:
                                fn(e).then_inc(sems[("E", target)], inc)
                            else:
                                fn(e).then_inc(sems[("D", target)], inc)
                return run

            block.tensor(runner("tensor"))
            block.vector(runner("vector"))
            block.scalar(runner("scalar"))
            block.sync(runner("sync"))
            block.gpsimd(runner("gpsimd"))


def host_constants():
    ut64 = np.triu(np.ones((L, L), np.float32))
    ut64s = np.triu(np.ones((L, L), np.float32), 1)
    lt64s = np.tril(np.ones((L, L), np.float32), -1)
    c = {}
    c["c_identb"] = np.eye(128, dtype=np.float32).astype(BF)
    c["c_identf"] = np.eye(128, dtype=np.float32)
    c["c_ut64x2"] = np.tile(ut64, (2, 1))
    c["c_ones"] = np.ones((128, 64), np.float32)
    c["c_negones"] = -np.ones((128, 64), np.float32)
    c["c_mstrict"] = np.tile(ut64s, (2, 2))
    c["c_mincl"] = np.tile(ut64, (2, 2))
    c["c_mstrictlow"] = np.tile(lt64s, (2, 2))
    blk = np.zeros((128, 128), np.float32)
    blk[0:64, 0:64] = ut64
    blk[64:128, 64:128] = ut64
    c["c_utblk"] = blk
    e63 = np.zeros((128, 2), np.float32)
    e63[63, 0] = 1.0
    e63[127, 1] = 1.0
    c["c_e63"] = e63
    ex2 = np.zeros((2, 128), np.float32)
    ex2[0, 0:64] = 1.0
    ex2[1, 64:128] = 1.0
    c["c_exp2f"] = ex2
    c["c_exp2b"] = ex2.astype(BF)
    c["c_ones1f"] = np.ones((1, 128), np.float32)
    c["c_ones1b"] = np.ones((1, 128), np.float32).astype(BF)
    bd = np.zeros((128, 2), np.float32)
    bd[0:64, 0] = 1.0
    bd[64:128, 1] = 1.0
    c["c_blkdiag2b"] = bd.astype(BF)
    c["c_onesb"] = np.ones((128, 64), np.float32).astype(BF)
    pool = np.tile(np.array([[1e-5, 64e-6, 1e-6, 1.0, float(np.finfo(np.float32).eps)]],
                            np.float32), (128, 1))
    c["c_pool"] = pool
    return c


CONST_SPECS = {
    "c_identb": ([128, 128], BF16), "c_identf": ([128, 128], F32),
    "c_ut64x2": ([128, 64], F32), "c_ones": ([128, 64], F32),
    "c_negones": ([128, 64], F32), "c_mstrict": ([128, 128], F32),
    "c_mincl": ([128, 128], F32), "c_mstrictlow": ([128, 128], F32),
    "c_utblk": ([128, 128], F32), "c_e63": ([128, 2], F32),
    "c_exp2f": ([2, 128], F32), "c_exp2b": ([2, 128], BF16),
    "c_ones1f": ([1, 128], F32), "c_ones1b": ([1, 128], BF16),
    "c_blkdiag2b": ([128, 2], BF16),
    "c_onesb": ([128, 64], BF16),
    "c_pool": ([128, 5], F32),
}

XNK = [tuple(f"xn{p}.{fc}" for fc in range(16)) for p in range(2)]
XNFK = tuple(f"xnf.{fc}" for fc in range(16))
OFMK = tuple(f"ofm.{vc}" for vc in range(16))


def build_program():
    STAGE = float(os.environ.get("MK_STAGE", "9"))
    nc = bass.Bass("TRN2", target_bir_lowering=False, debug=False)
    st = nc._state

    d_xtok = nc.dram_tensor("x_tok", [TOWN, C], F32, kind="ExternalInput")
    d_xfm = nc.dram_tensor("x_fm", [C, TOWN], F32, kind="ExternalInput")
    d_wmix = nc.dram_tensor("w_mix", [C, 772], F32, kind="ExternalInput")
    d_wconv = nc.dram_tensor("w_conv", [512, 4], F32, kind="ExternalInput")
    d_vhead = nc.dram_tensor("vhead", [1, 8], F32, kind="ExternalInput")
    d_wo = nc.dram_tensor("w_o", [C, C], F32, kind="ExternalInput")
    d_w0a = nc.dram_tensor("w0a", [C, 4096], F32, kind="ExternalInput")
    d_w0b = nc.dram_tensor("w0b", [4096, C], F32, kind="ExternalInput")
    d_w1a = nc.dram_tensor("w1a", [C, 4096], F32, kind="ExternalInput")
    d_w1b = nc.dram_tensor("w1b", [4096, C], F32, kind="ExternalInput")
    d_b0a = nc.dram_tensor("b0a_t", [128, 32], F32, kind="ExternalInput")
    d_b1a = nc.dram_tensor("b1a_t", [128, 32], F32, kind="ExternalInput")
    d_b0b = nc.dram_tensor("b0b_t", [128, 16], F32, kind="ExternalInput")
    d_b1b = nc.dram_tensor("b1b_t", [128, 16], F32, kind="ExternalInput")
    d_wg = nc.dram_tensor("wg", [C, 2], F32, kind="ExternalInput")
    d_out = nc.dram_tensor("out_fm", [C, TOWN], F32, kind="ExternalOutput")
    d_const = {k: nc.dram_tensor("d_" + k, sh, dt, kind="ExternalInput")
               for k, (sh, dt) in CONST_SPECS.items()}

    ag_in = nc.dram_tensor("ag_in", [C, TOWN], BF16)
    ag_out = nc.dram_tensor("ag_out", [NCORE * C, TOWN], BF16)
    a2a_in = nc.dram_tensor("a2a_in", [NCORE * 256, TOWN], BF16)
    a2a_out = nc.dram_tensor("a2a_out", [NCORE * 256, TOWN], BF16)
    h0_dram = nc.dram_tensor("h0_dram", [4096, TOWN], BF16)
    h1_dram = nc.dram_tensor("h1_dram", [4096, TOWN], BF16)
    wo_b = nc.dram_tensor("wo_b", [C, C], BF16)
    w0a_b = nc.dram_tensor("w0a_b", [C, 4096], BF16)
    w1a_b = nc.dram_tensor("w1a_b", [C, 4096], BF16)
    w0b_b = nc.dram_tensor("w0b_b", [4096, C], BF16)
    w1b_b = nc.dram_tensor("w1b_b", [4096, C], BF16)

    sb = {}

    def S(name, shape, dtype):
        sb[name] = st.alloc_sbuf_tensor(name, shape, dtype)
        return sb[name]

    for k, (sh, dt) in CONST_SPECS.items():
        S(k, sh, dt)
    S("convw", [128, 16], F32)
    S("vh_row", [1, 8], F32)
    S("vh_all", [128, 8], F32)
    S("negexpA", [128, 2], F32)
    S("biasb", [128, 2], F32)
    S("WBUF", [128, 12352], BF16)
    S("xtile", [128, 2048], F32)
    S("bn24", [128, 24], F32)
    S("mv", [128, 2], F32)
    S("rs", [128, 1], F32)
    S("negmrs", [128, 1], F32)
    S("mtmp", [128, 1], F32)
    S("mtmp2", [128, 1], F32)
    S("xnstage", [128, 2048], BF16)
    S("BB0", [128, 16384], BF16)
    S("ARE", [128, 16384], BF16)
    S("conva", [128, 8 * 515], BF16)
    S("convt", [128, 2 * 512], F32)
    S("qsil", [128, 512], BF16)
    S("ksil", [128, 512], BF16)
    S("vsil", [128, 1024], BF16)
    S("sqbuf", [128, 512], BF16)
    S("rsq2", [2, 512], BF16)
    S("rsqt", [2, 512], F32)
    S("qhat", [128, 512], BF16)
    S("khat", [128, 512], BF16)
    S("Ktok", [128, 512], BF16)
    S("Vtok", [128, 1024], BF16)
    S("gates", [128, 1024], BF16)
    S("thin_sb", [4, 512], F32)
    S("thin_tok", [128, 16], F32)
    S("g_all", [128, 8], F32)
    S("sp_tmp", [128, 8], F32)
    S("sp_tmp2", [128, 8], F32)
    S("sig2_all", [128, 8], F32)
    S("brow2", [128, 64], BF16)
    S("G_all", [128, 8], F32)
    S("eG_all", [128, 8], F32)
    S("negeG_all", [128, 8], F32)
    S("d2_tmp", [128, 8], F32)
    S("eGLmG", [128, 8], F32)
    S("GL2", [2, 8], F32)
    S("GLflat", [1, 16], F32)
    S("eGLflat", [1, 16], F32)
    S("eGLk", [128, 16], F32)
    S("gUT", [128, 2 * 64], F32)
    S("expD", [128, 128], F32)
    S("expDT", [128, 128], F32)
    S("tmpA", [128, 128], F32)
    S("tmpB", [128, 128], F32)
    S("nst", [128, 2 * 384], BF16)
    S("vcur", [128, 2 * 2 * 64], BF16)
    S("attnT", [128, 2 * 64], BF16)
    S("Vsl", [128, 2 * 2 * 128], BF16)
    S("Ubuf", [128, 2 * 128], BF16)
    S("kbar", [128, 2 * 64], BF16)
    S("Otmp", [128, 2 * 128], F32)
    S("Oat", [128, 2 * 128], F32)
    S("Osq", [128, 256], F32)
    S("ssO", [128, 2], F32)
    S("rsO", [128, 2], F32)
    S("rsOt", [128, 2], F32)
    S("ONp", [128, 2 * 128], BF16)
    S("OTr", [128, 2 * 128], BF16)
    S("Ssb", [128, 512], F32)
    S("Ssb_b", [128, 512], BF16)
    S("staging", [128, 1024], BF16)
    S("wg_sb", [128, 32], F32)
    S("wd_f", [128, 16], F32)
    S("wd_b", [128, 16], BF16)
    S("FSCR", [128, 8192], BF16)
    S("mrow", [1, 256], BF16)
    S("mball", [128, 1024], BF16)
    S("b0a_sb", [128, 32], F32)
    S("b1a_sb", [128, 32], F32)
    S("b0b_sb", [128, 16], F32)
    S("b1b_sb", [128, 16], F32)
    S("hstage", [128, 4 * 256], BF16)
    S("hbuf", [128, 16384], BF16)
    S("fb0", [128, 128], F32)
    S("fb1", [128, 128], F32)
    S("fb2", [128, 128], F32)
    S("fbo", [128, 2 * 128], F32)
    S("xfr", [128, 2 * 128], F32)

    ps = {}

    def P(name, shape, dtype=F32):
        ps[name] = st.alloc_psum_tensor(name, shape, dtype)
        return ps[name]

    P("PB0", [128, 512])
    P("PB1", [128, 512])
    P("PB2", [128, 512])
    P("PB3", [128, 512])
    P("PB4", [128, 512])
    P("PB5", [128, 512])
    P("PB6", [128, 512])
    P("PB7", [128, 512])

    E = Emitter()

    def dma(eng, stream, out_ap, in_ap, reads, writes):
        E.dma(eng, stream, out_ap, in_ap, reads=reads, writes=writes)

    col8 = lambda h, subq: 4 * h + subq


    def _finish_partial():
        E.dma("sync", "ydbg", d_out[0:128, 0:128], sb["Ssb"][:, 0:128],
              reads=(), writes=("dbgout",))
        E.dcut("ydbg")
        E.op("sync", lambda e: e.nop(), reads=("dbgout",), writes=())
        E.play(nc)
        return nc

    for k in CONST_SPECS:
        dma("sync", "yc", sb[k][:, :], d_const[k].ap(), (), (k,))
    for ccv in range(4):
        dma("sync", "yc", sb["convw"][:, 4 * ccv:4 * ccv + 4],
            d_wconv[128 * ccv:128 * ccv + 128, :], (), ("convw",))
    dma("sync", "yc", sb["vh_row"][:, :], d_vhead.ap(), (), ("vh_row",))
    dma("sync", "yc", sb["wg_sb"][:, :],
        d_wg.ap().rearrange("(a p) c -> p a c", p=128), (), ("wg_sb",))
    dma("sync", "yc", sb["b0a_sb"][:, :], d_b0a.ap(), (), ("b0a_sb",))
    dma("sync", "yc", sb["b1a_sb"][:, :], d_b1a.ap(), (), ("b1a_sb",))
    dma("sync", "yc", sb["b0b_sb"][:, :], d_b0b.ap(), (), ("b0b_sb",))
    dma("sync", "yc", sb["b1b_sb"][:, :], d_b1b.ap(), (), ("b1b_sb",))
    E.dcut("yc")
    for fc in range(16):
        dma("gpsimd", "gw", sb["WBUF"][:, 772 * fc:772 * (fc + 1)],
            d_wmix[128 * fc:128 * (fc + 1), :], (), ("wmix",))
    for src, dst, key in ((d_wo, wo_b, "wob"), (d_w0a, w0a_b, "w0ab"),
                          (d_w1a, w1a_b, "w1ab"), (d_w0b, w0b_b, "w0bb"),
                          (d_w1b, w1b_b, "w1bb")):
        rows = src.ap().shape[0]
        q = rows // 4
        for i in range(4):
            dma("gpsimd", "gc", dst[q * i:q * (i + 1), :], src[q * i:q * (i + 1), :],
                (), (f"{key}.{i}",))
    E.dcut("gw")
    E.dcut("gc")
    WOBK = tuple(f"wob.{i}" for i in range(4))
    W0AK = tuple(f"w0ab.{i}" for i in range(4))
    W1AK = tuple(f"w1ab.{i}" for i in range(4))
    W0BK = tuple(f"w0bb.{i}" for i in range(4))
    W1BK = tuple(f"w1bb.{i}" for i in range(4))

    if STAGE < 0.3:
        return _finish_partial()
    E.op("tensor", lambda e: e.matmul(ps["PB5"][:, 0:8], sb["c_ones1f"][:, :],
                                      sb["vh_row"][:, :], start=True, stop=True),
         ("c_ones1f", "vh_row"), ("B5",))
    E.op("scalar", lambda e: e.copy(sb["vh_all"][:, :], ps["PB5"][:, 0:8]),
         ("B5",), ("vh_all",))
    if STAGE < 0.35:
        return _finish_partial()
    E.op("scalar", lambda e: e.activation(sb["negexpA"][:, :], sb["vh_all"][:, 0:2], AF.Exp),
         ("vh_all",), ("negexpA",))
    E.op("scalar", lambda e: e.mul(sb["negexpA"][:, :], sb["negexpA"][:, :], -1.0),
         ("negexpA",), ("negexpA",))
    if STAGE < 0.4:
        return _finish_partial()
    E.op("vector", lambda e: e.tensor_add(sb["biasb"][:, :], sb["vh_all"][:, 4:6],
                                          sb["vh_all"][:, 6:8]),
         ("vh_all",), ("biasb",))
    if STAGE < 0.45:
        return _finish_partial()
    E.op("vector", lambda e: e.tensor_tensor(
            sb["wd_f"][:, :],
            bass.AP(sb["wg_sb"], 0, [[32, 128], [2, 16]]),
            bass.AP(sb["wg_sb"], 1, [[32, 128], [2, 16]]), ALU.subtract),
         ("wg_sb",), ("wd_f",))
    E.op("scalar", lambda e: e.copy(sb["wd_b"][:, :], sb["wd_f"][:, :]),
         ("wd_f",), ("wd_b",))
    if STAGE < 0.5:
        return _finish_partial()
    E.op("vector", lambda e: e.memset(sb["Ssb"][:, 0:256], 0.0), (), ("S.0.0", "S.0.1"))
    E.op("vector", lambda e: e.memset(sb["Ssb"][:, 256:512], 0.0), (), ("S.1.0", "S.1.1"))
    E.op("vector", lambda e: e.memset(sb["Ssb_b"][:, 0:256], 0.0), (), ("Sb.0.0", "Sb.0.1"))
    E.op("vector", lambda e: e.memset(sb["Ssb_b"][:, 256:512], 0.0), (), ("Sb.1.0", "Sb.1.1"))

    if STAGE < 0.6:
        return _finish_partial()
    _ln_iters = 8 if STAGE >= 0.74 else (4 if STAGE >= 0.72 else (2 if STAGE >= 0.71 else 1))
    for i in range(_ln_iters):
        dma("sync", "yx", sb["xtile"][:, :], d_xtok[128 * i:128 * (i + 1), :], (), ("xtile",))
        E.dcut("yx")
        for q in range(4):
            E.op("vector", lambda e, q=q: e.bn_stats(
                    sb["bn24"][:, 6 * q:6 * (q + 1)],
                    sb["xtile"][:, 512 * q:512 * (q + 1)]),
                 ("xtile",), (f"bn24.{q}",))
        E.op("vector", lambda e: e.bn_aggr(sb["mv"][:, :], sb["bn24"][:, :]),
             ("bn24.0", "bn24.1", "bn24.2", "bn24.3"), ("mv",))
        E.op("scalar", lambda e: e.activation(sb["mtmp2"][:, :], sb["mv"][:, 1:2],
                                              AF.Sqrt, bias=sb["c_pool"][:, 0:1]),
             ("mv", "c_pool"), ("mtmp2",))
        E.op("vector", lambda e: e.reciprocal(sb["rs"][:, :], sb["mtmp2"][:, :]),
             ("mtmp2",), ("rs",))
        E.op("vector", lambda e: e.tensor_mul(sb["mtmp"][:, :], sb["mv"][:, 0:1],
                                              sb["rs"][:, :]),
             ("mv", "rs"), ("mtmp",))
        E.op("vector", lambda e: e.tensor_scalar_mul(sb["negmrs"][:, :],
                                                     sb["mtmp"][:, :], -1.0),
             ("mtmp",), ("negmrs",))
        E.op("vector", lambda e: e.tensor_scalar(
                sb["xnstage"][:, :], sb["xtile"][:, :],
                sb["rs"][:, 0:1], sb["negmrs"][:, 0:1], ALU.mult, ALU.add),
             ("xtile", "rs", "negmrs"), ("xnstage",))
        for fc in range(0 if (STAGE < 0.7 or abs(STAGE - 0.74) < 1e-9) else 16):
            slot = fc % 2
            pso = (ps["PB6"] if slot == 0 else ps["PB7"])[:, 0:128]
            E.op("tensor", lambda e, fc=fc, pso=pso: e.matmul(
                    pso, sb["xnstage"][:, 128 * fc:128 * (fc + 1)], sb["c_identb"][:, :],
                    start=True, stop=True),
                 ("xnstage", "c_identb"), ("B6" if slot == 0 else "B7",))
            E.op("scalar", lambda e, fc=fc, i=i, pso=pso: e.copy(
                    sb["ARE"][:, 1024 * fc + 128 * i:1024 * fc + 128 * (i + 1)], pso),
                 ("B6" if slot == 0 else "B7",), (f"LNare.{fc}.{i}",))
    if STAGE < 0.8:
        return _finish_partial()
    for fc in range(16):
        dma("sync", "ya", ag_in[128 * fc:128 * (fc + 1), :],
            sb["ARE"][:, 1024 * fc:1024 * (fc + 1)],
            tuple(f"LNare.{fc}.{i}" for i in range(8)), (f"agin.{fc}",))
    E.dcut("ya")
    AGINK = tuple(f"agin.{fc}" for fc in range(16))
    if STAGE < 0.9:
        return _finish_partial()
    E.coll("gpsimd", "cag", lambda e: e.collective_compute(
            "AllGather", ALU.bypass, replica_groups=[list(range(NCORE))],
            ins=[ag_in.ap().opt()], outs=[ag_out.ap().opt()]),
         AGINK, ("agout",))

 
    if STAGE < 2:
        return _finish_partial()

    for j in range(NT):
        pj = j % 2
        rblk = j // 2
        seq = j // 8
        jj = j % 8
        xnbuf = sb["BB0"][:, 8192 * pj:8192 * (pj + 1)]
        for fc in range(16):
            dma("sync", f"yn{pj}", xnbuf[:, 512 * fc:512 * (fc + 1)],
                ag_out[2048 * rblk + 128 * fc:2048 * rblk + 128 * (fc + 1),
                       512 * pj:512 * pj + 512],
                ("agout",), (XNK[pj][fc],))
        E.dcut(f"yn{pj}")

        for oc in range(7):
            bank = ps["PB0"] if oc % 2 == 0 else ps["PB1"]
            bkk = f"PB{oc % 2}"
            m0 = 768 if oc == 6 else 128 * oc
            mrows = 4 if oc == 6 else 128

            def projgroup(e, m0=m0, mrows=mrows, bank=bank, xnbuf=xnbuf):
                inst = None
                for fc in range(16):
                    inst = e.matmul(
                        bank[0:mrows, 0:512],
                        sb["WBUF"][:, 772 * fc + m0:772 * fc + m0 + mrows],
                        xnbuf[:, 512 * fc:512 * (fc + 1)],
                        start=(fc == 0), stop=(fc == 15))
                return inst
            E.op("tensor", projgroup, ("wmix",) + XNK[pj], (bkk,))

            if oc <= 3:
                cb = sb["conva"][:, 515 * (oc + 4 * pj):515 * (oc + 4 * pj) + 515]
                cbp = sb["conva"][:, 515 * (oc + 4 * (1 - pj)):515 * (oc + 4 * (1 - pj)) + 515]
                kcb = f"conv{oc}.{pj}"
                kcbp = f"conv{oc}.{1 - pj}"
                if jj == 0:
                    E.op("vector", lambda e, cb=cb: e.memset(cb[:, 0:3], 0.0),
                         (), (kcb,))
                else:
                    E.op("vector", lambda e, cb=cb, cbp=cbp: e.tensor_copy(
                            cb[:, 0:3], cbp[:, 512:515]), (kcbp,), (kcb,))
                E.op("scalar", lambda e, cb=cb, bank=bank: e.copy(
                        cb[:, 3:515], bank[:, 0:512]), (bkk,), (kcb,))
                ct = sb["convt"][:, 512 * (oc % 2):512 * (oc % 2) + 512]
                kct = f"convt{oc % 2}"
                E.op("vector", lambda e, cb=cb, ct=ct, oc=oc: e.tensor_scalar_mul(
                        ct, cb[:, 0:512], sb["convw"][:, 4 * oc:4 * oc + 1]),
                     (kcb, "convw"), (kct,))
                for tap in range(1, 4):
                    E.op("vector", lambda e, cb=cb, ct=ct, oc=oc, tap=tap:
                         e.scalar_tensor_tensor(
                             ct, cb[:, tap:tap + 512],
                             sb["convw"][:, 4 * oc + tap:4 * oc + tap + 1],
                             ct, ALU.mult, ALU.add),
                         (kcb, "convw", kct), (kct,))
                dstb = [sb["qsil"][:, :], sb["ksil"][:, :],
                        sb["vsil"][:, 0:512], sb["vsil"][:, 512:1024]][oc]
                kdst = ["qsil", "ksil", "vsil0", "vsil1"][oc]
                E.op("scalar", lambda e, ct=ct, dstb=dstb: e.activation(
                        dstb, ct, AF.Silu), (kct,), (kdst,))
            elif oc <= 5:
                h = oc - 4
                E.op("scalar", lambda e, bank=bank, h=h: e.activation(
                        sb["gates"][:, 512 * h:512 * (h + 1)], bank[:, 0:512],
                        AF.Silu), (bkk,), (f"gate{h}",))
            else:
                E.op("scalar", lambda e, bank=bank: e.copy(
                        sb["thin_sb"][:, :], bank[0:4, 0:512]), (bkk,), ("thin_sb",))

        for which in range(2):
            src = sb["qsil"] if which == 0 else sb["ksil"]
            ksrc = "qsil" if which == 0 else "ksil"
            dst = sb["qhat"] if which == 0 else sb["khat"]
            hatk = "qhat" if which == 0 else "khat"
            scl = 64.0 if which == 0 else 1.0
            biap = sb["c_pool"][0:2, 1:2] if which == 0 else sb["c_pool"][0:2, 2:3]
            E.op("vector", lambda e, src=src: e.tensor_mul(
                    sb["sqbuf"][:, :], src[:, :], src[:, :]), (ksrc,), ("sqbuf",))
            E.op("tensor", lambda e: e.matmul(
                    ps["PB5"][0:2, 0:512], sb["c_blkdiag2b"][:, :], sb["sqbuf"][:, :],
                    start=True, stop=True),
                 ("c_blkdiag2b", "sqbuf"), ("B5",))
            E.op("scalar", lambda e, scl=scl, biap=biap: e.activation(
                    sb["rsqt"][:, :], ps["PB5"][0:2, 0:512], AF.Sqrt,
                    bias=biap, scale=scl),
                 ("B5", "c_pool"), ("rsqt",))
            def _recip_b(e):
                with nc.allow_low_precision(reason="bf16 l2norm scale operand"):
                    return e.reciprocal(sb["rsq2"][:, :], sb["rsqt"][:, :])
            E.op("vector", _recip_b, ("rsqt",), ("rsq2",))
            E.op("tensor", lambda e: e.matmul(
                    ps["PB5"][:, 0:512], sb["c_exp2b"][:, :], sb["rsq2"][:, :],
                    start=True, stop=True),
                 ("c_exp2b", "rsq2"), ("B5",))
            E.op("vector", lambda e, src=src, dst=dst: e.tensor_mul(
                    dst[:, :], src[:, :], ps["PB5"][:, 0:512]),
                 (ksrc, "B5"), (hatk,))

        for subq in range(4):
            slot = subq % 2
            pso = (ps["PB6"] if slot == 0 else ps["PB7"])[:, 0:128]
            E.op("tensor", lambda e, subq=subq, pso=pso: e.matmul(
                    pso, sb["khat"][:, 128 * subq:128 * (subq + 1)], sb["c_identb"][:, :],
                    start=True, stop=True),
                 ("khat", "c_identb"), ("B6" if slot == 0 else "B7",))
            E.op("scalar", lambda e, subq=subq, pso=pso: e.copy(
                    sb["Ktok"][:, 128 * subq:128 * (subq + 1)], pso),
                 ("B6" if slot == 0 else "B7",), (f"Ktok.{subq}",))
        for h in range(2):
            for subq in range(4):
                slot = (h * 4 + subq) % 2
                pso = (ps["PB6"] if slot == 0 else ps["PB7"])[:, 0:128]
                E.op("tensor", lambda e, h=h, subq=subq, pso=pso: e.matmul(
                        pso, sb["vsil"][:, 512 * h + 128 * subq:512 * h + 128 * (subq + 1)],
                        sb["c_identb"][:, :], start=True, stop=True),
                     (f"vsil{h}", "c_identb"), ("B6" if slot == 0 else "B7",))
                E.op("scalar", lambda e, h=h, subq=subq, pso=pso: e.copy(
                        sb["Vtok"][:, 128 * col8(h, subq):128 * (col8(h, subq) + 1)], pso),
                     ("B6" if slot == 0 else "B7",), (f"Vtok.{col8(h, subq)}",))
        for subq in range(4):
            tslot = subq % 2
            pso = ps["PB4"][:, 384 + 4 * tslot:384 + 4 * (tslot + 1)]
            E.op("tensor", lambda e, subq=subq, pso=pso: e.matmul(
                    pso, sb["thin_sb"][:, 128 * subq:128 * (subq + 1)],
                    sb["c_identf"][0:4, 0:4], start=True, stop=True),
                 ("thin_sb", "c_identf"), ("B4",))
            E.op("vector", lambda e, subq=subq, pso=pso: e.tensor_copy(
                    bass.AP(sb["thin_tok"], subq, [[16, 128], [4, 4]]), pso),
                 ("B4",), ("thin_tok",))

        for h in range(2):
            E.op("vector", lambda e, h=h: e.tensor_scalar_add(
                    sb["sp_tmp"][:, 4 * h:4 * h + 4], sb["thin_tok"][:, 4 * h:4 * (h + 1)],
                    sb["vh_all"][:, 2 + h:3 + h]),
                 ("thin_tok", "vh_all"), ("sp_tmp",))
        E.op("scalar", lambda e: e.activation(sb["sp_tmp2"][:, :], sb["sp_tmp"][:, :], AF.Abs),
             ("sp_tmp",), ("sp_tmp2",))
        E.op("scalar", lambda e: e.activation(sb["sp_tmp2"][:, :], sb["sp_tmp2"][:, :],
                                              AF.Exp, scale=-1.0),
             ("sp_tmp2",), ("sp_tmp2",))
        E.op("scalar", lambda e: e.activation(sb["sp_tmp2"][:, :], sb["sp_tmp2"][:, :],
                                              AF.Ln, bias=sb["c_pool"][:, 3:4]),
             ("sp_tmp2", "c_pool"), ("sp_tmp2",))
        E.op("scalar", lambda e: e.activation(sb["sp_tmp"][:, :], sb["sp_tmp"][:, :], AF.Relu),
             ("sp_tmp",), ("sp_tmp",))
        E.op("vector", lambda e: e.tensor_add(sb["sp_tmp"][:, :], sb["sp_tmp"][:, :],
                                              sb["sp_tmp2"][:, :]),
             ("sp_tmp", "sp_tmp2"), ("sp_tmp",))
        for h in range(2):
            E.op("vector", lambda e, h=h: e.tensor_scalar_mul(
                    sb["g_all"][:, 4 * h:4 * h + 4], sb["sp_tmp"][:, 4 * h:4 * h + 4],
                    sb["negexpA"][:, h:h + 1]),
                 ("sp_tmp", "negexpA"), ("g_all",))
        for h in range(2):
            E.op("vector", lambda e, h=h: e.tensor_scalar_add(
                    sb["sp_tmp2"][:, 4 * h:4 * h + 4],
                    sb["thin_tok"][:, 8 + 4 * h:12 + 4 * h],
                    sb["biasb"][:, h:h + 1]),
                 ("thin_tok", "biasb"), ("sp_tmp2",))
        E.op("scalar", lambda e: e.activation(sb["sig2_all"][:, :], sb["sp_tmp2"][:, :],
                                              AF.Sigmoid),
             ("sp_tmp2",), ("sig2_all",))
        E.op("vector", lambda e: e.tensor_scalar_mul(sb["sig2_all"][:, :],
                                                     sb["sig2_all"][:, :], 2.0),
             ("sig2_all",), ("sig2_all",))
        E.op("tensor", lambda e: e.matmul(ps["PB4"][:, 392:400], sb["c_utblk"][:, :],
                                          sb["g_all"][:, :], start=True, stop=True),
             ("c_utblk", "g_all"), ("B4",))
        E.op("scalar", lambda e: e.copy(sb["G_all"][:, :], ps["PB4"][:, 392:400]),
             ("B4",), ("G_all",))
        E.op("scalar", lambda e: e.activation(sb["eG_all"][:, :], sb["G_all"][:, :], AF.Exp),
             ("G_all",), ("eG_all",))
        E.op("vector", lambda e: e.tensor_scalar_mul(sb["negeG_all"][:, :],
                                                     sb["eG_all"][:, :], -1.0),
             ("eG_all",), ("negeG_all",))
        E.op("tensor", lambda e: e.matmul(ps["PB4"][0:2, 400:408], sb["c_e63"][:, :],
                                          sb["G_all"][:, :], start=True, stop=True),
             ("c_e63", "G_all"), ("B4",))
        E.op("vector", lambda e: e.tensor_copy(sb["GL2"][:, :], ps["PB4"][0:2, 400:408]),
             ("B4",), ("GL2",))
        E.op("tensor", lambda e: e.matmul(ps["PB4"][:, 408:416], sb["c_exp2f"][:, :],
                                          sb["GL2"][:, :], start=True, stop=True),
             ("GL2", "c_exp2f"), ("B4",))
        E.op("vector", lambda e: e.tensor_sub(sb["d2_tmp"][:, :], ps["PB4"][:, 408:416],
                                              sb["G_all"][:, :]),
             ("B4", "G_all"), ("d2_tmp",))
        E.op("scalar", lambda e: e.activation(sb["eGLmG"][:, :], sb["d2_tmp"][:, :], AF.Exp),
             ("d2_tmp",), ("eGLmG",))
        for p in range(2):
            E.op("tensor", lambda e, p=p: e.matmul(
                    ps["PB4"][0:1, 416 + 8 * p:424 + 8 * p], sb["c_e63"][:, p:p + 1],
                    sb["G_all"][:, :], start=True, stop=True),
                 ("c_e63", "G_all"), ("B4",))
            E.op("vector", lambda e, p=p: e.tensor_copy(
                    sb["GLflat"][:, 8 * p:8 * (p + 1)],
                    ps["PB4"][0:1, 416 + 8 * p:424 + 8 * p]),
                 ("B4",), ("GLflat",))
        E.op("scalar", lambda e: e.activation(sb["eGLflat"][:, :], sb["GLflat"][:, :], AF.Exp),
             ("GLflat",), ("eGLflat",))
        E.op("tensor", lambda e: e.matmul(ps["PB4"][:, 432:448], sb["c_ones1f"][:, :],
                                          sb["eGLflat"][:, :], start=True, stop=True),
             ("c_ones1f", "eGLflat"), ("B4",))
        E.op("vector", lambda e: e.tensor_copy(sb["eGLk"][:, :], ps["PB4"][:, 432:448]),
             ("B4",), ("eGLk",))

        for ccq in range(8):
            subq = ccq // 2
            p = ccq % 2
            if p == 0:
                for h in range(2):
                    E.op("vector", lambda e, h=h, subq=subq: e.tensor_scalar_mul(
                            sb["gUT"][:, 64 * h:64 * (h + 1)], sb["c_ut64x2"][:, :],
                            sb["g_all"][:, col8(h, subq):col8(h, subq) + 1]),
                         ("g_all", "c_ut64x2"), (f"gUT{h}",))
                for pq in range(2):
                    for h in range(2):
                        def dmm(e, pq=pq, h=h):
                            e.matmul(ps["PB2"][64 * pq:64 * pq + 64, 64 * h:64 * h + 64],
                                     sb["c_ones"][64 * pq:64 * pq + 64, :],
                                     sb["gUT"][64 * pq:64 * pq + 64, 64 * h:64 * h + 64],
                                     start=True, stop=False)
                            return e.matmul(
                                ps["PB2"][64 * pq:64 * pq + 64, 64 * h:64 * h + 64],
                                sb["gUT"][64 * pq:64 * pq + 64, 64 * h:64 * h + 64],
                                sb["c_negones"][64 * pq:64 * pq + 64, 0:64],
                                start=False, stop=True)
                        E.op("tensor", dmm, (f"gUT{h}", "c_ones", "c_negones"),
                             ("B2",))
                        tok0 = 128 * subq + 64 * pq
                        E.op("tensor", lambda e, pq=pq, h=h, tok0=tok0: e.matmul(
                                ps["PB2"][64 * pq:64 * pq + 64,
                                          128 + 64 * h:128 + 64 * h + 64],
                                sb["khat"][64 * h:64 * h + 64, tok0:tok0 + 64],
                                sb["khat"][64 * h:64 * h + 64, tok0:tok0 + 64],
                                start=True, stop=True),
                             ("khat",), ("B2",))
                        E.op("tensor", lambda e, pq=pq, h=h, tok0=tok0: e.matmul(
                                ps["PB2"][64 * pq:64 * pq + 64,
                                          256 + 64 * h:256 + 64 * h + 64],
                                sb["khat"][64 * h:64 * h + 64, tok0:tok0 + 64],
                                sb["qhat"][64 * h:64 * h + 64, tok0:tok0 + 64],
                                start=True, stop=True),
                             ("khat", "qhat"), ("B2",))
                    for h in range(2):
                        bsl = (2 * pq + h) % 2
                        E.op("tensor", lambda e, pq=pq, h=h, subq=subq, bsl=bsl: e.matmul(
                                ps["PB4"][32 * bsl:32 * bsl + 1, 448:512],
                                sb["sig2_all"][64 * pq:64 * pq + 64,
                                               col8(h, subq):col8(h, subq) + 1],
                                sb["c_identf"][64 * pq:64 * pq + 64,
                                               64 * pq:64 * pq + 64],
                                start=True, stop=True),
                             ("sig2_all", "c_identf"), ("B4",))
                        E.op("scalar", lambda e, bsl=bsl: e.copy(
                                sb["brow2"][32 * bsl:32 * bsl + 1, :],
                                ps["PB4"][32 * bsl:32 * bsl + 1, 448:512]),
                             ("B4",), (f"brow{bsl}",))
                        E.op("tensor", lambda e, pq=pq, h=h, bsl=bsl: e.matmul(
                                ps["PB2"][64 * pq:64 * pq + 64,
                                          384 + 64 * h:384 + 64 * h + 64],
                                sb["c_onesb"][32 * bsl:32 * bsl + 1, 0:64],
                                sb["brow2"][32 * bsl:32 * bsl + 1, :],
                                start=True, stop=True),
                             (f"brow{bsl}", "c_onesb"), ("B2",))
                dquads = tuple("B2" for pq in range(2) for h in range(2))
                kkq = tuple("B2" for pq in range(2) for h in range(2))
                kqq = tuple("B2" for pq in range(2) for h in range(2))
                bbq = tuple("B2" for pq in range(2) for h in range(2))
                E.op("vector", lambda e: e.tensor_scalar_min(
                        sb["tmpA"][:, :], ps["PB2"][:, 0:128], 0.0),
                     dquads, ("tmpA",))
                E.op("scalar", lambda e: e.activation(sb["expD"][:, :],
                                                      sb["tmpA"][:, :], AF.Exp),
                     ("tmpA",), ("expD",))
                E.op("vector", lambda e: e.tensor_scalar_max(
                        sb["tmpB"][:, :], ps["PB2"][:, 0:128], 0.0),
                     dquads, ("tmpB",))
                E.op("scalar", lambda e: e.activation(sb["expDT"][:, :],
                                                      sb["tmpB"][:, :], AF.Exp, scale=-1.0),
                     ("tmpB",), ("expDT",))
                E.op("vector", lambda e: e.tensor_mul(sb["tmpA"][:, :], sb["expD"][:, :],
                                                      ps["PB2"][:, 128:256]),
                     ("expD",) + kkq, ("tmpA",))
                E.op("vector", lambda e: e.tensor_mul(sb["tmpB"][:, :], sb["tmpA"][:, :],
                                                      sb["c_mstrict"][:, :]),
                     ("tmpA", "c_mstrict"), ("tmpB",))
                for h in range(2):
                    E.op("vector", lambda e, h=h, subq=subq: e.tensor_scalar_mul(
                            sb["nst"][:, 384 * h:384 * h + 64],
                            sb["tmpB"][:, 64 * h:64 * h + 64],
                            sb["sig2_all"][:, col8(h, subq):col8(h, subq) + 1]),
                         ("tmpB", "sig2_all"), (f"nst{h}.0",))
                E.op("vector", lambda e: e.tensor_mul(sb["tmpA"][:, :], sb["expD"][:, :],
                                                      sb["c_mincl"][:, :]),
                     ("expD", "c_mincl"), ("tmpA",))
                for h in range(2):
                    E.op("vector", lambda e, h=h: e.tensor_mul(
                            sb["attnT"][:, 64 * h:64 * h + 64],
                            sb["tmpA"][:, 64 * h:64 * h + 64],
                            ps["PB2"][:, 256 + 64 * h:256 + 64 * h + 64]),
                         ("tmpA",) + kqq, (f"attnT{h}",))
                E.op("vector", lambda e: e.tensor_mul(sb["tmpB"][:, :], sb["expDT"][:, :],
                                                      ps["PB2"][:, 128:256]),
                     ("expDT",) + kkq, ("tmpB",))
                E.op("vector", lambda e: e.tensor_mul(sb["tmpA"][:, :], sb["tmpB"][:, :],
                                                      sb["c_mstrictlow"][:, :]),
                     ("tmpB", "c_mstrictlow"), ("tmpA",))
                for h in range(2):
                    E.op("vector", lambda e, h=h: e.tensor_mul(
                            sb["vcur"][:, 128 * h:128 * h + 64],
                            sb["tmpA"][:, 64 * h:64 * h + 64],
                            ps["PB2"][:, 384 + 64 * h:384 + 64 * h + 64]),
                         ("tmpA",) + bbq, (f"vcur{h}.0",))
                for lvl in range(5):
                    for h in range(2):
                        ibank = ps["PB4"] if h == 0 else ps["PB2"]
                        ibk = "B4" if h == 0 else "B2"
                        vp = lvl % 2
                        ncur = sb["nst"][:, 384 * h + 64 * lvl:384 * h + 64 * (lvl + 1)]
                        nnew = sb["nst"][:, 384 * h + 64 * (lvl + 1):384 * h + 64 * (lvl + 2)]
                        vc_ = sb["vcur"][:, 128 * h + 64 * vp:128 * h + 64 * vp + 64]
                        vn_ = sb["vcur"][:, 128 * h + 64 * (1 - vp):128 * h + 64 * (1 - vp) + 64]

                        def nmm(e, ncur=ncur, vc_=vc_, ibank=ibank):
                            i1 = None
                            for pq in range(2):
                                i1 = e.matmul(
                                    ibank[64 * pq:64 * pq + 64, 0:64],
                                    vc_[64 * pq:64 * pq + 64, :],
                                    ncur[64 * pq:64 * pq + 64, :],
                                    start=True, stop=True)
                            return i1
                        E.op("tensor", nmm, (f"nst{h}.{lvl}", f"vcur{h}.{vp}"),
                             (ibk,))
                        E.op("scalar", lambda e, nnew=nnew, ibank=ibank: e.activation(
                                nnew, ibank[:, 0:64], AF.Copy, scale=-1.0),
                             (ibk,), (f"nst{h}.{lvl + 1}",))
                        if lvl < 4:
                            def vmm(e, ncur=ncur, vc_=vc_, ibank=ibank):
                                i1 = None
                                for pq in range(2):
                                    i1 = e.matmul(
                                        ibank[64 * pq:64 * pq + 64, 64:128],
                                        ncur[64 * pq:64 * pq + 64, :],
                                        vc_[64 * pq:64 * pq + 64, :],
                                        start=True, stop=True)
                                return i1
                            E.op("tensor", vmm, (f"nst{h}.{lvl}", f"vcur{h}.{vp}"),
                                 (ibk,))
                            E.op("scalar", lambda e, vn_=vn_, ibank=ibank: e.activation(
                                    vn_, ibank[:, 64:128], AF.Copy, scale=-1.0),
                                 (ibk,), (f"vcur{h}.{1 - vp}",))

            gchunk = 8 * jj + ccq
            sslot = gchunk % 2

            def chain(h, ccq=ccq, subq=subq, p=p, seq=seq, sslot=sslot):
                PBC = ps["PB3"] if h == 0 else ps["PB5"]
                BKC = "B3" if h == 0 else "B5"
                c8 = col8(h, subq)
                sread = sb["Ssb_b"][64 * h:64 * h + 64,
                                    256 * sslot + 128 * seq:256 * sslot + 128 * seq + 128]
                sreadf = sb["Ssb"][64 * h:64 * h + 64,
                                   256 * sslot + 128 * seq:256 * sslot + 128 * seq + 128]
                swrite = sb["Ssb"][64 * h:64 * h + 64,
                                   256 * (1 - sslot) + 128 * seq:
                                   256 * (1 - sslot) + 128 * seq + 128]
                swrite_b = sb["Ssb_b"][64 * h:64 * h + 64,
                                       256 * (1 - sslot) + 128 * seq:
                                       256 * (1 - sslot) + 128 * seq + 128]
                ksr = f"Sb.{sslot}.{h}"
                ksrf = f"S.{sslot}.{h}"
                ksw = f"S.{1 - sslot}.{h}"
                kswb = f"Sb.{1 - sslot}.{h}"
                tok0 = 64 * ccq
                E.op("tensor", lambda e: e.matmul(
                        PBC[64 * p:64 * p + 64, 0:128],
                        sb["khat"][64 * h:64 * h + 64, tok0:tok0 + 64],
                        sread, start=True, stop=True),
                     ("khat", ksr), (BKC,))
                yield
                E.op("vector", lambda e: e.scalar_tensor_tensor(
                        sb["Vsl"][64 * p:64 * p + 64, 256 * h:256 * h + 128],
                        PBC[64 * p:64 * p + 64, 0:128],
                        sb["negeG_all"][64 * p:64 * p + 64, c8:c8 + 1],
                        sb["Vtok"][64 * p:64 * p + 64, 128 * c8:128 * c8 + 128],
                        ALU.mult, ALU.add),
                     (BKC, "negeG_all", f"Vtok.{c8}"), (f"Vsl{h}.0",))
                yield
                for step in range(6):
                    lvl = 5 - step
                    src_ = sb["Vsl"][:, 256 * h + 128 * (step % 2):
                                     256 * h + 128 * (step % 2) + 128]
                    dst_ = sb["Vsl"][:, 256 * h + 128 * ((step + 1) % 2):
                                     256 * h + 128 * ((step + 1) % 2) + 128]
                    nsl = sb["nst"][:, 384 * h + 64 * lvl:384 * h + 64 * (lvl + 1)]
                    E.op("tensor", lambda e, nsl=nsl, src_=src_: e.matmul(
                            PBC[64 * p:64 * p + 64, 128:256],
                            nsl[64 * p:64 * p + 64, :],
                            src_[64 * p:64 * p + 64, :], start=True, stop=True),
                         (f"nst{h}.{lvl}", f"Vsl{h}.{step % 2}"), (BKC,))
                    yield
                    E.op("vector", lambda e, src_=src_, dst_=dst_: e.scalar_tensor_tensor(
                            dst_[64 * p:64 * p + 64, :],
                            PBC[64 * p:64 * p + 64, 128:256], -1.0,
                            src_[64 * p:64 * p + 64, :], ALU.mult, ALU.add),
                         (BKC, f"Vsl{h}.{step % 2}"), (f"Vsl{h}.{(step + 1) % 2}",))
                    yield
                E.op("vector", lambda e: e.tensor_scalar_mul(
                        sb["Ubuf"][64 * p:64 * p + 64, 128 * h:128 * h + 128],
                        sb["Vsl"][64 * p:64 * p + 64, 256 * h:256 * h + 128],
                        sb["sig2_all"][64 * p:64 * p + 64, c8:c8 + 1]),
                     (f"Vsl{h}.0", "sig2_all"), (f"Ubuf{h}",))
                yield
                E.op("tensor", lambda e: e.matmul(
                        PBC[64 * p:64 * p + 64, 256:384],
                        sb["attnT"][64 * p:64 * p + 64, 64 * h:64 * h + 64],
                        sb["Ubuf"][64 * p:64 * p + 64, 128 * h:128 * h + 128],
                        start=True, stop=True),
                     (f"attnT{h}", f"Ubuf{h}"), (BKC,))
                yield
                E.op("tensor", lambda e: e.matmul(
                        PBC[64 * p:64 * p + 64, 384:512],
                        sb["qhat"][64 * h:64 * h + 64, tok0:tok0 + 64],
                        sread, start=True, stop=True),
                     ("qhat", ksr), (BKC,))
                yield
                E.op("scalar", lambda e: e.copy(
                        sb["Oat"][64 * p:64 * p + 64, 128 * h:128 * h + 128],
                        PBC[64 * p:64 * p + 64, 256:384]),
                     (BKC,), (f"Oat{h}",))
                yield
                E.op("vector", lambda e: e.scalar_tensor_tensor(
                        sb["Otmp"][64 * p:64 * p + 64, 128 * h:128 * h + 128],
                        PBC[64 * p:64 * p + 64, 384:512],
                        sb["eG_all"][64 * p:64 * p + 64, c8:c8 + 1],
                        sb["Oat"][64 * p:64 * p + 64, 128 * h:128 * h + 128],
                        ALU.mult, ALU.add),
                     (BKC, f"Oat{h}", "eG_all"), (f"Otmp{h}",))
                yield
                E.op("scalar", lambda e: e.activation(
                        sb["Osq"][64 * p:64 * p + 64, 128 * h:128 * h + 128],
                        sb["Otmp"][64 * p:64 * p + 64, 128 * h:128 * h + 128],
                        AF.Square),
                     (f"Otmp{h}",), (f"Osq{h}",))
                yield
                E.op("vector", lambda e: e.reduce_sum(
                        sb["ssO"][64 * p:64 * p + 64, h:h + 1],
                        sb["Osq"][64 * p:64 * p + 64, 128 * h:128 * h + 128],
                        axis=AX.X),
                     (f"Osq{h}",), (f"ssO{h}",))
                yield
                E.op("scalar", lambda e: e.activation(
                        sb["rsOt"][64 * p:64 * p + 64, h:h + 1],
                        sb["ssO"][64 * p:64 * p + 64, h:h + 1],
                        AF.Sqrt, bias=sb["c_pool"][64 * p:64 * p + 64, 4:5],
                        scale=1.0 / 128.0),
                     (f"ssO{h}", "c_pool"), (f"rsOt{h}",))
                yield
                E.op("vector", lambda e: e.reciprocal(
                        sb["rsO"][64 * p:64 * p + 64, h:h + 1],
                        sb["rsOt"][64 * p:64 * p + 64, h:h + 1]),
                     (f"rsOt{h}",), (f"rsO{h}",))
                yield
                E.op("vector", lambda e: e.tensor_scalar_mul(
                        sb["ONp"][64 * p:64 * p + 64, 128 * h:128 * h + 128],
                        sb["Otmp"][64 * p:64 * p + 64, 128 * h:128 * h + 128],
                        sb["rsO"][64 * p:64 * p + 64, h:h + 1]),
                     (f"Otmp{h}", f"rsO{h}"), (f"ONp{h}.{p}",))
                yield
                E.op("vector", lambda e: e.tensor_scalar_mul(
                        sb["kbar"][64 * p:64 * p + 64, 64 * h:64 * h + 64],
                        sb["Ktok"][64 * p:64 * p + 64,
                                   128 * subq + 64 * h:128 * subq + 64 * h + 64],
                        sb["eGLmG"][64 * p:64 * p + 64, c8:c8 + 1]),
                     (f"Ktok.{subq}", "eGLmG"), (f"kbar{h}",))
                yield
                E.op("tensor", lambda e: e.matmul(
                        ps["PB4"][64 * h:64 * h + 64, 256:384],
                        sb["kbar"][64 * p:64 * p + 64, 64 * h:64 * h + 64],
                        sb["Ubuf"][64 * p:64 * p + 64, 128 * h:128 * h + 128],
                        start=True, stop=True),
                     (f"kbar{h}", f"Ubuf{h}"), ("B4",))
                yield
                cflat = 8 * p + 4 * h + subq
                E.op("vector", lambda e: e.scalar_tensor_tensor(
                        swrite, sreadf,
                        sb["eGLk"][64 * h:64 * h + 64, cflat:cflat + 1],
                        ps["PB4"][64 * h:64 * h + 64, 256:384], ALU.mult, ALU.add),
                     (ksrf, "eGLk", "B4"), (ksw,))
                yield
                E.op("scalar", lambda e: e.copy(swrite_b, swrite),
                     (ksw,), (kswb,))

            gens = [chain(0), chain(1)]
            alive = [True, True]
            while alive[0] or alive[1]:
                for gi in (0, 1):
                    if alive[gi]:
                        try:
                            next(gens[gi])
                        except StopIteration:
                            alive[gi] = False

            if p == 1:
                for h in range(2):
                    slot = h
                    pso = (ps["PB6"] if slot == 0 else ps["PB7"])[:, 0:128]
                    E.op("tensor", lambda e, h=h, pso=pso: e.matmul(
                            pso, sb["ONp"][:, 128 * h:128 * h + 128], sb["c_identb"][:, :],
                            start=True, stop=True),
                         (f"ONp{h}.0", f"ONp{h}.1", "c_identb"), ("B6" if slot == 0 else "B7",))
                    E.op("scalar", lambda e, h=h, pso=pso: e.copy(
                            sb["OTr"][:, 128 * h:128 * h + 128], pso),
                         ("B6" if slot == 0 else "B7",), (f"OTr{h}",))
                    E.op("vector", lambda e, h=h, subq=subq: e.tensor_mul(
                            sb["staging"][:, 512 * h + 128 * subq:512 * h + 128 * (subq + 1)],
                            sb["OTr"][:, 128 * h:128 * h + 128],
                            sb["gates"][:, 512 * h + 128 * subq:512 * h + 128 * (subq + 1)]),
                         (f"OTr{h}", f"gate{h}"), (f"stag{h}.{subq}",))

        dblk = j // 2
        for h in range(2):
            dma("sync", f"ys{pj}", a2a_in[256 * dblk + 128 * h:256 * dblk + 128 * (h + 1),
                               512 * pj:512 * pj + 512],
                sb["staging"][:, 512 * h:512 * (h + 1)],
                tuple(f"stag{h}.{sq}" for sq in range(4)), (f"a2ain.{j}.{h}",))
        E.dcut(f"ys{pj}")

 
    if STAGE < 3:
        return _finish_partial()

    def a2a(e):
        with allow_ici_all_to_all():
            return e.collective_compute(
                "AllToAll", ALU.bypass, replica_groups=[list(range(NCORE))],
                ins=[a2a_in.ap().opt()], outs=[a2a_out.ap().opt()])
    E.coll("gpsimd", "ca2a", a2a,
           tuple(f"a2ain.{j}.{h}" for j in range(NT) for h in range(2)),
           ("a2aout",))

 
    if STAGE < 4:
        return _finish_partial()

    w0a_r = w0a_b.ap().rearrange("(a p) (b q) -> b p a q", p=128, q=128)
    w1a_r = w1a_b.ap().rearrange("(a p) (b q) -> b p a q", p=128, q=128)
    for ft in range(4):
        for fc in range(16):
            dma("sync", "yf", sb["FSCR"][:, 256 * fc:256 * (fc + 1)],
                ag_in[128 * fc:128 * (fc + 1), 256 * ft:256 * (ft + 1)],
                (f"agin.{fc}",), (XNFK[fc],))
        E.dcut("yf")

        def maskmm(e):
            inst = None
            for fc in range(16):
                inst = e.matmul(ps["PB5"][0:1, 0:256],
                                sb["wd_b"][:, fc:fc + 1],
                                sb["FSCR"][:, 256 * fc:256 * (fc + 1)],
                                start=(fc == 0), stop=(fc == 15))
            return inst
        E.op("tensor", maskmm, ("wd_b",) + XNFK, ("B5",))
        E.op("vector", lambda e: e.tensor_scalar(sb["mrow"][:, :], ps["PB5"][0:1, 0:256],
                                                 0.0, None, ALU.is_ge),
             ("B5",), ("mrow",))
        E.op("tensor", lambda e: e.matmul(ps["PB5"][:, 256:512], sb["c_ones1b"][:, :],
                                          sb["mrow"][:, :], start=True, stop=True),
             ("c_ones1b", "mrow"), ("B5",))
        E.op("scalar", lambda e, ft=ft: e.copy(sb["mball"][:, 256 * ft:256 * (ft + 1)],
                                               ps["PB5"][:, 256:512]),
             ("B5",), ("mball",))
        for ex in range(2):
            wa_r = w0a_r if ex == 0 else w1a_r
            wkeys = W0AK if ex == 0 else W1AK
            biask = "b0a_sb" if ex == 0 else "b1a_sb"
            act = AF.Gelu if ex == 0 else AF.Relu
            hd = h0_dram if ex == 0 else h1_dram
            deng = "sync" if ex == 0 else "scalar"
            def wload(ic):
                wp = ic % 2
                wsl = sb["FSCR"][:, 4096 + 2048 * wp:4096 + 2048 * (wp + 1)]
                kw = f"wstr.{wp}"
                wstream = ("ywa" if deng == "sync" else "swa") + str(wp)
                dma(deng, wstream, wsl.rearrange("p (a q) -> p a q", q=128), wa_r[ic],
                    wkeys, (kw,))
                E.dcut(wstream)
            wload(0)
            for ic in range(32):
                wp = ic % 2
                wsl = sb["FSCR"][:, 4096 + 2048 * wp:4096 + 2048 * (wp + 1)]
                kw = f"wstr.{wp}"
                if ic + 1 < 32:
                    wload(ic + 1)
                bank = ps["PB0"] if ex == 0 else ps["PB1"]
                bkk = f"PB{ex}"

                def hmm(e, wsl=wsl, bank=bank):
                    inst = None
                    for fc in range(16):
                        inst = e.matmul(bank[:, 0:256],
                                        wsl[:, 128 * fc:128 * (fc + 1)],
                                        sb["FSCR"][:, 256 * fc:256 * (fc + 1)],
                                        start=(fc == 0), stop=(fc == 15))
                    return inst
                E.op("tensor", hmm, (kw,) + XNFK, (bkk,))
                hsl = ic % 2
                hst = sb["hstage"][:, 512 * ex + 256 * hsl:512 * ex + 256 * (hsl + 1)]
                khs = f"hstage{ex}.{hsl}"
                E.op("scalar", lambda e, bank=bank, hst=hst, ic=ic, biask=biask, act=act:
                     e.activation(hst, bank[:, 0:256], act, bias=sb[biask][:, ic:ic + 1]),
                     (bkk, biask), (khs,))
                hstream = f"gh{ex}{ic % 2}"
                dma("gpsimd", hstream, hd[128 * ic:128 * (ic + 1), 256 * ft:256 * (ft + 1)],
                    hst, (khs,), (f"hd{ex}.{ic}",))
                E.dcut(hstream)
    HD0K = tuple(f"hd0.{ic}" for ic in range(32))
    HD1K = tuple(f"hd1.{ic}" for ic in range(32))

 
    if STAGE < 5:
        return _finish_partial()

    E.handoff(("wo.0", "wo.1"), ("wmix",))
    E.handoff(OFMK, XNFK + ("wstr.0", "wstr.1"))
    E.handoff(tuple(f"ARE.{cc}" for cc in range(16)),
              tuple(f"LNare.{fc}.{i}" for fc in range(16) for i in range(8))
              + AGINK)
    wo_r = wo_b.ap().rearrange("(a p) (b q) -> b p a q", p=128, q=128)
    for t2 in range(2):
        for vc in range(16):
            dma("sync", "yo", sb["FSCR"][:, 512 * vc:512 * (vc + 1)],
                a2a_out[128 * vc:128 * (vc + 1), 512 * t2:512 * (t2 + 1)],
                ("a2aout",), (OFMK[vc],))
        E.dcut("yo")
        def woload(ccq):
            wp = ccq % 2
            wsl = sb["WBUF"][:, 2048 * wp:2048 * (wp + 1)]
            dma("scalar", f"swo{wp}", wsl.rearrange("p (a q) -> p a q", q=128),
                wo_r[ccq], WOBK, (f"wo.{wp}",))
            E.dcut(f"swo{wp}")
        if t2 == 0:
            woload(0)
        for ccq in range(16):
            wp = ccq % 2
            wsl = sb["WBUF"][:, 2048 * wp:2048 * (wp + 1)]
            kw = f"wo.{wp}"
            if ccq + 1 < 16:
                woload(ccq + 1)
            elif t2 == 0:
                woload(0)

            def owmm(e, wsl=wsl):
                inst = None
                for vc in range(16):
                    inst = e.matmul(ps["PB0"][:, 0:512],
                                    wsl[:, 128 * vc:128 * (vc + 1)],
                                    sb["FSCR"][:, 512 * vc:512 * (vc + 1)],
                                    start=(vc == 0), stop=(vc == 15))
                return inst
            E.op("tensor", owmm, (kw,) + OFMK, ("PB0",))
            E.op("scalar", lambda e, ccq=ccq, t2=t2: e.copy(
                    sb["ARE"][:, 1024 * ccq + 512 * t2:1024 * ccq + 512 * (t2 + 1)],
                    ps["PB0"][:, 0:512]),
                 ("PB0",), (f"ARE.{ccq}",))

 
    if STAGE < 6:
        return _finish_partial()

    E.handoff(tuple(f"w0bB.{cj}" for cj in range(4)), ("xn0.0",) + XNK[0] + XNK[1])
    E.handoff(("w1bB.0", "w1bB.1", "w1bB.2"), ("wo.0", "wo.1"))
    E.handoff(("w1bB.3",), OFMK)
    w0b_r = w0b_b.ap().rearrange("(a p) (b q) -> b p a q", p=128, q=128)
    w1b_r = w1b_b.ap().rearrange("(a p) (b q) -> b p a q", p=128, q=128)
    h0_r = h0_dram.ap().rearrange("(a p) (b q) -> b p a q", p=128, q=128)
    h1_r = h1_dram.ap().rearrange("(a p) (b q) -> b p a q", p=128, q=128)
    OUTK = []
    for cblk in range(4):
        for cj in range(4):
            ccq = 4 * cblk + cj
            dma("sync", "yb", sb["BB0"][:, 4096 * cj:4096 * (cj + 1)]
                .rearrange("p (a q) -> p a q", q=128), w0b_r[ccq],
                W0BK, (f"w0bB.{cj}",))
            if cj < 3:
                w1dst = sb["WBUF"][:, 4096 * cj:4096 * (cj + 1)]
            else:
                w1dst = sb["FSCR"][:, 0:4096]
            dma("scalar", "sb1", w1dst.rearrange("p (a q) -> p a q", q=128),
                w1b_r[ccq], W1BK, (f"w1bB.{cj}",))
        E.dcut("yb")
        E.dcut("sb1")
        def hload(ft):
            hp2 = ft % 2
            dma("sync", f"yh{hp2}",
                sb["hbuf"][:, 8192 * hp2:8192 * hp2 + 4096]
                .rearrange("p (a q) -> p a q", q=128),
                h0_r[ft], HD0K, (f"hb0.{hp2}",))
            dma("sync", f"yh{hp2}",
                sb["hbuf"][:, 8192 * hp2 + 4096:8192 * hp2 + 8192]
                .rearrange("p (a q) -> p a q", q=128),
                h1_r[ft], HD1K, (f"hb1.{hp2}",))
            E.dcut(f"yh{hp2}")
        if cblk == 0:
            hload(0)
        for ft in range(8):
            hp2 = ft % 2
            if ft + 1 < 8:
                hload(ft + 1)
            elif cblk < 3:
                hload(0)
            for cj in range(4):
                ccq = 4 * cblk + cj
                if cj < 3:
                    w1sl = sb["WBUF"][:, 4096 * cj:4096 * (cj + 1)]
                else:
                    w1sl = sb["FSCR"][:, 0:4096]

                def o0mm(e, cj=cj, hp2=hp2):
                    inst = None
                    for ic in range(32):
                        inst = e.matmul(ps["PB0"][:, 0:128],
                                        sb["BB0"][:, 4096 * cj + 128 * ic:
                                                  4096 * cj + 128 * (ic + 1)],
                                        sb["hbuf"][:, 8192 * hp2 + 128 * ic:
                                                   8192 * hp2 + 128 * (ic + 1)],
                                        start=(ic == 0), stop=(ic == 31))
                    return inst
                E.op("tensor", o0mm, (f"w0bB.{cj}", f"hb0.{hp2}"), ("PB0",))

                def o1mm(e, w1sl=w1sl, hp2=hp2):
                    inst = None
                    for ic in range(32):
                        inst = e.matmul(ps["PB1"][:, 0:128],
                                        w1sl[:, 128 * ic:128 * (ic + 1)],
                                        sb["hbuf"][:, 8192 * hp2 + 4096 + 128 * ic:
                                                   8192 * hp2 + 4096 + 128 * (ic + 1)],
                                        start=(ic == 0), stop=(ic == 31))
                    return inst
                E.op("tensor", o1mm, (f"w1bB.{cj}", f"hb1.{hp2}"), ("PB1",))
                xp = (cblk * 32 + ft * 4 + cj) % 2
                xsl = sb["xfr"][:, 128 * xp:128 * (xp + 1)]
                kxf = f"xfr.{xp}"
                dma("gpsimd", f"gx{xp}", xsl, d_xfm[128 * ccq:128 * (ccq + 1),
                                         128 * ft:128 * (ft + 1)], (), (kxf,))
                E.dcut(f"gx{xp}")
                E.op("vector", lambda e, ccq=ccq: e.tensor_scalar_add(
                        sb["fb0"][:, :], ps["PB0"][:, 0:128],
                        sb["b0b_sb"][:, ccq:ccq + 1]),
                     ("PB0", "b0b_sb"), ("fb0",))
                E.op("vector", lambda e, ccq=ccq: e.tensor_scalar_add(
                        sb["fb1"][:, :], ps["PB1"][:, 0:128],
                        sb["b1b_sb"][:, ccq:ccq + 1]),
                     ("PB1", "b1b_sb"), ("fb1",))
                E.op("vector", lambda e: e.tensor_sub(sb["fb2"][:, :], sb["fb0"][:, :],
                                                      sb["fb1"][:, :]),
                     ("fb0", "fb1"), ("fb2",))
                E.op("vector", lambda e, ft=ft: e.tensor_mul(
                        sb["fb0"][:, :], sb["fb2"][:, :],
                        sb["mball"][:, 128 * ft:128 * (ft + 1)]),
                     ("fb2", "mball"), ("fb0",))
                E.op("vector", lambda e: e.tensor_add(sb["fb2"][:, :], sb["fb1"][:, :],
                                                      sb["fb0"][:, :]),
                     ("fb1", "fb0"), ("fb2",))
                E.op("vector", lambda e, ccq=ccq, ft=ft: e.tensor_add(
                        sb["fb0"][:, :], sb["fb2"][:, :],
                        sb["ARE"][:, 1024 * ccq + 128 * ft:1024 * ccq + 128 * (ft + 1)]),
                     ("fb2", f"ARE.{ccq}"), ("fb0",))
                op_ = (cblk * 32 + ft * 4 + cj) % 2
                osl = sb["fbo"][:, 128 * op_:128 * (op_ + 1)]
                ko = f"fbo.{op_}"
                E.op("vector", lambda e, xsl=xsl, osl=osl: e.tensor_add(
                        osl, sb["fb0"][:, :], xsl),
                     ("fb0", kxf), (ko,))
                okey = f"outw.{ccq}.{ft}"
                OUTK.append(okey)
                dma("sync", f"yw{op_}", d_out[128 * ccq:128 * (ccq + 1),
                                              128 * ft:128 * (ft + 1)],
                    osl, (ko,), (okey,))
                E.dcut(f"yw{op_}")

    E.op("sync", lambda e: e.nop(), reads=tuple(OUTK), writes=())

    E.play(nc)
    return nc


_PROG_CACHE = {}


def kernel(**inputs):
    x = np.asarray(inputs["L0_attn_x"])
    qk_w = np.asarray(inputs["L0_attn_qk_w"])
    vg_w = np.asarray(inputs["L0_attn_vg_w"])
    o_w = np.asarray(inputs["L0_attn_o_w"])
    a_w = np.asarray(inputs["L0_attn_a_w"])
    b_w = np.asarray(inputs["L0_attn_b_w"])
    b_bias = np.asarray(inputs["L0_attn_b_bias"])
    beta_bias = np.asarray(inputs["L0_attn_beta_bias"])
    A_log = np.asarray(inputs["L0_attn_A_log"])
    dt_bias = np.asarray(inputs["L0_attn_dt_bias"])
    q_conv_w = np.asarray(inputs["L0_attn_q_conv_w"])
    k_conv_w = np.asarray(inputs["L0_attn_k_conv_w"])
    v_conv_w = np.asarray(inputs["L0_attn_v_conv_w"])
    Wg = np.asarray(inputs["L0_ffn_Wg"])
    W0a = np.asarray(inputs["L0_ffn_W0a"])
    b0a = np.asarray(inputs["L0_ffn_b0a"])
    W0b = np.asarray(inputs["L0_ffn_W0b"])
    b0b = np.asarray(inputs["L0_ffn_b0b"])
    W1a = np.asarray(inputs["L0_ffn_W1a"])
    b1a = np.asarray(inputs["L0_ffn_b1a"])
    W1b = np.asarray(inputs["L0_ffn_W1b"])
    b1b = np.asarray(inputs["L0_ffn_b1b"])

    shared = {
        "w_o": np.ascontiguousarray(o_w.T),
        "w0a": np.ascontiguousarray(W0a.T),
        "w0b": np.ascontiguousarray(W0b.T),
        "w1a": np.ascontiguousarray(W1a.T),
        "w1b": np.ascontiguousarray(W1b.T),
        "b0a_t": np.ascontiguousarray(b0a.reshape(32, 128).T),
        "b1a_t": np.ascontiguousarray(b1a.reshape(32, 128).T),
        "b0b_t": np.ascontiguousarray(b0b.reshape(16, 128).T),
        "b1b_t": np.ascontiguousarray(b1b.reshape(16, 128).T),
        "wg": np.ascontiguousarray(Wg.T),
    }
    shared.update({"d_" + k: v for k, v in host_constants().items()})

    in_maps = []
    for c in range(NCORE):
        bsel = c // 4
        t0 = (c % 4) * TOWN
        x_own = np.ascontiguousarray(x[bsel, t0:t0 + TOWN])
        w_mix = np.ascontiguousarray(np.concatenate([
            qk_w[128 * c:128 * (c + 1)],
            qk_w[1024 + 128 * c:1024 + 128 * (c + 1)],
            vg_w[256 * c:256 * (c + 1)],
            vg_w[2048 + 256 * c:2048 + 256 * (c + 1)],
            a_w[2 * c:2 * c + 2],
            b_w[2 * c:2 * c + 2],
        ], axis=0).T)
        w_conv = np.ascontiguousarray(np.concatenate([
            q_conv_w[128 * c:128 * (c + 1), 0, :],
            k_conv_w[128 * c:128 * (c + 1), 0, :],
            v_conv_w[256 * c:256 * (c + 1), 0, :],
        ], axis=0))
        vhead = np.ascontiguousarray(np.concatenate([
            A_log[2 * c:2 * c + 2], dt_bias[2 * c:2 * c + 2],
            b_bias[2 * c:2 * c + 2], beta_bias[2 * c:2 * c + 2],
        ]).reshape(1, 8))
        m = {
            "x_tok": x_own,
            "x_fm": np.ascontiguousarray(x_own.T),
            "w_mix": w_mix,
            "w_conv": w_conv,
            "vhead": vhead,
        }
        m.update(shared)
        in_maps.append(m)

    if "nc" not in _PROG_CACHE:
        _PROG_CACHE["nc"] = build_program()
    nc = _PROG_CACHE["nc"]
    res = run_bass_kernel_spmd(nc, in_maps, core_ids=list(range(NCORE)))
    out = np.empty((2, 4096, 2048), np.float32)
    for c in range(NCORE):
        bsel = c // 4
        t0 = (c % 4) * TOWN
        out[bsel, t0:t0 + TOWN, :] = np.asarray(res.results[c]["out_fm"]).T
    return out


# version 10 (final)
# speedup vs baseline: 1.3386x; 1.0180x over previous
import os
os.environ.setdefault("TRN_ALLOW_ICI_ALL_TO_ALL", "1")

import numpy as np
import ml_dtypes
import concourse.bass as bass
import concourse.mybir as mybir
from concourse.bass_utils import run_bass_kernel_spmd
from concourse.aot_env import allow_ici_all_to_all

F32 = mybir.dt.float32
BF16 = mybir.dt.bfloat16
AF = mybir.ActivationFunctionType
ALU = mybir.AluOpType
AX = mybir.AxisListType
BF = ml_dtypes.bfloat16

NCORE = 8
C = 2048
TOWN = 1024
NT = 16
L = 64
EPS_F32 = float(np.finfo(np.float32).eps)

ENGINES = ("tensor", "vector", "scalar", "sync", "gpsimd")


class Emitter:

    def __init__(self):
        self.streams = {e: [] for e in ENGINES}
        self.cnt = {e: 0 for e in ENGINES}
        self.lastw = {}
        self.readers = {}
        self.maxwait = {e: {} for e in ENGINES}
        self.dstreams = {}
        self.know = {e: {} for e in ENGINES}
        self.kckpt = {e: [(0, {})] for e in ENGINES}
        self._know_now = {}
        self._kdirty = False

    def _stream(self, name):
        return self.dstreams.setdefault(
            name, {"cnt": 0, "ends": [], "know": [(0, {})]})

    def dcut(self, name):
        st = self._stream(name)
        if not st["ends"] or st["ends"][-1] != st["cnt"]:
            st["ends"].append(st["cnt"])
            st["know"].append((len(st["ends"]) - 1,
                               dict(st.get("_open_know", {}))))

    def _ckpt(self, eng):
        lst = self.kckpt[eng]
        if lst[-1][0] != self.cnt[eng] or len(lst) == 1:
            lst.append((self.cnt[eng], dict(self.know[eng])))
        else:
            lst[-1] = (self.cnt[eng], dict(self.know[eng]))

    def _know_at(self, Y, c):
        lst = self.kckpt[Y]
        lo, hi = 0, len(lst) - 1
        best = lst[0][1]
        while lo <= hi:
            mid = (lo + hi) // 2
            if lst[mid][0] < c:
                best = lst[mid][1]
                lo = mid + 1
            else:
                hi = mid - 1
        return best

    def _merge_know(self, eng, other):
        k = self.know[eng]
        for key, val in other.items():
            if val > k.get(key, -1):
                k[key] = val

    def _dep(self, eng, deps):
        agg = {}
        for tok in deps:
            key = (tok[0], tok[1])
            if key not in agg or tok[2] > agg[key][2]:
                agg[key] = tok
        know = self.know[eng]
        for key, tok in sorted(agg.items(), key=lambda kv: str(kv[0])):
            if tok[2] <= know.get(key, -1):
                continue
            if tok[0] == "E" and tok[1] == eng:
                self.streams[eng].append(("dr",))
                know[key] = self.cnt[eng]
                self._kdirty = True
                continue
            know[key] = tok[2]
            self._kdirty = True
            if tok[0] == "E":
                self.streams[eng].append(("we", tok[1], tok[2]))
                self._merge_know(eng, self._know_at(tok[1], tok[2]))
            else:
                self.streams[eng].append(("wd", tok[1], tok[2]))
                st = self.dstreams[tok[1]]
                gk = st["know"]
                gidx = min(tok[2], len(gk) - 1 - 1) if False else tok[2]
                best = {}
                for gi, kd in gk:
                    if gi <= gidx:
                        best = kd
                    else:
                        break
                self._merge_know(eng, best)

    def _gather(self, reads, writes):
        deps = []
        for r in reads:
            lw = self.lastw.get(r)
            if lw:
                deps.append(lw)
        for w in writes:
            lw = self.lastw.get(w)
            if lw:
                deps.append(lw)
            for tok in self.readers.get(w, {}).values():
                deps.append(tok)
        return deps

    def _register(self, tok, reads, writes):
        for r in reads:
            d = self.readers.setdefault(r, {})
            key = (tok[0], tok[1])
            if key not in d or tok[2] > d[key][2]:
                d[key] = tok
        for w in writes:
            self.lastw[w] = tok
            self.readers[w] = {}

    def op(self, eng, fn, reads=(), writes=()):
        self._kdirty = False
        self._dep(eng, self._gather(reads, writes))
        if self._kdirty:
            self._ckpt(eng)
        self.cnt[eng] += 1
        self.streams[eng].append(("i", fn, eng, 1))
        tok = ("E", eng, self.cnt[eng])
        self._register(tok, reads, writes)
        return tok

    def dma(self, eng, stream, out_ap, in_ap, reads=(), writes=()):
        st0 = self._stream(stream)
        open_tok = ("D", stream, len(st0["ends"]))
        deps = [d for d in self._gather(reads, writes) if d != open_tok]
        self._kdirty = False
        self._dep(eng, deps)
        if self._kdirty:
            self._ckpt(eng)
        st0["cnt"] += 16
        self.streams[eng].append(
            ("i", lambda e, o=out_ap, i=in_ap: e.dma_start(out=o, in_=i),
             stream, 16))
        st0["_open_know"] = dict(self.know[eng])
        tok = ("D", stream, len(st0["ends"]))
        self._register(tok, reads, writes)
        return tok

    def coll(self, eng, stream, fn, reads=(), writes=()):
        self._kdirty = False
        self._dep(eng, self._gather(reads, writes))
        if self._kdirty:
            self._ckpt(eng)
        st = self._stream(stream)
        st["cnt"] += 1
        self.streams[eng].append(("i", fn, stream, 1))
        st["_open_know"] = dict(self.know[eng])
        self.dcut(stream)
        tok = ("D", stream, len(st["ends"]) - 1)
        self._register(tok, reads, writes)
        return tok

    def handoff(self, news, olds):
        rd = {}
        for old in olds:
            lw = self.lastw.pop(old, None)
            if lw:
                key = (lw[0], lw[1])
                if key not in rd or lw[2] > rd[key][2]:
                    rd[key] = lw
            for key, tok in self.readers.pop(old, {}).items():
                if key not in rd or tok[2] > rd[key][2]:
                    rd[key] = tok
        for new in news:
            self.lastw.pop(new, None)
            self.readers[new] = dict(rd)

    def play(self, nc):
        for s in self.dstreams:
            self.dcut(s)
        import contextlib
        with contextlib.ExitStack() as stack:
            sems = {}
            for e in ENGINES:
                sems[("E", e)] = stack.enter_context(nc.semaphore("se_" + e))
            for s in self.dstreams:
                sems[("D", s)] = stack.enter_context(nc.semaphore("sd_" + s))
            block = stack.enter_context(nc.Block())

            def runner(name):
                def run(e):
                    for item in self.streams[name]:
                        kind = item[0]
                        if kind == "dr":
                            e.drain()
                        elif kind == "we":
                            e.wait_ge(sems[("E", item[1])], item[2])
                        elif kind == "wd":
                            st = self.dstreams[item[1]]
                            gidx = min(item[2], len(st["ends"]) - 1)
                            e.wait_ge(sems[("D", item[1])], st["ends"][gidx])
                        else:
                            _, fn, target, inc = item
                            if target in ENGINES:
                                fn(e).then_inc(sems[("E", target)], inc)
                            else:
                                fn(e).then_inc(sems[("D", target)], inc)
                return run

            block.tensor(runner("tensor"))
            block.vector(runner("vector"))
            block.scalar(runner("scalar"))
            block.sync(runner("sync"))
            block.gpsimd(runner("gpsimd"))


def host_constants():
    ut64 = np.triu(np.ones((L, L), np.float32))
    ut64s = np.triu(np.ones((L, L), np.float32), 1)
    lt64s = np.tril(np.ones((L, L), np.float32), -1)
    c = {}
    c["c_identb"] = np.eye(128, dtype=np.float32).astype(BF)
    c["c_identf"] = np.eye(128, dtype=np.float32)
    c["c_ut64x2"] = np.tile(ut64, (2, 1))
    c["c_ones"] = np.ones((128, 64), np.float32)
    c["c_negones"] = -np.ones((128, 64), np.float32)
    c["c_mstrict"] = np.tile(ut64s, (2, 2))
    c["c_mincl"] = np.tile(ut64, (2, 2))
    c["c_mstrictlow"] = np.tile(lt64s, (2, 2))
    blk = np.zeros((128, 128), np.float32)
    blk[0:64, 0:64] = ut64
    blk[64:128, 64:128] = ut64
    c["c_utblk"] = blk
    e63 = np.zeros((128, 2), np.float32)
    e63[63, 0] = 1.0
    e63[127, 1] = 1.0
    c["c_e63"] = e63
    ex2 = np.zeros((2, 128), np.float32)
    ex2[0, 0:64] = 1.0
    ex2[1, 64:128] = 1.0
    c["c_exp2f"] = ex2
    c["c_exp2b"] = ex2.astype(BF)
    c["c_ones1f"] = np.ones((1, 128), np.float32)
    c["c_ones1b"] = np.ones((1, 128), np.float32).astype(BF)
    bd = np.zeros((128, 2), np.float32)
    bd[0:64, 0] = 1.0
    bd[64:128, 1] = 1.0
    c["c_blkdiag2b"] = bd.astype(BF)
    c["c_onesb"] = np.ones((128, 64), np.float32).astype(BF)
    pool = np.tile(np.array([[1e-5, 64e-6, 1e-6, 1.0, float(np.finfo(np.float32).eps)]],
                            np.float32), (128, 1))
    c["c_pool"] = pool
    return c


CONST_SPECS = {
    "c_identb": ([128, 128], BF16), "c_identf": ([128, 128], F32),
    "c_ut64x2": ([128, 64], F32), "c_ones": ([128, 64], F32),
    "c_negones": ([128, 64], F32), "c_mstrict": ([128, 128], F32),
    "c_mincl": ([128, 128], F32), "c_mstrictlow": ([128, 128], F32),
    "c_utblk": ([128, 128], F32), "c_e63": ([128, 2], F32),
    "c_exp2f": ([2, 128], F32), "c_exp2b": ([2, 128], BF16),
    "c_ones1f": ([1, 128], F32), "c_ones1b": ([1, 128], BF16),
    "c_blkdiag2b": ([128, 2], BF16),
    "c_onesb": ([128, 64], BF16),
    "c_pool": ([128, 5], F32),
}

XNK = [tuple(f"xn{p}.{fc}" for fc in range(16)) for p in range(2)]
XNFK = tuple(f"xnf.{fc}" for fc in range(16))
OFMK = tuple(f"ofm.{vc}" for vc in range(16))


def build_program():
    STAGE = float(os.environ.get("MK_STAGE", "9"))
    nc = bass.Bass("TRN2", target_bir_lowering=False, debug=False)
    st = nc._state

    d_xtok = nc.dram_tensor("x_tok", [TOWN, C], F32, kind="ExternalInput")
    d_xfm = nc.dram_tensor("x_fm", [C, TOWN], F32, kind="ExternalInput")
    d_wmix = nc.dram_tensor("w_mix", [C, 772], F32, kind="ExternalInput")
    d_wconv = nc.dram_tensor("w_conv", [512, 4], F32, kind="ExternalInput")
    d_vhead = nc.dram_tensor("vhead", [1, 8], F32, kind="ExternalInput")
    d_wo = nc.dram_tensor("w_o", [C, C], F32, kind="ExternalInput")
    d_w0a = nc.dram_tensor("w0a", [C, 4096], F32, kind="ExternalInput")
    d_w0b = nc.dram_tensor("w0b", [4096, C], F32, kind="ExternalInput")
    d_w1a = nc.dram_tensor("w1a", [C, 4096], F32, kind="ExternalInput")
    d_w1b = nc.dram_tensor("w1b", [4096, C], F32, kind="ExternalInput")
    d_b0a = nc.dram_tensor("b0a_t", [128, 32], F32, kind="ExternalInput")
    d_b1a = nc.dram_tensor("b1a_t", [128, 32], F32, kind="ExternalInput")
    d_b0b = nc.dram_tensor("b0b_t", [128, 16], F32, kind="ExternalInput")
    d_b1b = nc.dram_tensor("b1b_t", [128, 16], F32, kind="ExternalInput")
    d_wg = nc.dram_tensor("wg", [C, 2], F32, kind="ExternalInput")
    d_out = nc.dram_tensor("out_fm", [C, TOWN], F32, kind="ExternalOutput")
    d_const = {k: nc.dram_tensor("d_" + k, sh, dt, kind="ExternalInput")
               for k, (sh, dt) in CONST_SPECS.items()}

    ag_in = nc.dram_tensor("ag_in", [C, TOWN], BF16)
    ag_out = nc.dram_tensor("ag_out", [NCORE * C, TOWN], BF16,
                            addr_space="Shared")
    a2a_in = nc.dram_tensor("a2a_in", [NCORE * 256, TOWN], BF16)
    a2a_out = nc.dram_tensor("a2a_out", [NCORE * 256, TOWN], BF16)
    h0_dram = nc.dram_tensor("h0_dram", [4096, TOWN], BF16)
    h1_dram = nc.dram_tensor("h1_dram", [4096, TOWN], BF16)
    wo_b = nc.dram_tensor("wo_b", [C, C], BF16)
    w0a_b = nc.dram_tensor("w0a_b", [C, 4096], BF16)
    w1a_b = nc.dram_tensor("w1a_b", [C, 4096], BF16)
    w0b_b = nc.dram_tensor("w0b_b", [4096, C], BF16)
    w1b_b = nc.dram_tensor("w1b_b", [4096, C], BF16)

    sb = {}

    def S(name, shape, dtype):
        sb[name] = st.alloc_sbuf_tensor(name, shape, dtype)
        return sb[name]

    for k, (sh, dt) in CONST_SPECS.items():
        S(k, sh, dt)
    S("convw", [128, 16], F32)
    S("vh_row", [1, 8], F32)
    S("vh_all", [128, 8], F32)
    S("negexpA", [128, 2], F32)
    S("biasb", [128, 2], F32)
    S("WBUF", [128, 12352], BF16)
    S("xtile", [128, 2048], F32)
    S("bn24", [128, 24], F32)
    S("mv", [128, 2], F32)
    S("rs", [128, 1], F32)
    S("negmrs", [128, 1], F32)
    S("mtmp", [128, 1], F32)
    S("mtmp2", [128, 1], F32)
    S("xnstage", [128, 2048], BF16)
    S("BB0", [128, 16384], BF16)
    S("ARE", [128, 16384], BF16)
    S("conva", [128, 8 * 515], BF16)
    S("convt", [128, 2 * 512], F32)
    S("qsil", [128, 512], BF16)
    S("ksil", [128, 512], BF16)
    S("vsil", [128, 1024], BF16)
    S("sqbuf", [128, 512], BF16)
    S("rsq2", [2, 512], BF16)
    S("rsqt", [2, 512], F32)
    S("qhat", [128, 512], BF16)
    S("khat", [128, 512], BF16)
    S("Ktok", [128, 512], BF16)
    S("Vtok", [128, 1024], BF16)
    S("gates", [128, 1024], BF16)
    S("thin_sb", [4, 512], F32)
    S("thin_tok", [128, 16], F32)
    S("g_all", [128, 8], F32)
    S("sp_tmp", [128, 8], F32)
    S("sp_tmp2", [128, 8], F32)
    S("sig2_all", [128, 8], F32)
    S("brow2", [128, 64], BF16)
    S("G_all", [128, 8], F32)
    S("eG_all", [128, 8], F32)
    S("negeG_all", [128, 8], F32)
    S("d2_tmp", [128, 8], F32)
    S("eGLmG", [128, 8], F32)
    S("GL2", [2, 8], F32)
    S("GLflat", [1, 16], F32)
    S("eGLflat", [1, 16], F32)
    S("eGLk", [128, 16], F32)
    S("gUT", [128, 2 * 64], F32)
    S("expD", [128, 128], F32)
    S("expDT", [128, 128], F32)
    S("tmpA", [128, 128], F32)
    S("tmpB", [128, 128], F32)
    S("nst", [128, 2 * 384], BF16)
    S("vcur", [128, 2 * 2 * 64], BF16)
    S("attnT", [128, 2 * 64], BF16)
    S("Vsl", [128, 2 * 2 * 128], BF16)
    S("Ubuf", [128, 2 * 128], BF16)
    S("kbar", [128, 2 * 64], BF16)
    S("Otmp", [128, 2 * 128], F32)
    S("Oat", [128, 2 * 128], F32)
    S("Osq", [128, 256], F32)
    S("ssO", [128, 2], F32)
    S("rsO", [128, 2], F32)
    S("rsOt", [128, 2], F32)
    S("ONp", [128, 2 * 128], BF16)
    S("OTr", [128, 2 * 128], BF16)
    S("Ssb", [128, 512], F32)
    S("Ssb_b", [128, 512], BF16)
    S("staging", [128, 1024], BF16)
    S("wg_sb", [128, 32], F32)
    S("wd_f", [128, 16], F32)
    S("wd_b", [128, 16], BF16)
    S("FSCR", [128, 8192], BF16)
    S("mrow", [1, 256], BF16)
    S("mball", [128, 1024], BF16)
    S("b0a_sb", [128, 32], F32)
    S("b1a_sb", [128, 32], F32)
    S("b0b_sb", [128, 16], F32)
    S("b1b_sb", [128, 16], F32)
    S("hstage", [128, 4 * 256], BF16)
    S("hbuf", [128, 16384], BF16)
    S("fb0", [128, 128], F32)
    S("fb1", [128, 128], F32)
    S("fb2", [128, 128], F32)
    S("fbo", [128, 2 * 128], F32)
    S("xfr", [128, 2 * 128], F32)

    ps = {}

    def P(name, shape, dtype=F32):
        ps[name] = st.alloc_psum_tensor(name, shape, dtype)
        return ps[name]

    P("PB0", [128, 512])
    P("PB1", [128, 512])
    P("PB2", [128, 512])
    P("PB3", [128, 512])
    P("PB4", [128, 512])
    P("PB5", [128, 512])
    P("PB6", [128, 512])
    P("PB7", [128, 512])

    E = Emitter()

    def dma(eng, stream, out_ap, in_ap, reads, writes):
        E.dma(eng, stream, out_ap, in_ap, reads=reads, writes=writes)

    col8 = lambda h, subq: 4 * h + subq


    def _finish_partial():
        E.dma("sync", "ydbg", d_out[0:128, 0:128], sb["Ssb"][:, 0:128],
              reads=(), writes=("dbgout",))
        E.dcut("ydbg")
        E.op("sync", lambda e: e.nop(), reads=("dbgout",), writes=())
        E.play(nc)
        return nc

    for k in CONST_SPECS:
        dma("sync", "yc", sb[k][:, :], d_const[k].ap(), (), (k,))
    for ccv in range(4):
        dma("sync", "yc", sb["convw"][:, 4 * ccv:4 * ccv + 4],
            d_wconv[128 * ccv:128 * ccv + 128, :], (), ("convw",))
    dma("sync", "yc", sb["vh_row"][:, :], d_vhead.ap(), (), ("vh_row",))
    dma("sync", "yc", sb["wg_sb"][:, :],
        d_wg.ap().rearrange("(a p) c -> p a c", p=128), (), ("wg_sb",))
    dma("sync", "yc", sb["b0a_sb"][:, :], d_b0a.ap(), (), ("b0a_sb",))
    dma("sync", "yc", sb["b1a_sb"][:, :], d_b1a.ap(), (), ("b1a_sb",))
    dma("sync", "yc", sb["b0b_sb"][:, :], d_b0b.ap(), (), ("b0b_sb",))
    dma("sync", "yc", sb["b1b_sb"][:, :], d_b1b.ap(), (), ("b1b_sb",))
    E.dcut("yc")
    for fc in range(16):
        dma("gpsimd", "gw", sb["WBUF"][:, 772 * fc:772 * (fc + 1)],
            d_wmix[128 * fc:128 * (fc + 1), :], (), ("wmix",))
    for src, dst, key in ((d_wo, wo_b, "wob"), (d_w0a, w0a_b, "w0ab"),
                          (d_w1a, w1a_b, "w1ab"), (d_w0b, w0b_b, "w0bb"),
                          (d_w1b, w1b_b, "w1bb")):
        rows = src.ap().shape[0]
        q = rows // 4
        for i in range(4):
            dma("gpsimd", "gc", dst[q * i:q * (i + 1), :], src[q * i:q * (i + 1), :],
                (), (f"{key}.{i}",))
    E.dcut("gw")
    E.dcut("gc")
    WOBK = tuple(f"wob.{i}" for i in range(4))
    W0AK = tuple(f"w0ab.{i}" for i in range(4))
    W1AK = tuple(f"w1ab.{i}" for i in range(4))
    W0BK = tuple(f"w0bb.{i}" for i in range(4))
    W1BK = tuple(f"w1bb.{i}" for i in range(4))

    if STAGE < 0.3:
        return _finish_partial()
    E.op("tensor", lambda e: e.matmul(ps["PB5"][:, 0:8], sb["c_ones1f"][:, :],
                                      sb["vh_row"][:, :], start=True, stop=True),
         ("c_ones1f", "vh_row"), ("B5",))
    E.op("scalar", lambda e: e.copy(sb["vh_all"][:, :], ps["PB5"][:, 0:8]),
         ("B5",), ("vh_all",))
    if STAGE < 0.35:
        return _finish_partial()
    E.op("scalar", lambda e: e.activation(sb["negexpA"][:, :], sb["vh_all"][:, 0:2], AF.Exp),
         ("vh_all",), ("negexpA",))
    E.op("scalar", lambda e: e.mul(sb["negexpA"][:, :], sb["negexpA"][:, :], -1.0),
         ("negexpA",), ("negexpA",))
    if STAGE < 0.4:
        return _finish_partial()
    E.op("vector", lambda e: e.tensor_add(sb["biasb"][:, :], sb["vh_all"][:, 4:6],
                                          sb["vh_all"][:, 6:8]),
         ("vh_all",), ("biasb",))
    if STAGE < 0.45:
        return _finish_partial()
    E.op("vector", lambda e: e.tensor_tensor(
            sb["wd_f"][:, :],
            bass.AP(sb["wg_sb"], 0, [[32, 128], [2, 16]]),
            bass.AP(sb["wg_sb"], 1, [[32, 128], [2, 16]]), ALU.subtract),
         ("wg_sb",), ("wd_f",))
    E.op("scalar", lambda e: e.copy(sb["wd_b"][:, :], sb["wd_f"][:, :]),
         ("wd_f",), ("wd_b",))
    if STAGE < 0.5:
        return _finish_partial()
    E.op("vector", lambda e: e.memset(sb["Ssb"][:, 0:256], 0.0), (), ("S.0.0", "S.0.1"))
    E.op("vector", lambda e: e.memset(sb["Ssb"][:, 256:512], 0.0), (), ("S.1.0", "S.1.1"))
    E.op("vector", lambda e: e.memset(sb["Ssb_b"][:, 0:256], 0.0), (), ("Sb.0.0", "Sb.0.1"))
    E.op("vector", lambda e: e.memset(sb["Ssb_b"][:, 256:512], 0.0), (), ("Sb.1.0", "Sb.1.1"))

    if STAGE < 0.6:
        return _finish_partial()
    _ln_iters = 8 if STAGE >= 0.74 else (4 if STAGE >= 0.72 else (2 if STAGE >= 0.71 else 1))
    for i in range(_ln_iters):
        dma("sync", "yx", sb["xtile"][:, :], d_xtok[128 * i:128 * (i + 1), :], (), ("xtile",))
        E.dcut("yx")
        for q in range(4):
            E.op("vector", lambda e, q=q: e.bn_stats(
                    sb["bn24"][:, 6 * q:6 * (q + 1)],
                    sb["xtile"][:, 512 * q:512 * (q + 1)]),
                 ("xtile",), (f"bn24.{q}",))
        E.op("vector", lambda e: e.bn_aggr(sb["mv"][:, :], sb["bn24"][:, :]),
             ("bn24.0", "bn24.1", "bn24.2", "bn24.3"), ("mv",))
        E.op("scalar", lambda e: e.activation(sb["mtmp2"][:, :], sb["mv"][:, 1:2],
                                              AF.Sqrt, bias=sb["c_pool"][:, 0:1]),
             ("mv", "c_pool"), ("mtmp2",))
        E.op("vector", lambda e: e.reciprocal(sb["rs"][:, :], sb["mtmp2"][:, :]),
             ("mtmp2",), ("rs",))
        E.op("vector", lambda e: e.tensor_mul(sb["mtmp"][:, :], sb["mv"][:, 0:1],
                                              sb["rs"][:, :]),
             ("mv", "rs"), ("mtmp",))
        E.op("vector", lambda e: e.tensor_scalar_mul(sb["negmrs"][:, :],
                                                     sb["mtmp"][:, :], -1.0),
             ("mtmp",), ("negmrs",))
        E.op("vector", lambda e: e.tensor_scalar(
                sb["xnstage"][:, :], sb["xtile"][:, :],
                sb["rs"][:, 0:1], sb["negmrs"][:, 0:1], ALU.mult, ALU.add),
             ("xtile", "rs", "negmrs"), ("xnstage",))
        for fc in range(0 if (STAGE < 0.7 or abs(STAGE - 0.74) < 1e-9) else 16):
            slot = fc % 2
            pso = (ps["PB6"] if slot == 0 else ps["PB7"])[:, 0:128]
            E.op("tensor", lambda e, fc=fc, pso=pso: e.matmul(
                    pso, sb["xnstage"][:, 128 * fc:128 * (fc + 1)], sb["c_identb"][:, :],
                    start=True, stop=True),
                 ("xnstage", "c_identb"), ("B6" if slot == 0 else "B7",))
            E.op("scalar", lambda e, fc=fc, i=i, pso=pso: e.copy(
                    sb["ARE"][:, 1024 * fc + 128 * i:1024 * fc + 128 * (i + 1)], pso),
                 ("B6" if slot == 0 else "B7",), (f"LNare.{fc}.{i}",))
    if STAGE < 0.8:
        return _finish_partial()
    for fc in range(16):
        dma("sync", "ya", ag_in[128 * fc:128 * (fc + 1), :],
            sb["ARE"][:, 1024 * fc:1024 * (fc + 1)],
            tuple(f"LNare.{fc}.{i}" for i in range(8)), (f"agin.{fc}",))
    E.dcut("ya")
    AGINK = tuple(f"agin.{fc}" for fc in range(16))
    if STAGE < 0.9:
        return _finish_partial()
    E.coll("gpsimd", "cag", lambda e: e.collective_compute(
            "AllGather", ALU.bypass, replica_groups=[list(range(NCORE))],
            ins=[ag_in.ap().opt()], outs=[ag_out.ap().opt()]),
         AGINK, ("agout",))

 
    if STAGE < 2:
        return _finish_partial()

    for j in range(NT):
        pj = j % 2
        rblk = j // 2
        seq = j // 8
        jj = j % 8
        xnbuf = sb["BB0"][:, 8192 * pj:8192 * (pj + 1)]
        for fc in range(16):
            dma("sync", f"yn{pj}", xnbuf[:, 512 * fc:512 * (fc + 1)],
                ag_out[2048 * rblk + 128 * fc:2048 * rblk + 128 * (fc + 1),
                       512 * pj:512 * pj + 512],
                ("agout",), (XNK[pj][fc],))
        E.dcut(f"yn{pj}")

        for oc in range(7):
            bank = ps["PB0"] if oc % 2 == 0 else ps["PB1"]
            bkk = f"PB{oc % 2}"
            m0 = 768 if oc == 6 else 128 * oc
            mrows = 4 if oc == 6 else 128

            def projgroup(e, m0=m0, mrows=mrows, bank=bank, xnbuf=xnbuf):
                inst = None
                for fc in range(16):
                    inst = e.matmul(
                        bank[0:mrows, 0:512],
                        sb["WBUF"][:, 772 * fc + m0:772 * fc + m0 + mrows],
                        xnbuf[:, 512 * fc:512 * (fc + 1)],
                        start=(fc == 0), stop=(fc == 15))
                return inst
            E.op("tensor", projgroup, ("wmix",) + XNK[pj], (bkk,))

            if oc <= 3:
                cb = sb["conva"][:, 515 * (oc + 4 * pj):515 * (oc + 4 * pj) + 515]
                cbp = sb["conva"][:, 515 * (oc + 4 * (1 - pj)):515 * (oc + 4 * (1 - pj)) + 515]
                kcb = f"conv{oc}.{pj}"
                kcbp = f"conv{oc}.{1 - pj}"
                if jj == 0:
                    E.op("vector", lambda e, cb=cb: e.memset(cb[:, 0:3], 0.0),
                         (), (kcb,))
                else:
                    E.op("vector", lambda e, cb=cb, cbp=cbp: e.tensor_copy(
                            cb[:, 0:3], cbp[:, 512:515]), (kcbp,), (kcb,))
                E.op("scalar", lambda e, cb=cb, bank=bank: e.copy(
                        cb[:, 3:515], bank[:, 0:512]), (bkk,), (kcb,))
                ct = sb["convt"][:, 512 * (oc % 2):512 * (oc % 2) + 512]
                kct = f"convt{oc % 2}"
                E.op("vector", lambda e, cb=cb, ct=ct, oc=oc: e.tensor_scalar_mul(
                        ct, cb[:, 0:512], sb["convw"][:, 4 * oc:4 * oc + 1]),
                     (kcb, "convw"), (kct,))
                for tap in range(1, 4):
                    E.op("vector", lambda e, cb=cb, ct=ct, oc=oc, tap=tap:
                         e.scalar_tensor_tensor(
                             ct, cb[:, tap:tap + 512],
                             sb["convw"][:, 4 * oc + tap:4 * oc + tap + 1],
                             ct, ALU.mult, ALU.add),
                         (kcb, "convw", kct), (kct,))
                dstb = [sb["qsil"][:, :], sb["ksil"][:, :],
                        sb["vsil"][:, 0:512], sb["vsil"][:, 512:1024]][oc]
                kdst = ["qsil", "ksil", "vsil0", "vsil1"][oc]
                E.op("scalar", lambda e, ct=ct, dstb=dstb: e.activation(
                        dstb, ct, AF.Silu), (kct,), (kdst,))
            elif oc <= 5:
                h = oc - 4
                E.op("scalar", lambda e, bank=bank, h=h: e.activation(
                        sb["gates"][:, 512 * h:512 * (h + 1)], bank[:, 0:512],
                        AF.Silu), (bkk,), (f"gate{h}",))
            else:
                E.op("scalar", lambda e, bank=bank: e.copy(
                        sb["thin_sb"][:, :], bank[0:4, 0:512]), (bkk,), ("thin_sb",))

        for which in range(2):
            src = sb["qsil"] if which == 0 else sb["ksil"]
            ksrc = "qsil" if which == 0 else "ksil"
            dst = sb["qhat"] if which == 0 else sb["khat"]
            hatk = "qhat" if which == 0 else "khat"
            scl = 64.0 if which == 0 else 1.0
            biap = sb["c_pool"][0:2, 1:2] if which == 0 else sb["c_pool"][0:2, 2:3]
            E.op("vector", lambda e, src=src: e.tensor_mul(
                    sb["sqbuf"][:, :], src[:, :], src[:, :]), (ksrc,), ("sqbuf",))
            E.op("tensor", lambda e: e.matmul(
                    ps["PB5"][0:2, 0:512], sb["c_blkdiag2b"][:, :], sb["sqbuf"][:, :],
                    start=True, stop=True),
                 ("c_blkdiag2b", "sqbuf"), ("B5",))
            E.op("scalar", lambda e, scl=scl, biap=biap: e.activation(
                    sb["rsqt"][:, :], ps["PB5"][0:2, 0:512], AF.Sqrt,
                    bias=biap, scale=scl),
                 ("B5", "c_pool"), ("rsqt",))
            def _recip_b(e):
                with nc.allow_low_precision(reason="bf16 l2norm scale operand"):
                    return e.reciprocal(sb["rsq2"][:, :], sb["rsqt"][:, :])
            E.op("vector", _recip_b, ("rsqt",), ("rsq2",))
            E.op("tensor", lambda e: e.matmul(
                    ps["PB5"][:, 0:512], sb["c_exp2b"][:, :], sb["rsq2"][:, :],
                    start=True, stop=True),
                 ("c_exp2b", "rsq2"), ("B5",))
            E.op("vector", lambda e, src=src, dst=dst: e.tensor_mul(
                    dst[:, :], src[:, :], ps["PB5"][:, 0:512]),
                 (ksrc, "B5"), (hatk,))

        for subq in range(4):
            slot = subq % 2
            pso = (ps["PB6"] if slot == 0 else ps["PB7"])[:, 0:128]
            E.op("tensor", lambda e, subq=subq, pso=pso: e.matmul(
                    pso, sb["khat"][:, 128 * subq:128 * (subq + 1)], sb["c_identb"][:, :],
                    start=True, stop=True),
                 ("khat", "c_identb"), ("B6" if slot == 0 else "B7",))
            E.op("scalar", lambda e, subq=subq, pso=pso: e.copy(
                    sb["Ktok"][:, 128 * subq:128 * (subq + 1)], pso),
                 ("B6" if slot == 0 else "B7",), (f"Ktok.{subq}",))
        for h in range(2):
            for subq in range(4):
                slot = (h * 4 + subq) % 2
                pso = (ps["PB6"] if slot == 0 else ps["PB7"])[:, 0:128]
                E.op("tensor", lambda e, h=h, subq=subq, pso=pso: e.matmul(
                        pso, sb["vsil"][:, 512 * h + 128 * subq:512 * h + 128 * (subq + 1)],
                        sb["c_identb"][:, :], start=True, stop=True),
                     (f"vsil{h}", "c_identb"), ("B6" if slot == 0 else "B7",))
                E.op("scalar", lambda e, h=h, subq=subq, pso=pso: e.copy(
                        sb["Vtok"][:, 128 * col8(h, subq):128 * (col8(h, subq) + 1)], pso),
                     ("B6" if slot == 0 else "B7",), (f"Vtok.{col8(h, subq)}",))
        for subq in range(4):
            tslot = subq % 2
            pso = ps["PB4"][:, 384 + 4 * tslot:384 + 4 * (tslot + 1)]
            E.op("tensor", lambda e, subq=subq, pso=pso: e.matmul(
                    pso, sb["thin_sb"][:, 128 * subq:128 * (subq + 1)],
                    sb["c_identf"][0:4, 0:4], start=True, stop=True),
                 ("thin_sb", "c_identf"), ("B4",))
            E.op("vector", lambda e, subq=subq, pso=pso: e.tensor_copy(
                    bass.AP(sb["thin_tok"], subq, [[16, 128], [4, 4]]), pso),
                 ("B4",), ("thin_tok",))

        for h in range(2):
            E.op("vector", lambda e, h=h: e.tensor_scalar_add(
                    sb["sp_tmp"][:, 4 * h:4 * h + 4], sb["thin_tok"][:, 4 * h:4 * (h + 1)],
                    sb["vh_all"][:, 2 + h:3 + h]),
                 ("thin_tok", "vh_all"), ("sp_tmp",))
        E.op("scalar", lambda e: e.activation(sb["sp_tmp2"][:, :], sb["sp_tmp"][:, :], AF.Abs),
             ("sp_tmp",), ("sp_tmp2",))
        E.op("scalar", lambda e: e.activation(sb["sp_tmp2"][:, :], sb["sp_tmp2"][:, :],
                                              AF.Exp, scale=-1.0),
             ("sp_tmp2",), ("sp_tmp2",))
        E.op("scalar", lambda e: e.activation(sb["sp_tmp2"][:, :], sb["sp_tmp2"][:, :],
                                              AF.Ln, bias=sb["c_pool"][:, 3:4]),
             ("sp_tmp2", "c_pool"), ("sp_tmp2",))
        E.op("scalar", lambda e: e.activation(sb["sp_tmp"][:, :], sb["sp_tmp"][:, :], AF.Relu),
             ("sp_tmp",), ("sp_tmp",))
        E.op("vector", lambda e: e.tensor_add(sb["sp_tmp"][:, :], sb["sp_tmp"][:, :],
                                              sb["sp_tmp2"][:, :]),
             ("sp_tmp", "sp_tmp2"), ("sp_tmp",))
        for h in range(2):
            E.op("vector", lambda e, h=h: e.tensor_scalar_mul(
                    sb["g_all"][:, 4 * h:4 * h + 4], sb["sp_tmp"][:, 4 * h:4 * h + 4],
                    sb["negexpA"][:, h:h + 1]),
                 ("sp_tmp", "negexpA"), ("g_all",))
        for h in range(2):
            E.op("vector", lambda e, h=h: e.tensor_scalar_add(
                    sb["sp_tmp2"][:, 4 * h:4 * h + 4],
                    sb["thin_tok"][:, 8 + 4 * h:12 + 4 * h],
                    sb["biasb"][:, h:h + 1]),
                 ("thin_tok", "biasb"), ("sp_tmp2",))
        E.op("scalar", lambda e: e.activation(sb["sig2_all"][:, :], sb["sp_tmp2"][:, :],
                                              AF.Sigmoid),
             ("sp_tmp2",), ("sig2_all",))
        E.op("vector", lambda e: e.tensor_scalar_mul(sb["sig2_all"][:, :],
                                                     sb["sig2_all"][:, :], 2.0),
             ("sig2_all",), ("sig2_all",))
        E.op("tensor", lambda e: e.matmul(ps["PB4"][:, 392:400], sb["c_utblk"][:, :],
                                          sb["g_all"][:, :], start=True, stop=True),
             ("c_utblk", "g_all"), ("B4",))
        E.op("scalar", lambda e: e.copy(sb["G_all"][:, :], ps["PB4"][:, 392:400]),
             ("B4",), ("G_all",))
        E.op("scalar", lambda e: e.activation(sb["eG_all"][:, :], sb["G_all"][:, :], AF.Exp),
             ("G_all",), ("eG_all",))
        E.op("vector", lambda e: e.tensor_scalar_mul(sb["negeG_all"][:, :],
                                                     sb["eG_all"][:, :], -1.0),
             ("eG_all",), ("negeG_all",))
        E.op("tensor", lambda e: e.matmul(ps["PB4"][0:2, 400:408], sb["c_e63"][:, :],
                                          sb["G_all"][:, :], start=True, stop=True),
             ("c_e63", "G_all"), ("B4",))
        E.op("vector", lambda e: e.tensor_copy(sb["GL2"][:, :], ps["PB4"][0:2, 400:408]),
             ("B4",), ("GL2",))
        E.op("tensor", lambda e: e.matmul(ps["PB4"][:, 408:416], sb["c_exp2f"][:, :],
                                          sb["GL2"][:, :], start=True, stop=True),
             ("GL2", "c_exp2f"), ("B4",))
        E.op("vector", lambda e: e.tensor_sub(sb["d2_tmp"][:, :], ps["PB4"][:, 408:416],
                                              sb["G_all"][:, :]),
             ("B4", "G_all"), ("d2_tmp",))
        E.op("scalar", lambda e: e.activation(sb["eGLmG"][:, :], sb["d2_tmp"][:, :], AF.Exp),
             ("d2_tmp",), ("eGLmG",))
        for p in range(2):
            E.op("tensor", lambda e, p=p: e.matmul(
                    ps["PB4"][0:1, 416 + 8 * p:424 + 8 * p], sb["c_e63"][:, p:p + 1],
                    sb["G_all"][:, :], start=True, stop=True),
                 ("c_e63", "G_all"), ("B4",))
            E.op("vector", lambda e, p=p: e.tensor_copy(
                    sb["GLflat"][:, 8 * p:8 * (p + 1)],
                    ps["PB4"][0:1, 416 + 8 * p:424 + 8 * p]),
                 ("B4",), ("GLflat",))
        E.op("scalar", lambda e: e.activation(sb["eGLflat"][:, :], sb["GLflat"][:, :], AF.Exp),
             ("GLflat",), ("eGLflat",))
        E.op("tensor", lambda e: e.matmul(ps["PB4"][:, 432:448], sb["c_ones1f"][:, :],
                                          sb["eGLflat"][:, :], start=True, stop=True),
             ("c_ones1f", "eGLflat"), ("B4",))
        E.op("vector", lambda e: e.tensor_copy(sb["eGLk"][:, :], ps["PB4"][:, 432:448]),
             ("B4",), ("eGLk",))

        for ccq in range(8):
            subq = ccq // 2
            p = ccq % 2
            if p == 0:
                for h in range(2):
                    E.op("vector", lambda e, h=h, subq=subq: e.tensor_scalar_mul(
                            sb["gUT"][:, 64 * h:64 * (h + 1)], sb["c_ut64x2"][:, :],
                            sb["g_all"][:, col8(h, subq):col8(h, subq) + 1]),
                         ("g_all", "c_ut64x2"), (f"gUT{h}",))
                for pq in range(2):
                    for h in range(2):
                        def dmm(e, pq=pq, h=h):
                            e.matmul(ps["PB2"][64 * pq:64 * pq + 64, 64 * h:64 * h + 64],
                                     sb["c_ones"][64 * pq:64 * pq + 64, :],
                                     sb["gUT"][64 * pq:64 * pq + 64, 64 * h:64 * h + 64],
                                     start=True, stop=False)
                            return e.matmul(
                                ps["PB2"][64 * pq:64 * pq + 64, 64 * h:64 * h + 64],
                                sb["gUT"][64 * pq:64 * pq + 64, 64 * h:64 * h + 64],
                                sb["c_negones"][64 * pq:64 * pq + 64, 0:64],
                                start=False, stop=True)
                        E.op("tensor", dmm, (f"gUT{h}", "c_ones", "c_negones"),
                             ("B2",))
                        tok0 = 128 * subq + 64 * pq
                        E.op("tensor", lambda e, pq=pq, h=h, tok0=tok0: e.matmul(
                                ps["PB2"][64 * pq:64 * pq + 64,
                                          128 + 64 * h:128 + 64 * h + 64],
                                sb["khat"][64 * h:64 * h + 64, tok0:tok0 + 64],
                                sb["khat"][64 * h:64 * h + 64, tok0:tok0 + 64],
                                start=True, stop=True),
                             ("khat",), ("B2",))
                        E.op("tensor", lambda e, pq=pq, h=h, tok0=tok0: e.matmul(
                                ps["PB2"][64 * pq:64 * pq + 64,
                                          256 + 64 * h:256 + 64 * h + 64],
                                sb["khat"][64 * h:64 * h + 64, tok0:tok0 + 64],
                                sb["qhat"][64 * h:64 * h + 64, tok0:tok0 + 64],
                                start=True, stop=True),
                             ("khat", "qhat"), ("B2",))
                    for h in range(2):
                        bsl = (2 * pq + h) % 2
                        E.op("tensor", lambda e, pq=pq, h=h, subq=subq, bsl=bsl: e.matmul(
                                ps["PB4"][32 * bsl:32 * bsl + 1, 448:512],
                                sb["sig2_all"][64 * pq:64 * pq + 64,
                                               col8(h, subq):col8(h, subq) + 1],
                                sb["c_identf"][64 * pq:64 * pq + 64,
                                               64 * pq:64 * pq + 64],
                                start=True, stop=True),
                             ("sig2_all", "c_identf"), ("B4",))
                        E.op("scalar", lambda e, bsl=bsl: e.copy(
                                sb["brow2"][32 * bsl:32 * bsl + 1, :],
                                ps["PB4"][32 * bsl:32 * bsl + 1, 448:512]),
                             ("B4",), (f"brow{bsl}",))
                        E.op("tensor", lambda e, pq=pq, h=h, bsl=bsl: e.matmul(
                                ps["PB2"][64 * pq:64 * pq + 64,
                                          384 + 64 * h:384 + 64 * h + 64],
                                sb["c_onesb"][32 * bsl:32 * bsl + 1, 0:64],
                                sb["brow2"][32 * bsl:32 * bsl + 1, :],
                                start=True, stop=True),
                             (f"brow{bsl}", "c_onesb"), ("B2",))
                dquads = tuple("B2" for pq in range(2) for h in range(2))
                kkq = tuple("B2" for pq in range(2) for h in range(2))
                kqq = tuple("B2" for pq in range(2) for h in range(2))
                bbq = tuple("B2" for pq in range(2) for h in range(2))
                E.op("vector", lambda e: e.tensor_scalar_min(
                        sb["tmpA"][:, :], ps["PB2"][:, 0:128], 0.0),
                     dquads, ("tmpA",))
                E.op("scalar", lambda e: e.activation(sb["expD"][:, :],
                                                      sb["tmpA"][:, :], AF.Exp),
                     ("tmpA",), ("expD",))
                E.op("vector", lambda e: e.tensor_scalar_max(
                        sb["tmpB"][:, :], ps["PB2"][:, 0:128], 0.0),
                     dquads, ("tmpB",))
                E.op("scalar", lambda e: e.activation(sb["expDT"][:, :],
                                                      sb["tmpB"][:, :], AF.Exp, scale=-1.0),
                     ("tmpB",), ("expDT",))
                E.op("vector", lambda e: e.tensor_mul(sb["tmpA"][:, :], sb["expD"][:, :],
                                                      ps["PB2"][:, 128:256]),
                     ("expD",) + kkq, ("tmpA",))
                E.op("vector", lambda e: e.tensor_mul(sb["tmpB"][:, :], sb["tmpA"][:, :],
                                                      sb["c_mstrict"][:, :]),
                     ("tmpA", "c_mstrict"), ("tmpB",))
                for h in range(2):
                    E.op("vector", lambda e, h=h, subq=subq: e.tensor_scalar_mul(
                            sb["nst"][:, 384 * h:384 * h + 64],
                            sb["tmpB"][:, 64 * h:64 * h + 64],
                            sb["sig2_all"][:, col8(h, subq):col8(h, subq) + 1]),
                         ("tmpB", "sig2_all"), (f"nst{h}.0",))
                E.op("vector", lambda e: e.tensor_mul(sb["tmpA"][:, :], sb["expD"][:, :],
                                                      sb["c_mincl"][:, :]),
                     ("expD", "c_mincl"), ("tmpA",))
                for h in range(2):
                    E.op("vector", lambda e, h=h: e.tensor_mul(
                            sb["attnT"][:, 64 * h:64 * h + 64],
                            sb["tmpA"][:, 64 * h:64 * h + 64],
                            ps["PB2"][:, 256 + 64 * h:256 + 64 * h + 64]),
                         ("tmpA",) + kqq, (f"attnT{h}",))
                E.op("vector", lambda e: e.tensor_mul(sb["tmpB"][:, :], sb["expDT"][:, :],
                                                      ps["PB2"][:, 128:256]),
                     ("expDT",) + kkq, ("tmpB",))
                E.op("vector", lambda e: e.tensor_mul(sb["tmpA"][:, :], sb["tmpB"][:, :],
                                                      sb["c_mstrictlow"][:, :]),
                     ("tmpB", "c_mstrictlow"), ("tmpA",))
                for h in range(2):
                    E.op("vector", lambda e, h=h: e.tensor_mul(
                            sb["vcur"][:, 128 * h:128 * h + 64],
                            sb["tmpA"][:, 64 * h:64 * h + 64],
                            ps["PB2"][:, 384 + 64 * h:384 + 64 * h + 64]),
                         ("tmpA",) + bbq, (f"vcur{h}.0",))
                for lvl in range(5):
                    for h in range(2):
                        ibank = ps["PB4"] if h == 0 else ps["PB2"]
                        ibk = "B4" if h == 0 else "B2"
                        vp = lvl % 2
                        ncur = sb["nst"][:, 384 * h + 64 * lvl:384 * h + 64 * (lvl + 1)]
                        nnew = sb["nst"][:, 384 * h + 64 * (lvl + 1):384 * h + 64 * (lvl + 2)]
                        vc_ = sb["vcur"][:, 128 * h + 64 * vp:128 * h + 64 * vp + 64]
                        vn_ = sb["vcur"][:, 128 * h + 64 * (1 - vp):128 * h + 64 * (1 - vp) + 64]

                        def nmm(e, ncur=ncur, vc_=vc_, ibank=ibank):
                            i1 = None
                            for pq in range(2):
                                i1 = e.matmul(
                                    ibank[64 * pq:64 * pq + 64, 0:64],
                                    vc_[64 * pq:64 * pq + 64, :],
                                    ncur[64 * pq:64 * pq + 64, :],
                                    start=True, stop=True)
                            return i1
                        E.op("tensor", nmm, (f"nst{h}.{lvl}", f"vcur{h}.{vp}"),
                             (ibk,))
                        E.op("scalar", lambda e, nnew=nnew, ibank=ibank: e.activation(
                                nnew, ibank[:, 0:64], AF.Copy, scale=-1.0),
                             (ibk,), (f"nst{h}.{lvl + 1}",))
                        if lvl < 4:
                            def vmm(e, ncur=ncur, vc_=vc_, ibank=ibank):
                                i1 = None
                                for pq in range(2):
                                    i1 = e.matmul(
                                        ibank[64 * pq:64 * pq + 64, 64:128],
                                        ncur[64 * pq:64 * pq + 64, :],
                                        vc_[64 * pq:64 * pq + 64, :],
                                        start=True, stop=True)
                                return i1
                            E.op("tensor", vmm, (f"nst{h}.{lvl}", f"vcur{h}.{vp}"),
                                 (ibk,))
                            E.op("scalar", lambda e, vn_=vn_, ibank=ibank: e.activation(
                                    vn_, ibank[:, 64:128], AF.Copy, scale=-1.0),
                                 (ibk,), (f"vcur{h}.{1 - vp}",))

            gchunk = 8 * jj + ccq
            sslot = gchunk % 2

            def chain(h, ccq=ccq, subq=subq, p=p, seq=seq, sslot=sslot):
                PBC = ps["PB3"] if h == 0 else ps["PB5"]
                BKC = "B3" if h == 0 else "B5"
                c8 = col8(h, subq)
                sread = sb["Ssb_b"][64 * h:64 * h + 64,
                                    256 * sslot + 128 * seq:256 * sslot + 128 * seq + 128]
                sreadf = sb["Ssb"][64 * h:64 * h + 64,
                                   256 * sslot + 128 * seq:256 * sslot + 128 * seq + 128]
                swrite = sb["Ssb"][64 * h:64 * h + 64,
                                   256 * (1 - sslot) + 128 * seq:
                                   256 * (1 - sslot) + 128 * seq + 128]
                swrite_b = sb["Ssb_b"][64 * h:64 * h + 64,
                                       256 * (1 - sslot) + 128 * seq:
                                       256 * (1 - sslot) + 128 * seq + 128]
                ksr = f"Sb.{sslot}.{h}"
                ksrf = f"S.{sslot}.{h}"
                ksw = f"S.{1 - sslot}.{h}"
                kswb = f"Sb.{1 - sslot}.{h}"
                tok0 = 64 * ccq
                E.op("tensor", lambda e: e.matmul(
                        PBC[64 * p:64 * p + 64, 0:128],
                        sb["khat"][64 * h:64 * h + 64, tok0:tok0 + 64],
                        sread, start=True, stop=True),
                     ("khat", ksr), (BKC,))
                yield
                E.op("vector", lambda e: e.scalar_tensor_tensor(
                        sb["Vsl"][64 * p:64 * p + 64, 256 * h:256 * h + 128],
                        PBC[64 * p:64 * p + 64, 0:128],
                        sb["negeG_all"][64 * p:64 * p + 64, c8:c8 + 1],
                        sb["Vtok"][64 * p:64 * p + 64, 128 * c8:128 * c8 + 128],
                        ALU.mult, ALU.add),
                     (BKC, "negeG_all", f"Vtok.{c8}"), (f"Vsl{h}.0",))
                yield
                for step in range(6):
                    lvl = 5 - step
                    src_ = sb["Vsl"][:, 256 * h + 128 * (step % 2):
                                     256 * h + 128 * (step % 2) + 128]
                    dst_ = sb["Vsl"][:, 256 * h + 128 * ((step + 1) % 2):
                                     256 * h + 128 * ((step + 1) % 2) + 128]
                    nsl = sb["nst"][:, 384 * h + 64 * lvl:384 * h + 64 * (lvl + 1)]
                    E.op("tensor", lambda e, nsl=nsl, src_=src_: e.matmul(
                            PBC[64 * p:64 * p + 64, 128:256],
                            nsl[64 * p:64 * p + 64, :],
                            src_[64 * p:64 * p + 64, :], start=True, stop=True),
                         (f"nst{h}.{lvl}", f"Vsl{h}.{step % 2}"), (BKC,))
                    yield
                    E.op("vector", lambda e, src_=src_, dst_=dst_: e.scalar_tensor_tensor(
                            dst_[64 * p:64 * p + 64, :],
                            PBC[64 * p:64 * p + 64, 128:256], -1.0,
                            src_[64 * p:64 * p + 64, :], ALU.mult, ALU.add),
                         (BKC, f"Vsl{h}.{step % 2}"), (f"Vsl{h}.{(step + 1) % 2}",))
                    yield
                E.op("vector", lambda e: e.tensor_scalar_mul(
                        sb["Ubuf"][64 * p:64 * p + 64, 128 * h:128 * h + 128],
                        sb["Vsl"][64 * p:64 * p + 64, 256 * h:256 * h + 128],
                        sb["sig2_all"][64 * p:64 * p + 64, c8:c8 + 1]),
                     (f"Vsl{h}.0", "sig2_all"), (f"Ubuf{h}",))
                yield
                E.op("tensor", lambda e: e.matmul(
                        PBC[64 * p:64 * p + 64, 256:384],
                        sb["attnT"][64 * p:64 * p + 64, 64 * h:64 * h + 64],
                        sb["Ubuf"][64 * p:64 * p + 64, 128 * h:128 * h + 128],
                        start=True, stop=True),
                     (f"attnT{h}", f"Ubuf{h}"), (BKC,))
                yield
                E.op("tensor", lambda e: e.matmul(
                        PBC[64 * p:64 * p + 64, 384:512],
                        sb["qhat"][64 * h:64 * h + 64, tok0:tok0 + 64],
                        sread, start=True, stop=True),
                     ("qhat", ksr), (BKC,))
                yield
                E.op("scalar", lambda e: e.copy(
                        sb["Oat"][64 * p:64 * p + 64, 128 * h:128 * h + 128],
                        PBC[64 * p:64 * p + 64, 256:384]),
                     (BKC,), (f"Oat{h}",))
                yield
                E.op("vector", lambda e: e.scalar_tensor_tensor(
                        sb["Otmp"][64 * p:64 * p + 64, 128 * h:128 * h + 128],
                        PBC[64 * p:64 * p + 64, 384:512],
                        sb["eG_all"][64 * p:64 * p + 64, c8:c8 + 1],
                        sb["Oat"][64 * p:64 * p + 64, 128 * h:128 * h + 128],
                        ALU.mult, ALU.add),
                     (BKC, f"Oat{h}", "eG_all"), (f"Otmp{h}",))
                yield
                E.op("scalar", lambda e: e.activation(
                        sb["Osq"][64 * p:64 * p + 64, 128 * h:128 * h + 128],
                        sb["Otmp"][64 * p:64 * p + 64, 128 * h:128 * h + 128],
                        AF.Square),
                     (f"Otmp{h}",), (f"Osq{h}",))
                yield
                E.op("vector", lambda e: e.reduce_sum(
                        sb["ssO"][64 * p:64 * p + 64, h:h + 1],
                        sb["Osq"][64 * p:64 * p + 64, 128 * h:128 * h + 128],
                        axis=AX.X),
                     (f"Osq{h}",), (f"ssO{h}",))
                yield
                E.op("scalar", lambda e: e.activation(
                        sb["rsOt"][64 * p:64 * p + 64, h:h + 1],
                        sb["ssO"][64 * p:64 * p + 64, h:h + 1],
                        AF.Sqrt, bias=sb["c_pool"][64 * p:64 * p + 64, 4:5],
                        scale=1.0 / 128.0),
                     (f"ssO{h}", "c_pool"), (f"rsOt{h}",))
                yield
                E.op("vector", lambda e: e.reciprocal(
                        sb["rsO"][64 * p:64 * p + 64, h:h + 1],
                        sb["rsOt"][64 * p:64 * p + 64, h:h + 1]),
                     (f"rsOt{h}",), (f"rsO{h}",))
                yield
                E.op("vector", lambda e: e.tensor_scalar_mul(
                        sb["ONp"][64 * p:64 * p + 64, 128 * h:128 * h + 128],
                        sb["Otmp"][64 * p:64 * p + 64, 128 * h:128 * h + 128],
                        sb["rsO"][64 * p:64 * p + 64, h:h + 1]),
                     (f"Otmp{h}", f"rsO{h}"), (f"ONp{h}.{p}",))
                yield
                E.op("vector", lambda e: e.tensor_scalar_mul(
                        sb["kbar"][64 * p:64 * p + 64, 64 * h:64 * h + 64],
                        sb["Ktok"][64 * p:64 * p + 64,
                                   128 * subq + 64 * h:128 * subq + 64 * h + 64],
                        sb["eGLmG"][64 * p:64 * p + 64, c8:c8 + 1]),
                     (f"Ktok.{subq}", "eGLmG"), (f"kbar{h}",))
                yield
                E.op("tensor", lambda e: e.matmul(
                        ps["PB4"][64 * h:64 * h + 64, 256:384],
                        sb["kbar"][64 * p:64 * p + 64, 64 * h:64 * h + 64],
                        sb["Ubuf"][64 * p:64 * p + 64, 128 * h:128 * h + 128],
                        start=True, stop=True),
                     (f"kbar{h}", f"Ubuf{h}"), ("B4",))
                yield
                cflat = 8 * p + 4 * h + subq
                E.op("vector", lambda e: e.scalar_tensor_tensor(
                        swrite, sreadf,
                        sb["eGLk"][64 * h:64 * h + 64, cflat:cflat + 1],
                        ps["PB4"][64 * h:64 * h + 64, 256:384], ALU.mult, ALU.add),
                     (ksrf, "eGLk", "B4"), (ksw,))
                yield
                E.op("scalar", lambda e: e.copy(swrite_b, swrite),
                     (ksw,), (kswb,))

            gens = [chain(0), chain(1)]
            alive = [True, True]
            while alive[0] or alive[1]:
                for gi in (0, 1):
                    if alive[gi]:
                        try:
                            next(gens[gi])
                        except StopIteration:
                            alive[gi] = False

            if p == 1:
                for h in range(2):
                    slot = h
                    pso = (ps["PB6"] if slot == 0 else ps["PB7"])[:, 0:128]
                    E.op("tensor", lambda e, h=h, pso=pso: e.matmul(
                            pso, sb["ONp"][:, 128 * h:128 * h + 128], sb["c_identb"][:, :],
                            start=True, stop=True),
                         (f"ONp{h}.0", f"ONp{h}.1", "c_identb"), ("B6" if slot == 0 else "B7",))
                    E.op("scalar", lambda e, h=h, pso=pso: e.copy(
                            sb["OTr"][:, 128 * h:128 * h + 128], pso),
                         ("B6" if slot == 0 else "B7",), (f"OTr{h}",))
                    E.op("vector", lambda e, h=h, subq=subq: e.tensor_mul(
                            sb["staging"][:, 512 * h + 128 * subq:512 * h + 128 * (subq + 1)],
                            sb["OTr"][:, 128 * h:128 * h + 128],
                            sb["gates"][:, 512 * h + 128 * subq:512 * h + 128 * (subq + 1)]),
                         (f"OTr{h}", f"gate{h}"), (f"stag{h}.{subq}",))

        dblk = j // 2
        for h in range(2):
            dma("sync", f"ys{pj}", a2a_in[256 * dblk + 128 * h:256 * dblk + 128 * (h + 1),
                               512 * pj:512 * pj + 512],
                sb["staging"][:, 512 * h:512 * (h + 1)],
                tuple(f"stag{h}.{sq}" for sq in range(4)), (f"a2ain.{j}.{h}",))
        E.dcut(f"ys{pj}")

 
    if STAGE < 3:
        return _finish_partial()

    def a2a(e):
        with allow_ici_all_to_all():
            return e.collective_compute(
                "AllToAll", ALU.bypass, replica_groups=[list(range(NCORE))],
                ins=[a2a_in.ap().opt()], outs=[a2a_out.ap().opt()])
    E.coll("gpsimd", "ca2a", a2a,
           tuple(f"a2ain.{j}.{h}" for j in range(NT) for h in range(2)),
           ("a2aout",))

 
    if STAGE < 4:
        return _finish_partial()

    w0a_r = w0a_b.ap().rearrange("(a p) (b q) -> b p a q", p=128, q=128)
    w1a_r = w1a_b.ap().rearrange("(a p) (b q) -> b p a q", p=128, q=128)
    for ft in range(4):
        for fc in range(16):
            dma("sync", "yf", sb["FSCR"][:, 256 * fc:256 * (fc + 1)],
                ag_in[128 * fc:128 * (fc + 1), 256 * ft:256 * (ft + 1)],
                (f"agin.{fc}",), (XNFK[fc],))
        E.dcut("yf")

        def maskmm(e):
            inst = None
            for fc in range(16):
                inst = e.matmul(ps["PB5"][0:1, 0:256],
                                sb["wd_b"][:, fc:fc + 1],
                                sb["FSCR"][:, 256 * fc:256 * (fc + 1)],
                                start=(fc == 0), stop=(fc == 15))
            return inst
        E.op("tensor", maskmm, ("wd_b",) + XNFK, ("B5",))
        E.op("vector", lambda e: e.tensor_scalar(sb["mrow"][:, :], ps["PB5"][0:1, 0:256],
                                                 0.0, None, ALU.is_ge),
             ("B5",), ("mrow",))
        E.op("tensor", lambda e: e.matmul(ps["PB5"][:, 256:512], sb["c_ones1b"][:, :],
                                          sb["mrow"][:, :], start=True, stop=True),
             ("c_ones1b", "mrow"), ("B5",))
        E.op("scalar", lambda e, ft=ft: e.copy(sb["mball"][:, 256 * ft:256 * (ft + 1)],
                                               ps["PB5"][:, 256:512]),
             ("B5",), ("mball",))
        for ex in range(2):
            wa_r = w0a_r if ex == 0 else w1a_r
            wkeys = W0AK if ex == 0 else W1AK
            biask = "b0a_sb" if ex == 0 else "b1a_sb"
            act = AF.Gelu if ex == 0 else AF.Relu
            hd = h0_dram if ex == 0 else h1_dram
            deng = "sync" if ex == 0 else "scalar"
            def wload(ic):
                wp = ic % 2
                wsl = sb["FSCR"][:, 4096 + 2048 * wp:4096 + 2048 * (wp + 1)]
                kw = f"wstr.{wp}"
                wstream = ("ywa" if deng == "sync" else "swa") + str(wp)
                dma(deng, wstream, wsl.rearrange("p (a q) -> p a q", q=128), wa_r[ic],
                    wkeys, (kw,))
                E.dcut(wstream)
            wload(0)
            for ic in range(32):
                wp = ic % 2
                wsl = sb["FSCR"][:, 4096 + 2048 * wp:4096 + 2048 * (wp + 1)]
                kw = f"wstr.{wp}"
                if ic + 1 < 32:
                    wload(ic + 1)
                bank = ps["PB0"] if ex == 0 else ps["PB1"]
                bkk = f"PB{ex}"

                def hmm(e, wsl=wsl, bank=bank):
                    inst = None
                    for fc in range(16):
                        inst = e.matmul(bank[:, 0:256],
                                        wsl[:, 128 * fc:128 * (fc + 1)],
                                        sb["FSCR"][:, 256 * fc:256 * (fc + 1)],
                                        start=(fc == 0), stop=(fc == 15))
                    return inst
                E.op("tensor", hmm, (kw,) + XNFK, (bkk,))
                hsl = ic % 2
                hst = sb["hstage"][:, 512 * ex + 256 * hsl:512 * ex + 256 * (hsl + 1)]
                khs = f"hstage{ex}.{hsl}"
                E.op("scalar", lambda e, bank=bank, hst=hst, ic=ic, biask=biask, act=act:
                     e.activation(hst, bank[:, 0:256], act, bias=sb[biask][:, ic:ic + 1]),
                     (bkk, biask), (khs,))
                hstream = f"gh{ex}{ic % 2}"
                dma("gpsimd", hstream, hd[128 * ic:128 * (ic + 1), 256 * ft:256 * (ft + 1)],
                    hst, (khs,), (f"hd{ex}.{ic}",))
                E.dcut(hstream)
    HD0K = tuple(f"hd0.{ic}" for ic in range(32))
    HD1K = tuple(f"hd1.{ic}" for ic in range(32))

 
    if STAGE < 5:
        return _finish_partial()

    E.handoff(("wo.0", "wo.1"), ("wmix",))
    E.handoff(OFMK, XNFK + ("wstr.0", "wstr.1"))
    E.handoff(tuple(f"ARE.{cc}" for cc in range(16)),
              tuple(f"LNare.{fc}.{i}" for fc in range(16) for i in range(8))
              + AGINK)
    wo_r = wo_b.ap().rearrange("(a p) (b q) -> b p a q", p=128, q=128)
    for t2 in range(2):
        for vc in range(16):
            dma("sync", "yo", sb["FSCR"][:, 512 * vc:512 * (vc + 1)],
                a2a_out[128 * vc:128 * (vc + 1), 512 * t2:512 * (t2 + 1)],
                ("a2aout",), (OFMK[vc],))
        E.dcut("yo")
        def woload(ccq):
            wp = ccq % 2
            wsl = sb["WBUF"][:, 2048 * wp:2048 * (wp + 1)]
            dma("scalar", f"swo{wp}", wsl.rearrange("p (a q) -> p a q", q=128),
                wo_r[ccq], WOBK, (f"wo.{wp}",))
            E.dcut(f"swo{wp}")
        if t2 == 0:
            woload(0)
        for ccq in range(16):
            wp = ccq % 2
            wsl = sb["WBUF"][:, 2048 * wp:2048 * (wp + 1)]
            kw = f"wo.{wp}"
            if ccq + 1 < 16:
                woload(ccq + 1)
            elif t2 == 0:
                woload(0)

            def owmm(e, wsl=wsl):
                inst = None
                for vc in range(16):
                    inst = e.matmul(ps["PB0"][:, 0:512],
                                    wsl[:, 128 * vc:128 * (vc + 1)],
                                    sb["FSCR"][:, 512 * vc:512 * (vc + 1)],
                                    start=(vc == 0), stop=(vc == 15))
                return inst
            E.op("tensor", owmm, (kw,) + OFMK, ("PB0",))
            E.op("scalar", lambda e, ccq=ccq, t2=t2: e.copy(
                    sb["ARE"][:, 1024 * ccq + 512 * t2:1024 * ccq + 512 * (t2 + 1)],
                    ps["PB0"][:, 0:512]),
                 ("PB0",), (f"ARE.{ccq}",))

 
    if STAGE < 6:
        return _finish_partial()

    E.handoff(tuple(f"w0bB.{cj}" for cj in range(4)), ("xn0.0",) + XNK[0] + XNK[1])
    E.handoff(("w1bB.0", "w1bB.1", "w1bB.2"), ("wo.0", "wo.1"))
    E.handoff(("w1bB.3",), OFMK)
    w0b_r = w0b_b.ap().rearrange("(a p) (b q) -> b p a q", p=128, q=128)
    w1b_r = w1b_b.ap().rearrange("(a p) (b q) -> b p a q", p=128, q=128)
    h0_r = h0_dram.ap().rearrange("(a p) (b q) -> b p a q", p=128, q=128)
    h1_r = h1_dram.ap().rearrange("(a p) (b q) -> b p a q", p=128, q=128)
    OUTK = []
    for cblk in range(4):
        for cj in range(4):
            ccq = 4 * cblk + cj
            dma("sync", "yb", sb["BB0"][:, 4096 * cj:4096 * (cj + 1)]
                .rearrange("p (a q) -> p a q", q=128), w0b_r[ccq],
                W0BK, (f"w0bB.{cj}",))
            if cj < 3:
                w1dst = sb["WBUF"][:, 4096 * cj:4096 * (cj + 1)]
            else:
                w1dst = sb["FSCR"][:, 0:4096]
            dma("scalar", "sb1", w1dst.rearrange("p (a q) -> p a q", q=128),
                w1b_r[ccq], W1BK, (f"w1bB.{cj}",))
        E.dcut("yb")
        E.dcut("sb1")
        def hload(ft):
            hp2 = ft % 2
            dma("sync", f"yh{hp2}",
                sb["hbuf"][:, 8192 * hp2:8192 * hp2 + 4096]
                .rearrange("p (a q) -> p a q", q=128),
                h0_r[ft], HD0K, (f"hb0.{hp2}",))
            dma("sync", f"yh{hp2}",
                sb["hbuf"][:, 8192 * hp2 + 4096:8192 * hp2 + 8192]
                .rearrange("p (a q) -> p a q", q=128),
                h1_r[ft], HD1K, (f"hb1.{hp2}",))
            E.dcut(f"yh{hp2}")
        if cblk == 0:
            hload(0)
        for ft in range(8):
            hp2 = ft % 2
            if ft + 1 < 8:
                hload(ft + 1)
            elif cblk < 3:
                hload(0)
            for cj in range(4):
                ccq = 4 * cblk + cj
                if cj < 3:
                    w1sl = sb["WBUF"][:, 4096 * cj:4096 * (cj + 1)]
                else:
                    w1sl = sb["FSCR"][:, 0:4096]

                def o0mm(e, cj=cj, hp2=hp2):
                    inst = None
                    for ic in range(32):
                        inst = e.matmul(ps["PB0"][:, 0:128],
                                        sb["BB0"][:, 4096 * cj + 128 * ic:
                                                  4096 * cj + 128 * (ic + 1)],
                                        sb["hbuf"][:, 8192 * hp2 + 128 * ic:
                                                   8192 * hp2 + 128 * (ic + 1)],
                                        start=(ic == 0), stop=(ic == 31))
                    return inst
                E.op("tensor", o0mm, (f"w0bB.{cj}", f"hb0.{hp2}"), ("PB0",))

                def o1mm(e, w1sl=w1sl, hp2=hp2):
                    inst = None
                    for ic in range(32):
                        inst = e.matmul(ps["PB1"][:, 0:128],
                                        w1sl[:, 128 * ic:128 * (ic + 1)],
                                        sb["hbuf"][:, 8192 * hp2 + 4096 + 128 * ic:
                                                   8192 * hp2 + 4096 + 128 * (ic + 1)],
                                        start=(ic == 0), stop=(ic == 31))
                    return inst
                E.op("tensor", o1mm, (f"w1bB.{cj}", f"hb1.{hp2}"), ("PB1",))
                xp = (cblk * 32 + ft * 4 + cj) % 2
                xsl = sb["xfr"][:, 128 * xp:128 * (xp + 1)]
                kxf = f"xfr.{xp}"
                dma("gpsimd", f"gx{xp}", xsl, d_xfm[128 * ccq:128 * (ccq + 1),
                                         128 * ft:128 * (ft + 1)], (), (kxf,))
                E.dcut(f"gx{xp}")
                E.op("vector", lambda e, ccq=ccq: e.tensor_scalar_add(
                        sb["fb0"][:, :], ps["PB0"][:, 0:128],
                        sb["b0b_sb"][:, ccq:ccq + 1]),
                     ("PB0", "b0b_sb"), ("fb0",))
                E.op("vector", lambda e, ccq=ccq: e.tensor_scalar_add(
                        sb["fb1"][:, :], ps["PB1"][:, 0:128],
                        sb["b1b_sb"][:, ccq:ccq + 1]),
                     ("PB1", "b1b_sb"), ("fb1",))
                E.op("vector", lambda e: e.tensor_sub(sb["fb2"][:, :], sb["fb0"][:, :],
                                                      sb["fb1"][:, :]),
                     ("fb0", "fb1"), ("fb2",))
                E.op("vector", lambda e, ft=ft: e.tensor_mul(
                        sb["fb0"][:, :], sb["fb2"][:, :],
                        sb["mball"][:, 128 * ft:128 * (ft + 1)]),
                     ("fb2", "mball"), ("fb0",))
                E.op("vector", lambda e: e.tensor_add(sb["fb2"][:, :], sb["fb1"][:, :],
                                                      sb["fb0"][:, :]),
                     ("fb1", "fb0"), ("fb2",))
                E.op("vector", lambda e, ccq=ccq, ft=ft: e.tensor_add(
                        sb["fb0"][:, :], sb["fb2"][:, :],
                        sb["ARE"][:, 1024 * ccq + 128 * ft:1024 * ccq + 128 * (ft + 1)]),
                     ("fb2", f"ARE.{ccq}"), ("fb0",))
                op_ = (cblk * 32 + ft * 4 + cj) % 2
                osl = sb["fbo"][:, 128 * op_:128 * (op_ + 1)]
                ko = f"fbo.{op_}"
                E.op("vector", lambda e, xsl=xsl, osl=osl: e.tensor_add(
                        osl, sb["fb0"][:, :], xsl),
                     ("fb0", kxf), (ko,))
                okey = f"outw.{ccq}.{ft}"
                OUTK.append(okey)
                dma("sync", f"yw{op_}", d_out[128 * ccq:128 * (ccq + 1),
                                              128 * ft:128 * (ft + 1)],
                    osl, (ko,), (okey,))
                E.dcut(f"yw{op_}")

    E.op("sync", lambda e: e.nop(), reads=tuple(OUTK), writes=())

    E.play(nc)
    return nc


_PROG_CACHE = {}


def kernel(**inputs):
    x = np.asarray(inputs["L0_attn_x"])
    qk_w = np.asarray(inputs["L0_attn_qk_w"])
    vg_w = np.asarray(inputs["L0_attn_vg_w"])
    o_w = np.asarray(inputs["L0_attn_o_w"])
    a_w = np.asarray(inputs["L0_attn_a_w"])
    b_w = np.asarray(inputs["L0_attn_b_w"])
    b_bias = np.asarray(inputs["L0_attn_b_bias"])
    beta_bias = np.asarray(inputs["L0_attn_beta_bias"])
    A_log = np.asarray(inputs["L0_attn_A_log"])
    dt_bias = np.asarray(inputs["L0_attn_dt_bias"])
    q_conv_w = np.asarray(inputs["L0_attn_q_conv_w"])
    k_conv_w = np.asarray(inputs["L0_attn_k_conv_w"])
    v_conv_w = np.asarray(inputs["L0_attn_v_conv_w"])
    Wg = np.asarray(inputs["L0_ffn_Wg"])
    W0a = np.asarray(inputs["L0_ffn_W0a"])
    b0a = np.asarray(inputs["L0_ffn_b0a"])
    W0b = np.asarray(inputs["L0_ffn_W0b"])
    b0b = np.asarray(inputs["L0_ffn_b0b"])
    W1a = np.asarray(inputs["L0_ffn_W1a"])
    b1a = np.asarray(inputs["L0_ffn_b1a"])
    W1b = np.asarray(inputs["L0_ffn_W1b"])
    b1b = np.asarray(inputs["L0_ffn_b1b"])

    shared = {
        "w_o": np.ascontiguousarray(o_w.T),
        "w0a": np.ascontiguousarray(W0a.T),
        "w0b": np.ascontiguousarray(W0b.T),
        "w1a": np.ascontiguousarray(W1a.T),
        "w1b": np.ascontiguousarray(W1b.T),
        "b0a_t": np.ascontiguousarray(b0a.reshape(32, 128).T),
        "b1a_t": np.ascontiguousarray(b1a.reshape(32, 128).T),
        "b0b_t": np.ascontiguousarray(b0b.reshape(16, 128).T),
        "b1b_t": np.ascontiguousarray(b1b.reshape(16, 128).T),
        "wg": np.ascontiguousarray(Wg.T),
    }
    shared.update({"d_" + k: v for k, v in host_constants().items()})

    in_maps = []
    for c in range(NCORE):
        bsel = c // 4
        t0 = (c % 4) * TOWN
        x_own = np.ascontiguousarray(x[bsel, t0:t0 + TOWN])
        w_mix = np.ascontiguousarray(np.concatenate([
            qk_w[128 * c:128 * (c + 1)],
            qk_w[1024 + 128 * c:1024 + 128 * (c + 1)],
            vg_w[256 * c:256 * (c + 1)],
            vg_w[2048 + 256 * c:2048 + 256 * (c + 1)],
            a_w[2 * c:2 * c + 2],
            b_w[2 * c:2 * c + 2],
        ], axis=0).T)
        w_conv = np.ascontiguousarray(np.concatenate([
            q_conv_w[128 * c:128 * (c + 1), 0, :],
            k_conv_w[128 * c:128 * (c + 1), 0, :],
            v_conv_w[256 * c:256 * (c + 1), 0, :],
        ], axis=0))
        vhead = np.ascontiguousarray(np.concatenate([
            A_log[2 * c:2 * c + 2], dt_bias[2 * c:2 * c + 2],
            b_bias[2 * c:2 * c + 2], beta_bias[2 * c:2 * c + 2],
        ]).reshape(1, 8))
        m = {
            "x_tok": x_own,
            "x_fm": np.ascontiguousarray(x_own.T),
            "w_mix": w_mix,
            "w_conv": w_conv,
            "vhead": vhead,
        }
        m.update(shared)
        in_maps.append(m)

    if "nc" not in _PROG_CACHE:
        _PROG_CACHE["nc"] = build_program()
    nc = _PROG_CACHE["nc"]
    res = run_bass_kernel_spmd(nc, in_maps, core_ids=list(range(NCORE)))
    out = np.empty((2, 4096, 2048), np.float32)
    for c in range(NCORE):
        bsel = c // 4
        t0 = (c % 4) * TOWN
        out[bsel, t0:t0 + TOWN, :] = np.asarray(res.results[c]["out_fm"]).T
    return out
